# version 05
# baseline (speedup 1.0000x reference)
import numpy as np
import ml_dtypes
from contextlib import ExitStack

import concourse.bass as bass
import concourse.mybir as mybir
from concourse.bass_utils import run_bass_kernel_spmd

F32 = mybir.dt.float32
BF16 = mybir.dt.bfloat16
I32 = mybir.dt.int32
AF = mybir.ActivationFunctionType
ALU = mybir.AluOpType
BF = ml_dtypes.bfloat16

NCORE = 8
NC_ACT = 8
D = 2048
TC = 1024
NCH = 16
B = 4
T = 2048

HA = 8
DK = 64
DV = 128
MSLOT = 64
CLAMP = 32.0
A_EPS = 1e-5
QSCALE = float(DK) ** -0.5
LN_EPS = 1e-5

CFG = {"n_sub": 4, "debug": [], "debug_shapes": {}}

ENGS = ("pe", "act", "dve", "pool", "sp")
BLKNAME = {"pe": "tensor", "act": "scalar", "dve": "vector", "pool": "gpsimd", "sp": "sync"}
QSEM = {"sp": "q_sp", "act": "q_act", "pool": "q_pool"}
CSEM = {e: "c_" + e for e in ENGS}
NDMASEM = 8


class Buf:
    __slots__ = ("name", "writer", "readers")

    def __init__(self, name):
        self.name = name
        self.writer = None
        self.readers = []


class Prog:
    def __init__(self, nc, semh):
        self.nc = nc
        self.semh = semh
        self.thunks = {e: [] for e in ENGS}
        self.cinc = {e: [] for e in ENGS}
        self.drain_cover = {e: 0 for e in ENGS}
        self.dma_idx = {q: 0 for q in QSEM}
        self.cc_idx = 0

    def buf(self, name):
        return Buf(name)

    def op(self, eng, fn, reads=(), writes=(), update=(), dma=False, cc=False, tag=None):
        if update and eng != "pe":
            writes = list(writes) + [b for b in update if b not in writes]
            update = ()
        seq = len(self.thunks[eng])
        kind = "d" if dma else ("x" if cc else "c")
        tok = (eng, seq, kind)
        th = {"fn": fn, "waits": [], "drain": False, "incs": [], "eng": eng,
              "seq": seq, "tag": tag, "xwaits": [], "dsem": None}
        if dma:
            assert eng in QSEM, f"dma on {eng}"
            i = self.dma_idx[eng]
            self.dma_idx[eng] = i + 1
            slot, gen = i % NDMASEM, i // NDMASEM + 1
            sem = f"{QSEM[eng]}_{slot}"
            th["dsem"] = (sem, 16 * gen)
            th["incs"].append((sem, 16))
            if gen > 1:
                th["xwaits"].append((sem, 16 * (gen - 1)))
        if cc:
            assert eng == "pool"
            self.cc_idx += 1
            th["dsem"] = ("cc", self.cc_idx)
            th["incs"].append(("cc", 1))
            if self.cc_idx > 1:
                th["xwaits"].append(("cc", self.cc_idx - 1))
        deps = []
        for b in reads:
            if b.writer is not None:
                deps.append(b.writer)
        for b in writes:
            if b.writer is not None:
                deps.append(b.writer)
            deps.extend(b.readers)
        seen = set()
        for d in deps:
            if d in seen:
                continue
            seen.add(d)
            self._need(th, d)
        self.thunks[eng].append(th)
        for b in reads:
            b.readers.append(tok)
        for b in writes:
            b.writer = tok
            b.readers = []
        for b in update:
            b.writer = tok
        return tok

    def _need(self, th, tok):
        eng, seq, kind = tok
        me = th["eng"]
        if eng == me and kind == "c":
            if me == "pe":
                return
            if seq >= self.drain_cover[me]:
                th["drain"] = True
                self.drain_cover[me] = th["seq"]
            return
        if kind == "c":
            self._force_inc(tok)
        th["waits"].append(tok)

    def _force_inc(self, tok):
        eng, seq, kind = tok
        import bisect
        lst = self.cinc[eng]
        i = bisect.bisect_left(lst, seq)
        if i < len(lst):
            return
        self.thunks[eng][seq]["incs"].append((CSEM[eng], 1))
        bisect.insort(lst, seq)

    def final_barrier(self, eng, toks):
        finals = {}
        for q, n in self.dma_idx.items():
            for i in range(n):
                sem = f"{QSEM[q]}_{i % NDMASEM}"
                finals[sem] = max(finals.get(sem, 0), 16 * (i // NDMASEM + 1))
        if self.cc_idx:
            finals["cc"] = self.cc_idx
        tok = self.barrier(eng, toks, tag="final")
        self.thunks[eng][-1]["xwaits"].extend(finals.items())
        return tok

    def fence(self, bufs, tag="fence"):
        toks = []
        for b in bufs:
            if b.writer is not None:
                toks.append(b.writer)
            toks.extend(b.readers)
        toks = list(dict.fromkeys(toks))
        for eng in ENGS:
            self.barrier(eng, toks, tag=f"{tag}_{eng}")

    def barrier(self, eng, toks, tag=None):
        def fn(e):
            return e.nop()
        th_tok = self.op(eng, fn, tag=tag or "barrier")
        th = self.thunks[eng][-1]
        for t in toks:
            self._need(th, t)
        return th_tok

    def finalize(self, blk):
        incpts = {}
        for eng in ENGS:
            cums = {}
            for th in self.thunks[eng]:
                for (sem, amt) in th["incs"]:
                    cums.setdefault(sem, [0, []])
                    cums[sem][0] += amt
                    cums[sem][1].append((th["seq"], cums[sem][0]))
            for sem, (_, lst) in cums.items():
                if sem.startswith("c_"):
                    assert sem not in incpts
                    incpts[sem] = lst
        import bisect

        def resolve(tok):
            eng, seq, kind = tok
            if kind in ("d", "x"):
                return self.thunks[eng][seq]["dsem"]
            sem = CSEM[eng]
            lst = incpts.get(sem)
            assert lst, f"no incs on {sem} for tok {tok}"
            seqs = [s for s, _ in lst]
            i = bisect.bisect_left(seqs, seq)
            assert i < len(lst), f"no inc covering {tok}"
            return sem, lst[i][1]

        for eng in ENGS:
            hw = {}
            for th in self.thunks[eng]:
                rw = {}
                for tok in th["waits"]:
                    sem, val = resolve(tok)
                    if val > rw.get(sem, -1):
                        rw[sem] = val
                for sem, val in th["xwaits"]:
                    if val > rw.get(sem, -1):
                        rw[sem] = val
                th["rwaits"] = []
                for sem, val in rw.items():
                    if val > hw.get(sem, -1):
                        th["rwaits"].append((sem, val))
                        hw[sem] = val

        semh = self.semh
        for eng in ENGS:
            ths = self.thunks[eng]
            if not ths:
                continue

            def runner(e, ths=ths):
                for th in ths:
                    if th["drain"]:
                        e.drain()
                    for (sem, val) in th["rwaits"]:
                        e.wait_ge(semh[sem], val)
                    inst = th["fn"](e)
                    for (sem, amt) in th["incs"]:
                        inst.then_inc(semh[sem], amt)
            getattr(blk, BLKNAME[eng])(runner)



def _alibi_slopes(n_heads):
    import math
    n = 2 ** int(math.floor(math.log2(n_heads)))
    m0 = 2.0 ** (-8.0 / n)
    slopes = m0 ** np.arange(1, n + 1)
    if n < n_heads:
        m1 = 2.0 ** (-4.0 / n)
        slopes = np.concatenate([slopes, m1 ** np.arange(1, 1 + 2 * (n_heads - n), 2)])
    return slopes.astype(np.float64)

STRIP_W = 2432


def _alibi_strips(hh):
    slopes = _alibi_slopes(16)
    heads = range(hh * 8, hh * 8 + 8)
    u = np.arange(STRIP_W)[None, :]
    p = np.arange(128)[:, None]
    d = (u - 384 - p).astype(np.float64)
    mask = d >= 0
    sq = np.sqrt(np.maximum(d, 0))
    out = np.empty((8, 128, STRIP_W), np.float32)
    for i, h in enumerate(heads):
        out[i] = np.where(mask, np.exp(-slopes[h] * sq), 0.0).astype(np.float32)
    return out.astype(BF)


def make_in_maps(inputs):
    x = np.asarray(inputs["L0_attn_a_hidden_states"])
    in_maps = []
    ffn_names = ["L0_ffn_Wg", "L0_ffn_bg", "L0_ffn_Wup", "L0_ffn_bup", "L0_ffn_Wsw",
                 "L0_ffn_bsw", "L0_ffn_Wdn", "L0_ffn_bdn", "L0_ffn_Wsu", "L0_ffn_bsu",
                 "L0_ffn_Wsa", "L0_ffn_bsa", "L0_ffn_Wsd", "L0_ffn_bsd"]
    ffn = {n.replace("L0_ffn_", "f_"): np.ascontiguousarray(np.asarray(inputs[n]), dtype=np.float32)
           for n in ffn_names}
    ffn["f_Wup_t"] = np.ascontiguousarray(
        ffn.pop("f_Wup").reshape(6, 16, 128, 8, 128).transpose(2, 0, 3, 1, 4).reshape(128, 6 * 8 * 2048))
    ffn["f_Wsw_t"] = np.ascontiguousarray(
        ffn.pop("f_Wsw").reshape(6, 8, 128, 8, 128).transpose(2, 0, 3, 1, 4).reshape(128, 6 * 8 * 1024))
    ffn["f_Wdn_t"] = np.ascontiguousarray(
        ffn.pop("f_Wdn").reshape(6, 2, 4, 128, 4, 512).transpose(3, 0, 1, 4, 2, 5).reshape(128, 6 * 2 * 4 * 2048))
    ffn["f_Wsu_t2"] = np.ascontiguousarray(
        ffn.pop("f_Wsu").reshape(16, 128, 16, 128).transpose(1, 2, 0, 3).reshape(128, 16 * 2048))
    ffn["f_Wsa_t2"] = np.ascontiguousarray(
        ffn.pop("f_Wsa").reshape(8, 128, 8, 128).transpose(1, 2, 0, 3).reshape(128, 8 * 1024))
    ffn["f_Wsd_t2"] = np.ascontiguousarray(
        ffn.pop("f_Wsd").reshape(8, 2, 128, 4, 512).transpose(2, 3, 0, 1, 4).reshape(128, 4 * 8 * 1024))
    wqbT = np.ascontiguousarray(np.asarray(inputs["L0_attn_b_Wq"]).T)
    wkbT = np.ascontiguousarray(np.asarray(inputs["L0_attn_b_Wk"]).T)
    wvbT = np.ascontiguousarray(np.asarray(inputs["L0_attn_b_Wv"]).T)
    wobT = np.ascontiguousarray(np.asarray(inputs["L0_attn_b_Wo"]).T)
    for c in range(NC_ACT):
        b, hh = c // 2, c % 2
        t0 = TC * hh
        im = {}
        im["x_core"] = np.ascontiguousarray(x[b, t0:t0 + TC, :].T)
        im["wq_a"] = np.ascontiguousarray(
            np.asarray(inputs["L0_attn_a_Wq"])[:, hh * 512:(hh + 1) * 512]
            .reshape(16, 128, 4, 128).transpose(1, 2, 0, 3).reshape(128, 4 * 2048))
        im["wk_a"] = np.ascontiguousarray(
            np.asarray(inputs["L0_attn_a_Wk"])[:, hh * 512:(hh + 1) * 512]
            .reshape(16, 128, 4, 128).transpose(1, 2, 0, 3).reshape(128, 4 * 2048))
        im["ws_a"] = np.ascontiguousarray(
            np.asarray(inputs["L0_attn_a_Ws"])[:, hh * 512:(hh + 1) * 512]
            .reshape(16, 128, 512).transpose(1, 0, 2).reshape(128, 16 * 512))
        im["wv_a"] = np.ascontiguousarray(np.asarray(inputs["L0_attn_a_Wv"])[:, hh * 1024:(hh + 1) * 1024])
        im["wg_a"] = np.ascontiguousarray(np.asarray(inputs["L0_attn_a_Wg"])[:, hh * 1024:(hh + 1) * 1024])
        im["wo_a"] = np.ascontiguousarray(np.asarray(inputs["L0_attn_a_Wo"])[hh * 1024:(hh + 1) * 1024, :])
        im["gnw_a"] = np.ascontiguousarray(np.asarray(inputs["L0_attn_a_g_norm_weight"]).reshape(128, 1))
        im["wq_b"] = np.ascontiguousarray(
            wqbT[:, hh * 1024:(hh + 1) * 1024]
            .reshape(16, 128, 8, 128).transpose(1, 2, 0, 3).reshape(128, 8 * 2048))
        im["wk_b"] = np.ascontiguousarray(
            wkbT[:, hh * 256:(hh + 1) * 256]
            .reshape(16, 128, 2, 128).transpose(1, 2, 0, 3).reshape(128, 2 * 2048))
        im["wv_b"] = np.ascontiguousarray(
            wvbT[:, hh * 256:(hh + 1) * 256]
            .reshape(16, 128, 256).transpose(1, 0, 2).reshape(128, 16 * 256))
        im["wo_b"] = np.ascontiguousarray(wobT[hh * 1024:(hh + 1) * 1024, :])
        im.update(ffn)
        im["f_bup_t"] = np.ascontiguousarray(ffn["f_bup"].reshape(6, 8, 128).transpose(2, 0, 1).reshape(128, 48))
        im["f_bsw_t"] = np.ascontiguousarray(ffn["f_bsw"].reshape(6, 8, 128).transpose(2, 0, 1).reshape(128, 48))
        im["f_bsa_t"] = np.ascontiguousarray(ffn["f_bsa"].reshape(8, 128).T)
        im["f_bsu_t"] = np.ascontiguousarray(ffn["f_bsu"].reshape(16, 128).T)
        im["f_bg_t"] = np.ascontiguousarray(ffn["f_bg"].T)
        im["f_wg_t"] = np.ascontiguousarray(
            ffn["f_Wg"].reshape(2, 16, 128, 8).transpose(2, 0, 1, 3).reshape(128, 2 * 16 * 8))
        im["strips"] = _alibi_strips(hh)
        lm = (np.arange(128)[:, None] <= np.arange(128)[None, :]).astype(np.float32)
        im["lmask"] = lm
        im["lmask8"] = np.ascontiguousarray(np.tile(lm, (1, 8)).astype(BF))
        im["ident_bf"] = np.eye(128, dtype=BF)
        im["ident_f"] = np.eye(128, dtype=np.float32)
        bc8 = np.zeros((8, 6 * 128), np.float32)
        for n_ in range(6):
            bc8[n_, n_ * 128:(n_ + 1) * 128] = 1.0
        im["bcast8"] = bc8.astype(BF)
        im["ones_bf"] = np.ones((128, 1), dtype=BF)
        im["ones_f"] = np.ones((1, 128), np.float32)
        in_maps.append(im)
    return in_maps



PAIRS = [[0, 1], [2, 3], [4, 5], [6, 7]]


def build_program():
    nsub = CFG["n_sub"]
    nc = bass.Bass("TRN2", target_bir_lowering=False)

    din = {}

    def dt_in(name, shape, dtype=F32):
        din[name] = nc.dram_tensor(name, list(shape), dtype, kind="ExternalInput").ap()
        return din[name]

    dt_in("x_core", [D, TC])
    for n, sh in [("wq_a", [128, 4 * 2048]), ("wk_a", [128, 4 * 2048]), ("ws_a", [128, 16 * 512]),
                  ("wv_a", [D, 1024]), ("wg_a", [D, 1024]), ("wo_a", [1024, D]),
                  ("gnw_a", [128, 1]),
                  ("wq_b", [128, 8 * 2048]), ("wk_b", [128, 2 * 2048]), ("wv_b", [128, 16 * 256]),
                  ("wo_b", [1024, D]),
                  ("f_Wg", [2, D, 8]), ("f_bg", [2, 8]),
                  ("f_Wup_t", [128, 6 * 8 * 2048]), ("f_bup", [6, 1024]),
                  ("f_Wsw_t", [128, 6 * 8 * 1024]), ("f_bsw", [6, 1024]),
                  ("f_Wdn_t", [128, 6 * 2 * 4 * 2048]), ("f_bdn", [6, D]),
                  ("f_Wsu_t2", [128, 16 * 2048]), ("f_bsu", [D]),
                  ("f_Wsa_t2", [128, 8 * 1024]), ("f_bsa", [1024]),
                  ("f_Wsd_t2", [128, 4 * 8 * 1024]), ("f_bsd", [D]),
                  ("lmask", [128, 128]),
                  ("ones_f", [1, 128]), ("ident_f", [128, 128]),
                  ("f_bup_t", [128, 48]), ("f_bsw_t", [128, 48]), ("f_bsa_t", [128, 8]),
                  ("f_bsu_t", [128, 16]), ("f_bg_t", [8, 2]), ("f_wg_t", [128, 256])]:
        dt_in(n, sh)
    dt_in("ident_bf", [128, 128], BF16)
    dt_in("lmask8", [128, 1024], BF16)
    dt_in("strips", [8, 128, STRIP_W], BF16)
    dt_in("bcast8", [8, 6 * 128], BF16)
    dt_in("ones_bf", [128, 1], BF16)

    out_x = nc.dram_tensor("out_x", [D, TC], F32, kind="ExternalOutput").ap()
    dbg_out = {}
    for name, shape in CFG.get("debug_shapes", {}).items():
        dbg_out[name] = nc.dram_tensor("dbg_" + name, list(shape), F32, kind="ExternalOutput").ap()

    cache_specs = [
        ("f_Wsu_t2", [128, 16 * 2048]), ("f_Wup_t", [128, 6 * 8 * 2048]),
        ("f_Wsa_t2", [128, 8 * 1024]), ("f_Wsw_t", [128, 6 * 8 * 1024]),
        ("f_Wdn_t", [128, 6 * 2 * 4 * 2048]), ("f_Wsd_t2", [128, 4 * 8 * 1024]),
        ("wq_b", [128, 16384]), ("wk_b", [128, 4096]), ("wo_b", [1024, D]),
    ]
    dcache = {n: nc.dram_tensor("c_" + n, list(sh), BF16, kind="Internal").ap()
              for n, sh in cache_specs}

    s_spill = nc.dram_tensor("s_spill", [64, HA * T], F32, kind="Internal").ap()
    ag_xn_in = nc.dram_tensor("ag_xn_in", [D, TC], BF16, kind="Internal").ap()
    ag_xn_out = nc.dram_tensor("ag_xn_out", [2 * D, TC], BF16, kind="Internal").ap()
    ag_pt_in2 = [nc.dram_tensor(f"ag_pt_in{i}", [2 * D, 512], F32, kind="Internal").ap() for i in range(2)]
    h_spill = nc.dram_tensor("h_spill", [128, 48 * TC], BF16, kind="Internal").ap()
    sh_spill = nc.dram_tensor("sh_spill", [128, 16 * TC], BF16, kind="Internal").ap()
    ag_pt_out2 = [nc.dram_tensor(f"ag_pt_out{i}", [D, 512], F32, kind="Internal").ap() for i in range(2)]

    with ExitStack() as ctx:
        e = ctx.enter_context
        semh = {}
        semnames = ["cc"] + [CSEM[x] for x in ENGS]
        for q in QSEM.values():
            semnames += [f"{q}_{i}" for i in range(NDMASEM)]
        for s in semnames:
            semh[s] = e(nc.semaphore(s))

        x_res = e(nc.sbuf_tensor("x_res", [128, NCH * TC], F32))
        lmask8 = e(nc.sbuf_tensor("sb_lmask8", [128, 1024], BF16))
        identf = e(nc.sbuf_tensor("sb_identf", [128, 128], F32))
        bc8t = e(nc.sbuf_tensor("sb_bc8", [8, 6 * 128], BF16))
        ones_bf = e(nc.sbuf_tensor("sb_ones_bf", [128, 1], BF16))
        ones_f = e(nc.sbuf_tensor("sb_ones_f", [1, 128], F32))
        gnw = e(nc.sbuf_tensor("sb_gnw", [128, 1], F32))
        epsc = e(nc.sbuf_tensor("sb_epsc", [128, 1], F32))

        P = Prog(nc, semh)

        bxc = [P.buf(f"x_res{c}") for c in range(NCH)]
        bconsts = {n: P.buf("const_" + n) for n in
                   ["lmask", "lmask8", "ident", "ones_bf", "ones_f", "gnw", "eps",
                    "identf", "bc8"]}

        bcstg = [P.buf("cstg0"), P.buf("cstg1")]
        bcache = {n: P.buf("cache_" + n) for n, _ in cache_specs}
        conv_jobs = []
        conv_last = {}
        for n_, sh_ in cache_specs:
            R_, CC_ = sh_
            for r_ in range(max(1, R_ // 128)):
                for i_ in range(max(1, CC_ // 1024)):
                    w_ = min(1024, CC_)
                    conv_jobs.append((n_, din[n_][r_ * 128:(r_ + 1) * 128, i_ * w_:(i_ + 1) * w_],
                                      dcache[n_][r_ * 128:(r_ + 1) * 128, i_ * w_:(i_ + 1) * w_], w_))
            conv_last[n_] = len(conv_jobs)
        conv_state = {"i": 0, "stg": None}

        def conv_done(name):
            return name in conv_last and conv_state["i"] >= conv_last[name]

        def feed_conv(k):
            stg = conv_state["stg"]
            while k > 0 and conv_state["i"] < len(conv_jobs):
                assert stg is not None
                idx = conv_state["i"]
                conv_state["i"] += 1
                nme, src, dst, w_ = conv_jobs[idx]
                par = idx % 2
                stg_v = stg[:, par * 1024:par * 1024 + w_]
                P.op("pool", lambda en, src=src, stg_v=stg_v: en.dma_start(out=stg_v, in_=src),
                     writes=[bcstg[par]], dma=True, tag=f"cvr{idx}")
                P.op("sp", lambda en, dst=dst, stg_v=stg_v: en.dma_start(out=dst, in_=stg_v),
                     reads=[bcstg[par]], update=[bcache[nme]], dma=True, tag=f"cvw{idx}")
                k -= 1

        for c_ in range(NCH):
            P.op("sp" if c_ % 2 == 0 else "pool", lambda en, c_=c_: en.dma_start(
                out=x_res[:, c_ * TC:(c_ + 1) * TC],
                in_=din["x_core"][c_ * 128:(c_ + 1) * 128, :]),
                writes=[bxc[c_]], dma=True, tag=f"ldx{c_}")
        for nm, dst in [("lmask8", lmask8), ("ones_f", ones_f),
                        ("gnw_a", gnw)]:
            key = "gnw" if nm == "gnw_a" else nm
            P.op("sp", lambda en, dst=dst, nm=nm: en.dma_start(out=dst[:], in_=din[nm]),
                 writes=[bconsts[key]], dma=True)
        P.op("sp", lambda en: en.dma_start(out=identf[:], in_=din["ident_f"]), writes=[bconsts["identf"]], dma=True)
        P.op("sp", lambda en: en.dma_start(out=bc8t[:], in_=din["bcast8"]), writes=[bconsts["bc8"]], dma=True)
        P.op("sp", lambda en: en.dma_start(out=ones_bf[:], in_=din["ones_bf"]), writes=[bconsts["ones_bf"]], dma=True)
        P.op("dve", lambda en: en.memset(epsc[:], LN_EPS), writes=[bconsts["eps"]], tag="eps")

        ps1 = e(nc.psum_tensor("ps1", [128, 2048], F32))
        ps2 = e(nc.psum_tensor("ps2", [128, 2048], F32))
        pb1 = [P.buf(f"ps1b{i}") for i in range(4)]
        pb2 = [P.buf(f"ps2b{i}") for i in range(4)]

        def layernorm(tag, xn_t, bxn_t):
            with ExitStack() as lctx:
                le = lctx.enter_context
                bca = le(nc.sbuf_tensor(tag + "_bca", [128, TC], F32))
                bcm = le(nc.sbuf_tensor(tag + "_bcm", [128, TC], F32))
                sc_a = le(nc.sbuf_tensor(tag + "_sca", [128, TC], F32))
                sc_b = le(nc.sbuf_tensor(tag + "_scb", [128, TC], F32))
                srow = le(nc.sbuf_tensor(tag + "_srow", [1, 3 * TC], F32))
                x2bf = le(nc.sbuf_tensor(tag + "_x2bf", [128, 2 * TC], BF16))
                xbf = le(nc.sbuf_tensor(tag + "_xbf", [128, 2 * TC], BF16))
                xnc = le(nc.sbuf_tensor(tag + "_xnc", [128, 2 * TC], BF16)) if xn_t is None else None
                bbc = P.buf(tag + "bcast")
                bsc = [P.buf(tag + "sca"), P.buf(tag + "scb")]
                bxbf = [P.buf(tag + "xbf0"), P.buf(tag + "xbf1")]
                bx2bf = [P.buf(tag + "x2bf0"), P.buf(tag + "x2bf1")]
                bsrow = P.buf(tag + "srow")
                bxnc = [P.buf(tag + "xnc0"), P.buf(tag + "xnc1")]
                stage_toks = []

                for c in range(NCH):
                    sl = slice(c * TC, (c + 1) * TC)
                    par = c % 2
                    xb = xbf[:, par * TC:(par + 1) * TC]
                    x2 = x2bf[:, par * TC:(par + 1) * TC]
                    P.op("dve", lambda en, sl=sl, xb=xb: en.tensor_copy(xb, x_res[:, sl]),
                         reads=[bxc[c]], writes=[bxbf[par]], tag=f"{tag}_xbf{c}")
                    P.op("act", lambda en, sl=sl, x2=x2: en.activation(out=x2, in_=x_res[:, sl], func=AF.Square),
                         reads=[bxc[c]], writes=[bx2bf[par]], tag=f"{tag}_x2{c}")
                    for h in range(2):
                        hs = slice(h * 512, (h + 1) * 512)
                        first, last = c == 0, c == NCH - 1
                        P.op("pe", lambda en, xb=xb, hs=hs, first=first, last=last, h=h:
                             en.matmul(ps1[0:1, h * 512:(h + 1) * 512], ones_bf[:], xb[:, hs], start=first, stop=last),
                             reads=[bxbf[par], bconsts["ones_bf"]],
                             writes=[pb1[h]] if first else (), update=[pb1[h]] if last else (),
                             tag=f"{tag}_s1_{c}_{h}")
                        P.op("pe", lambda en, x2=x2, hs=hs, first=first, last=last, h=h:
                             en.matmul(ps1[0:1, 1024 + h * 512:1024 + (h + 1) * 512], ones_bf[:], x2[:, hs], start=first, stop=last),
                             reads=[bx2bf[par], bconsts["ones_bf"]],
                             writes=[pb1[2 + h]] if first else (), update=[pb1[2 + h]] if last else (),
                             tag=f"{tag}_s2_{c}_{h}")
                m_row = srow[0:1, 0:TC]
                a_row = srow[0:1, TC:2 * TC]
                ss_row = srow[0:1, 2 * TC:3 * TC]
                v_row = a_row
                for h in range(2):
                    hs = slice(h * 512, (h + 1) * 512)
                    P.op("dve", lambda en, hs=hs, h=h:
                         en.tensor_scalar_mul(srow[0:1, hs], ps1[0:1, h * 512:(h + 1) * 512], 1.0 / D),
                         reads=[pb1[h]], writes=[bsrow] if h == 0 else (), update=[bsrow] if h else (),
                         tag=f"{tag}_m{h}")
                    P.op("dve", lambda en, h=h:
                         en.tensor_scalar_mul(srow[0:1, 2 * TC + h * 512:2 * TC + (h + 1) * 512],
                                              ps1[0:1, 1024 + h * 512:1024 + (h + 1) * 512], 1.0 / D),
                         reads=[pb1[2 + h]], update=[bsrow], tag=f"{tag}_ss{h}")
                P.op("dve", lambda en: en.scalar_tensor_tensor(
                    out=v_row, in0=m_row, scalar=-1.0, in1=m_row, op0=ALU.mult, op1=ALU.mult),
                    reads=[bsrow], update=[bsrow], tag=f"{tag}_mm")
                P.op("dve", lambda en: en.tensor_add(v_row, v_row, ss_row),
                     reads=[bsrow], update=[bsrow], tag=f"{tag}_var")
                P.op("act", lambda en: en.activation(out=ss_row, in_=v_row, func=AF.Ln, bias=epsc[0:1, :]),
                     reads=[bsrow, bconsts["eps"]], update=[bsrow], tag=f"{tag}_ln")
                P.op("act", lambda en: en.activation(out=a_row, in_=ss_row, func=AF.Exp, scale=-0.5),
                     reads=[bsrow], update=[bsrow], tag=f"{tag}_exp")
                bbcm = P.buf(tag + "bbcm")
                for h in range(2):
                    hs = slice(h * 512, (h + 1) * 512)
                    P.op("pe", lambda en, hs=hs:
                         en.matmul(ps2[:, 0:512], ones_f[:], m_row[0:1, hs], start=True, stop=True),
                         reads=[bsrow, bconsts["ones_f"]], writes=[pb2[0]], tag=f"{tag}_bcm{h}")
                    P.op("dve", lambda en, hs=hs: en.tensor_copy(bcm[:, hs], ps2[:, 0:512]),
                         reads=[pb2[0]], writes=[bbcm] if h == 0 else (), update=[bbcm] if h else (),
                         tag=f"{tag}_bcmc{h}")
                    P.op("pe", lambda en, hs=hs:
                         en.matmul(ps2[:, 512:1024], ones_f[:], a_row[0:1, hs], start=True, stop=True),
                         reads=[bsrow, bconsts["ones_f"]], writes=[pb2[1]], tag=f"{tag}_bca{h}")
                    P.op("dve", lambda en, hs=hs: en.tensor_copy(bca[:, hs], ps2[:, 512:1024]),
                         reads=[pb2[1]], writes=[bbc] if h == 0 else (), update=[bbc] if h else (),
                         tag=f"{tag}_bcac{h}")
                for c in range(NCH):
                    sl = slice(c * TC, (c + 1) * TC)
                    par = c % 2
                    sc = (sc_a, sc_b)[par]
                    P.op("dve", lambda en, sl=sl, sc=sc: en.tensor_sub(sc[:], x_res[:, sl], bcm[:]),
                         reads=[bxc[c], bbcm], writes=[bsc[par]], tag=f"{tag}_n1_{c}")
                    if xn_t is not None:
                        P.op("dve", lambda en, sl=sl, sc=sc: en.tensor_mul(xn_t[:, sl], sc[:], bca[:]),
                             reads=[bsc[par], bbc], writes=[bxn_t[c]], tag=f"{tag}_n2_{c}")
                    else:
                        dst = xnc[:, par * TC:(par + 1) * TC]
                        P.op("dve", lambda en, sc=sc, dst=dst: en.tensor_mul(dst, sc[:], bca[:]),
                             reads=[bsc[par], bbc], writes=[bxnc[par]], tag=f"{tag}_n2_{c}")
                        tkn = P.op("sp", lambda en, dst=dst, c=c: en.dma_start(
                            out=ag_xn_in.rearrange("(c p) t -> p c t", p=128)[:, c, :], in_=dst),
                            reads=[bxnc[par]], dma=True, tag=f"{tag}_st{c}")
                        stage_toks.append(tkn)
                P.fence([bbc, bbcm, bsc[0], bsc[1], bxbf[0], bxbf[1], bx2bf[0], bx2bf[1],
                         bsrow, bxnc[0], bxnc[1]], tag=tag + "f")
                return stage_toks

        bag_xn_in = P.buf("ag_xn_in")
        bag_xn_out = P.buf("ag_xn_out")
        bag_ptr = [[P.buf(f"agpt{w}_{dc}") for dc in range(16)] for w in range(4)]
        bag_pt_out2 = [P.buf("ag_pt_out0"), P.buf("ag_pt_out1")]
        bspill = [P.buf(f"s_spill{i}") for i in range(4)]

        def attn_a():
            stage_toks = layernorm("ln0", None, None)
            tcc = P.op("pool", lambda en: en.collective_compute(
                "AllGather", ALU.bypass, ins=[ag_xn_in.opt()], outs=[ag_xn_out.opt()],
                replica_groups=PAIRS),
                writes=[bag_xn_out], cc=True, tag="ag_xn")
            th = P.thunks["pool"][-1]
            for t in stage_toks:
                P._need(th, t)

            pt_toks = {0: [], 1: []}
            rs_emitted = [False, False]

            def emit_rs(colh, tg):
                rs_emitted[colh] = True
                P.op("pool", lambda en, colh=colh: en.collective_compute(
                    "ReduceScatter", ALU.add, ins=[ag_pt_in2[colh].opt()], outs=[ag_pt_out2[colh].opt()],
                    replica_groups=PAIRS),
                    writes=[bag_pt_out2[colh]], cc=True, tag=tg)
                thpt = P.thunks["pool"][-1]
                for t in pt_toks[colh]:
                    P._need(thpt, t)
            with ExitStack() as actx:
                ae = actx.enter_context
                xnw = ae(nc.sbuf_tensor("a_xnw", [128, NCH * 512], BF16))
                bxnw = [P.buf(f"xnw{c}") for c in range(NCH)]

                def gather_xnw(w, tag):
                    for c in range(NCH):
                        def fn(en, c=c, w=w):
                            return en.dma_start(
                                out=xnw[:, c * 512:(c + 1) * 512],
                                in_=ag_xn_out[(w // 2) * D + c * 128:(w // 2) * D + (c + 1) * 128,
                                              (w % 2) * 512:(w % 2) * 512 + 512])
                        P.op("pool", fn, reads=[bag_xn_out],
                             writes=[bxnw[c]], dma=True, tag=f"{tag}_g{c}")

                zst = ae(nc.sbuf_tensor("a_zst", [64, 8 * 8], F32))
                bzst = P.buf("zst")
                zneg8 = ae(nc.sbuf_tensor("a_zneg", [64, 8], F32))
                bzneg = P.buf("zneg8")
                with ExitStack() as p1ctx:
                    p1 = p1ctx.enter_context
                    wst = p1(nc.sbuf_tensor("a_wst", [128, NCH * 512], BF16))
                    s_win = p1(nc.sbuf_tensor("a_swin", [64, HA * 512], F32))
                    zdump = p1(nc.sbuf_tensor("a_zdump", [64, 512], F32))
                    bwst = [P.buf(f"wst{i}") for i in range(4)]
                    bswin = P.buf("swin")
                    bzdump = P.buf("zdump")
                    P.op("pool", lambda en: en.dma_start(out=wst[:], in_=din["ws_a"]),
                         writes=list(bwst), dma=True, tag="ws_all")
                    for w in range(4):
                        gather_xnw(w, f"p1x{w}")
                        for h in range(HA):
                            hl = h % 4
                            reg = ps2[0:64, hl * 512:(hl + 1) * 512]
                            rb = pb2[hl]
                            for c in range(NCH):
                                P.op("pe", lambda en, h=h, c=c, reg=reg:
                                     en.matmul(reg, wst[:, c * 512 + h * 64:c * 512 + h * 64 + 64],
                                               xnw[:, c * 512:(c + 1) * 512],
                                               start=(c == 0), stop=(c == NCH - 1)),
                                     reads=[bwst[h // 2], bxnw[c]],
                                     writes=[rb] if c == 0 else (), update=[rb] if c == NCH - 1 else (),
                                     tag=f"sproj{w}_{h}_{c}")
                            P.op("dve", lambda en, h=h, reg=reg, rb=rb: en.tensor_scalar(
                                out=s_win[:, h * 512:(h + 1) * 512], in0=reg,
                                scalar1=-CLAMP, scalar2=CLAMP, op0=ALU.max, op1=ALU.min),
                                reads=[rb],
                                writes=[bswin] if h == 0 else (), update=[bswin] if h == HA - 1 else (),
                                tag=f"sclip{w}_{h}")
                        P.op("sp", lambda en, w=w: en.dma_start(
                            out=s_spill.rearrange("m (h t) -> m h t", h=HA)[:, :, w * 512:(w + 1) * 512],
                            in_=s_win.ap().rearrange("m (h t) -> m h t", h=HA)),
                            reads=[bswin], writes=[bspill[w]], dma=True, tag=f"sspill{w}")
                        for h in range(HA):
                            P.op("dve", lambda en, h=h: en.reduce_max(
                                zst[:, h:h + 1], s_win[:, h * 512:(h + 1) * 512], axis=mybir.AxisListType.X),
                                reads=[bswin],
                                writes=[bzst] if (w == 0 and h == 0) else (), update=[bzst],
                                tag=f"zmax{w}_{h}")
                        P.op("dve", lambda en: en.tensor_scalar_mul(zst[:, 48:56], zst[:, 0:8], -1.0),
                             reads=[bzst], update=[bzst], tag=f"znegw{w}")
                        for h in range(HA):
                            P.op("act", lambda en, h=h: en.activation(
                                out=zdump[:], in_=s_win[:, h * 512:(h + 1) * 512], func=AF.Exp,
                                bias=zst[:, 48 + h:49 + h], accum_out=zst[:, 8 + h:9 + h]),
                                reads=[bswin, bzst], writes=[bzdump], update=[bzst], tag=f"zexp{w}_{h}")
                        if w == 0:
                            P.op("dve", lambda en: en.tensor_copy(zst[:, 16:24], zst[:, 0:8]),
                                 reads=[bzst], update=[bzst], tag="zm0")
                            P.op("dve", lambda en: en.tensor_copy(zst[:, 24:32], zst[:, 8:16]),
                                 reads=[bzst], update=[bzst], tag="zs0")
                        else:
                            P.op("dve", lambda en: en.tensor_max(zst[:, 32:40], zst[:, 16:24], zst[:, 0:8]),
                                 reads=[bzst], update=[bzst], tag=f"zmg{w}a")
                            P.op("dve", lambda en: en.tensor_sub(zst[:, 48:56], zst[:, 16:24], zst[:, 32:40]),
                                 reads=[bzst], update=[bzst], tag=f"zmg{w}b")
                            P.op("act", lambda en: en.activation(out=zst[:, 48:56], in_=zst[:, 48:56], func=AF.Exp),
                                 reads=[bzst], update=[bzst], tag=f"zmg{w}c")
                            P.op("dve", lambda en: en.tensor_mul(zst[:, 24:32], zst[:, 24:32], zst[:, 48:56]),
                                 reads=[bzst], update=[bzst], tag=f"zmg{w}d")
                            P.op("dve", lambda en: en.tensor_sub(zst[:, 56:64], zst[:, 0:8], zst[:, 32:40]),
                                 reads=[bzst], update=[bzst], tag=f"zmg{w}e")
                            P.op("act", lambda en: en.activation(out=zst[:, 56:64], in_=zst[:, 56:64], func=AF.Exp),
                                 reads=[bzst], update=[bzst], tag=f"zmg{w}f")
                            P.op("dve", lambda en: en.tensor_mul(zst[:, 48:56], zst[:, 8:16], zst[:, 56:64]),
                                 reads=[bzst], update=[bzst], tag=f"zmg{w}g")
                            P.op("dve", lambda en: en.tensor_add(zst[:, 24:32], zst[:, 24:32], zst[:, 48:56]),
                                 reads=[bzst], update=[bzst], tag=f"zmg{w}h")
                            P.op("dve", lambda en: en.tensor_copy(zst[:, 16:24], zst[:, 32:40]),
                                 reads=[bzst], update=[bzst], tag=f"zmg{w}i")
                    P.op("act", lambda en: en.activation(out=zst[:, 56:64], in_=zst[:, 24:32], func=AF.Ln),
                         reads=[bzst], update=[bzst], tag="zlog")
                    P.op("dve", lambda en: en.tensor_add(zst[:, 0:8], zst[:, 16:24], zst[:, 56:64]),
                         reads=[bzst], update=[bzst], tag="zfin")
                    P.op("dve", lambda en: en.tensor_scalar_mul(zneg8[:], zst[:, 0:8], -1.0),
                         reads=[bzst], writes=[bzneg], tag="znegf")
                    P.fence([bwst[0], bwst[1], bwst[2], bwst[3], bswin, bzdump] + bxnw, tag="p1f")

                wqk = ae(nc.sbuf_tensor("a_wqk", [128, 2 * 2048], BF16))
                wos = ae(nc.sbuf_tensor("a_wos", [128, 4 * 2048], BF16))
                wvg = ae(nc.sbuf_tensor("a_wvg", [128, 2 * 1024], BF16))
                q_win = ae(nc.sbuf_tensor("a_qwin", [64, HA * 512], BF16))
                k_win = ae(nc.sbuf_tensor("a_kwin", [64, HA * 512], BF16))
                v_win = ae(nc.sbuf_tensor("a_vwin", [128, 4 * 1024], BF16))
                g_win = ae(nc.sbuf_tensor("a_gwin", [128, 4 * 1024], BF16))
                s_ch = ae(nc.sbuf_tensor("a_sch", [64, HA * 128], F32))
                E_f = ae(nc.sbuf_tensor("a_Ef", [64, HA * 128], F32))
                E_bf = ae(nc.sbuf_tensor("a_Ebf", [64, HA * 128], BF16))
                k_f32 = ae(nc.sbuf_tensor("a_kf32", [64, HA * 128], F32))
                D_s = ae(nc.sbuf_tensor("a_Ds", [64, HA * 128], F32))
                R_t = ae(nc.sbuf_tensor("a_Rt", [64, HA * 128], F32))
                okh = ae(nc.sbuf_tensor("a_okh", [64, HA * 128], F32))
                pv_bf = ae(nc.sbuf_tensor("a_pv", [64, HA * 128], BF16))
                p_hat = ae(nc.sbuf_tensor("a_phat", [64, HA * 128], BF16))
                P_tm = ae(nc.sbuf_tensor("a_Ptm", [128, HA * 64], BF16))
                k_tm = ae(nc.sbuf_tensor("a_ktm", [128, HA * 64], BF16))
                ST_m = ae(nc.sbuf_tensor("a_STm", [128, HA * 128], BF16))
                Wt_m = ae(nc.sbuf_tensor("a_Wtm", [128, HA * 128], BF16))
                HK_f = ae(nc.sbuf_tensor("a_HKf", [64, HA * 64], F32))
                HK_bf = ae(nc.sbuf_tensor("a_HKbf", [64, HA * 64], BF16))
                HV_f = ae(nc.sbuf_tensor("a_HVf", [64, HA * 128], F32))
                HV_bf = ae(nc.sbuf_tensor("a_HVbf", [64, HA * 128], BF16))
                carry = ae(nc.sbuf_tensor("a_carry", [64, 8], F32))
                ones_scan = ae(nc.sbuf_tensor("a_ones", [64, 128], F32))
                sigf = ae(nc.sbuf_tensor("a_sigf", [128, 48], F32))
                o2d = ae(nc.sbuf_tensor("a_o2d", [128, 128], BF16))
                o_g = ae(nc.sbuf_tensor("a_og", [128, HA * 128], F32))
                og_fm = ae(nc.sbuf_tensor("a_ogfm", [128, HA * 512], BF16))
                stg = ae(nc.sbuf_tensor("a_stg", [128, 512], F32))

                bwqk = [P.buf("wqk0"), P.buf("wqk1")]
                bwos = [P.buf(f"wos{i}") for i in range(4)]
                bwvg = [P.buf("wvg0"), P.buf("wvg1")]
                bqw, bkw, bvw, bgw = P.buf("qwin"), P.buf("kwin"), P.buf("vwin"), P.buf("gwin")
                bsch = [P.buf("sch0")]
                bEf, bEbf, bkf32 = P.buf("Ef"), P.buf("Ebf"), P.buf("kf32")
                bDs, bRt = P.buf("Ds"), P.buf("Rt")
                bokh, bpv, bph = P.buf("okh"), P.buf("pv"), P.buf("phat")
                bPtm, bktm, bSTm, bWtm = P.buf("Ptm"), P.buf("ktm"), P.buf("STm"), P.buf("Wtm")
                bHKf, bHKbf, bHVf, bHVbf = P.buf("HKf"), P.buf("HKbf"), P.buf("HVf"), P.buf("HVbf")
                bcarry, bones_s, bsigf = P.buf("carry"), P.buf("ones_s"), P.buf("sigf")
                bo2, bog, bogfm = P.buf("o2"), P.buf("og"), P.buf("ogfm")
                bstg = [P.buf("stg0")]

                P.op("dve", lambda en: en.memset(ones_scan[:], 1.0), writes=[bones_s], tag="amemset_ones")
                P.op("dve", lambda en: en.memset(carry[:], 0.0), writes=[bcarry], tag="amemset_c")
                P.op("dve", lambda en: en.memset(HK_f[:], 0.0), writes=[bHKf], tag="amemset_hkf")
                P.op("dve", lambda en: en.memset(HK_bf[:], 0.0), writes=[bHKbf], tag="amemset_hkb")
                P.op("dve", lambda en: en.memset(HV_f[:], 0.0), writes=[bHVf], tag="amemset_hvf")
                P.op("dve", lambda en: en.memset(HV_bf[:], 0.0), writes=[bHVbf], tag="amemset_hvb")

                for w in range(4):
                    gather_xnw(w, f"p2x{w}")
                    for (wname, wslab, bslab, dwin, bwin, act_silu) in [
                            ("wv_a", wvg, bwvg, v_win, bvw, False),
                            ("wg_a", wvg, bwvg, g_win, bgw, True)]:
                        for c in range(NCH):
                            cp = c % 2
                            P.op("pool", lambda en, c=c, cp=cp, wname=wname, wslab=wslab: en.dma_start(
                                out=wslab[:, cp * 1024:(cp + 1) * 1024],
                                in_=din[wname][c * 128:(c + 1) * 128, :]),
                                writes=[bslab[cp]], dma=True, tag=f"{wname}s{w}_{c}")
                            for ridx in range(8):
                                tc4, hf = ridx // 2, ridx % 2
                                reg = (ps1 if ridx < 4 else ps2)[:, (ridx % 4) * 512:(ridx % 4 + 1) * 512]
                                rb = (pb1 if ridx < 4 else pb2)[ridx % 4]
                                P.op("pe", lambda en, c=c, cp=cp, tc4=tc4, hf=hf, reg=reg, wslab=wslab:
                                     en.matmul(reg, xnw[:, c * 512 + tc4 * 128:c * 512 + tc4 * 128 + 128],
                                               wslab[:, cp * 1024 + hf * 512:cp * 1024 + hf * 512 + 512],
                                               start=(c == 0), stop=(c == NCH - 1)),
                                     reads=[bxnw[c], bslab[cp]],
                                     writes=[rb] if c == 0 else (), update=[rb] if c == NCH - 1 else (),
                                     tag=f"{wname}mm{w}_{c}_{ridx}")
                        for ridx in range(8):
                            tc4, hf = ridx // 2, ridx % 2
                            reg = (ps1 if ridx < 4 else ps2)[:, (ridx % 4) * 512:(ridx % 4 + 1) * 512]
                            rb = (pb1 if ridx < 4 else pb2)[ridx % 4]
                            dst = dwin[:, tc4 * 1024 + hf * 512:tc4 * 1024 + hf * 512 + 512]
                            if act_silu:
                                P.op("act", lambda en, reg=reg, dst=dst: en.activation(out=dst, in_=reg, func=AF.Silu),
                                     reads=[rb], writes=[bwin] if ridx == 0 else (), update=[bwin] if ridx == 7 else (),
                                     tag=f"gsilu{w}_{ridx}")
                            else:
                                P.op("dve", lambda en, reg=reg, dst=dst: en.tensor_copy(dst, reg),
                                     reads=[rb], writes=[bwin] if ridx == 0 else (), update=[bwin] if ridx == 7 else (),
                                     tag=f"vevac{w}_{ridx}")
                    for (wname, wslab, bslab, dwin, bwin, scl) in [
                            ("wq_a", wqk, bwqk, q_win, bqw, QSCALE),
                            ("wk_a", wqk, bwqk, k_win, bkw, 1.0)]:
                        for pp in range(4):
                            par = pp % 2
                            P.op("pool", lambda en, pp=pp, par=par, wname=wname, wslab=wslab: en.dma_start(
                                out=wslab[:, par * 2048:(par + 1) * 2048],
                                in_=din[wname][:, pp * 2048:(pp + 1) * 2048]),
                                writes=[bslab[par]], dma=True, tag=f"{wname}l{w}_{pp}")
                            for hl in range(2):
                                h = pp * 2 + hl
                                reg = ps2[0:64, (h % 4) * 512:(h % 4 + 1) * 512]
                                rb = pb2[h % 4]
                                for c in range(NCH):
                                    P.op("pe", lambda en, h=h, c=c, par=par, reg=reg, wslab=wslab, hl=hl:
                                         en.matmul(reg, wslab[:, par * 2048 + c * 128 + hl * 64:par * 2048 + c * 128 + hl * 64 + 64],
                                                   xnw[:, c * 512:(c + 1) * 512],
                                                   start=(c == 0), stop=(c == NCH - 1)),
                                         reads=[bslab[par], bxnw[c]],
                                         writes=[rb] if c == 0 else (), update=[rb] if c == NCH - 1 else (),
                                         tag=f"{wname}mm{w}_{h}_{c}")
                                P.op("act", lambda en, h=h, reg=reg, dwin=dwin, scl=scl: en.activation(
                                    out=dwin[:, h * 512:(h + 1) * 512], in_=reg, func=AF.Identity, scale=scl),
                                    reads=[rb], writes=[bwin] if h == 0 else (), update=[bwin] if h == 7 else (),
                                    tag=f"{wname}ev{w}_{h}")

                    for tc4 in range(4):
                        cg = w * 4 + tc4
                        par = 0
                        schv = s_ch[:]
                        P.op("sp", lambda en, cg=cg, schv=schv: en.dma_start(
                            out=schv.rearrange("m (h t) -> m h t", h=HA),
                            in_=s_spill.rearrange("m (h t) -> m h t", h=HA)[:, :, cg * 128:(cg + 1) * 128]),
                            reads=[bspill[cg // 4]], writes=[bsch[par]], dma=True, tag=f"sld{cg}")
                        for h in range(HA):
                            P.op("act", lambda en, h=h, schv=schv: en.activation(
                                out=E_f[:, h * 128:(h + 1) * 128], in_=schv[:, h * 128:(h + 1) * 128],
                                func=AF.Exp, bias=zneg8[:, h:h + 1]),
                                reads=[bsch[par], bzneg],
                                writes=[bEf] if h == 0 else (), update=[bEf] if h == HA - 1 else (),
                                tag=f"E{cg}_{h}")
                        P.op("dve", lambda en: en.tensor_copy(E_bf[:], E_f[:]),
                             reads=[bEf], writes=[bEbf], tag=f"Ebf{cg}")
                        P.op("dve", lambda en, tc4=tc4: en.tensor_copy(
                            k_f32.ap().rearrange("m (h t) -> m h t", h=HA),
                            k_win.ap().rearrange("m (h t) -> m h t", h=HA)[:, :, tc4 * 128:(tc4 + 1) * 128]),
                            reads=[bkw], writes=[bkf32], tag=f"kf32{cg}")
                        for h in range(HA):
                            P.op("dve", lambda en, h=h: en.tensor_tensor_scan(
                                out=D_s[:, h * 128:(h + 1) * 128], data0=ones_scan[:],
                                data1=E_f[:, h * 128:(h + 1) * 128], initial=carry[:, h:h + 1],
                                op0=ALU.mult, op1=ALU.add),
                                reads=[bEf, bones_s, bcarry],
                                writes=[bDs] if h == 0 else (), update=[bDs] if h == HA - 1 else (),
                                tag=f"scan{cg}_{h}")
                        P.op("dve", lambda en: en.tensor_copy(
                            carry[:], D_s.ap().rearrange("m (h t) -> m h t", h=HA)[:, :, 127]),
                            reads=[bDs], writes=[bcarry], tag=f"carry{cg}")
                        P.op("act", lambda en: en.activation(out=okh[:], in_=D_s[:], func=AF.Ln),
                             reads=[bDs], writes=[bokh], tag=f"lnD{cg}")
                        P.op("act", lambda en: en.activation(out=R_t[:], in_=okh[:], func=AF.Exp, scale=-1.0),
                             reads=[bokh], writes=[bRt], tag=f"R{cg}")
                        for h in range(HA):
                            P.op("pe", lambda en, h=h: en.transpose(
                                ps1[:, h * 64:(h + 1) * 64], E_f[:, h * 128:(h + 1) * 128], identf[0:64, 0:64]),
                                reads=[bEf, bconsts["identf"]],
                                writes=[pb1[0]] if h == 0 else (),
                                update=[pb1[0]] if h == HA - 1 else (), tag=f"trP{cg}_{h}")
                        for h in range(HA):
                            P.op("pe", lambda en, h=h: en.transpose(
                                ps1[:, 512 + h * 64:512 + (h + 1) * 64], k_f32[:, h * 128:(h + 1) * 128], identf[0:64, 0:64]),
                                reads=[bkf32, bconsts["identf"]],
                                writes=[pb1[1]] if h == 0 else (),
                                update=[pb1[1]] if h == HA - 1 else (), tag=f"trK{cg}_{h}")
                        P.op("dve", lambda en: en.tensor_copy(P_tm[:], ps1[:, 0:512]),
                             reads=[pb1[0]], writes=[bPtm], tag=f"Ptm{cg}")
                        P.op("dve", lambda en: en.tensor_copy(k_tm[:], ps1[:, 512:1024]),
                             reads=[pb1[1]], writes=[bktm], tag=f"ktm{cg}")
                        for h in range(HA):
                            P.op("pe", lambda en, h=h, tc4=tc4: en.matmul(
                                ps1[:, h * 128:(h + 1) * 128],
                                k_win[:, h * 512 + tc4 * 128:h * 512 + tc4 * 128 + 128],
                                q_win[:, h * 512 + tc4 * 128:h * 512 + tc4 * 128 + 128],
                                start=True, stop=True),
                                reads=[bkw, bqw],
                                writes=[pb1[0], pb1[1]] if h == 0 else (),
                                update=[pb1[0], pb1[1]] if h == HA - 1 else (), tag=f"ST{cg}_{h}")
                        P.op("dve", lambda en: en.tensor_mul(ST_m[:], ps1[:, 0:1024], lmask8[:]),
                             reads=[pb1[0], pb1[1], bconsts["lmask8"]], writes=[bSTm], tag=f"STm{cg}")
                        for h in range(HA):
                            P.op("pe", lambda en, h=h: en.matmul(
                                ps2[0:64, h * 128:(h + 1) * 128], P_tm[:, h * 64:(h + 1) * 64],
                                ST_m[:, h * 128:(h + 1) * 128], start=True, stop=False),
                                reads=[bPtm, bSTm],
                                writes=[pb2[0], pb2[1]] if h == 0 else (), tag=f"oki{cg}_{h}")
                            P.op("pe", lambda en, h=h, tc4=tc4: en.matmul(
                                ps2[0:64, h * 128:(h + 1) * 128], HK_bf[:, h * 64:(h + 1) * 64],
                                q_win[:, h * 512 + tc4 * 128:h * 512 + tc4 * 128 + 128],
                                start=False, stop=True),
                                reads=[bHKbf, bqw],
                                update=[pb2[0], pb2[1]] if h == HA - 1 else (), tag=f"okc{cg}_{h}")
                        P.op("dve", lambda en: en.tensor_mul(okh[:], ps2[0:64, 0:1024], R_t[:]),
                             reads=[pb2[0], pb2[1], bRt], writes=[bokh], tag=f"okh{cg}")
                        P.op("act", lambda en: en.activation(out=pv_bf[:], in_=okh[:], func=AF.Exp),
                             reads=[bokh], writes=[bpv], tag=f"pv{cg}")
                        P.op("dve", lambda en: en.tensor_mul(p_hat[:], pv_bf[:], R_t[:]),
                             reads=[bpv, bRt], writes=[bph], tag=f"ph{cg}")
                        for h in range(HA):
                            P.op("pe", lambda en, h=h: en.matmul(
                                ps2[:, 2040 + h:2041 + h], pv_bf[:, h * 128:(h + 1) * 128],
                                ones_bf[0:64, :], start=True, stop=True),
                                reads=[bpv, bconsts["ones_bf"]],
                                writes=[pb2[3]] if h == 0 else (), update=[pb2[3]] if h == HA - 1 else (),
                                tag=f"sig{cg}_{h}")
                        for h in range(HA):
                            P.op("pe", lambda en, h=h: en.matmul(
                                ps1[:, h * 128:(h + 1) * 128], E_bf[:, h * 128:(h + 1) * 128],
                                p_hat[:, h * 128:(h + 1) * 128], start=True, stop=True),
                                reads=[bEbf, bph],
                                writes=[pb1[0], pb1[1]] if h == 0 else (),
                                update=[pb1[0], pb1[1]] if h == HA - 1 else (), tag=f"W{cg}_{h}")
                        P.op("dve", lambda en: en.tensor_mul(Wt_m[:], ps1[:, 0:1024], lmask8[:]),
                             reads=[pb1[0], pb1[1], bconsts["lmask8"]], writes=[bWtm], tag=f"Wtm{cg}")
                        for h in range(HA):
                            P.op("pe", lambda en, h=h, tc4=tc4: en.matmul(
                                ps2[:, h * 128:(h + 1) * 128], Wt_m[:, h * 128:(h + 1) * 128],
                                v_win[:, tc4 * 1024 + h * 128:tc4 * 1024 + h * 128 + 128],
                                start=True, stop=False),
                                reads=[bWtm, bvw],
                                writes=[pb2[0], pb2[1]] if h == 0 else (), tag=f"oi{cg}_{h}")
                            P.op("pe", lambda en, h=h: en.matmul(
                                ps2[:, h * 128:(h + 1) * 128], p_hat[:, h * 128:(h + 1) * 128],
                                HV_bf[:, h * 128:(h + 1) * 128], start=False, stop=True),
                                reads=[bph, bHVbf],
                                update=[pb2[0], pb2[1]] if h == HA - 1 else (), tag=f"oc{cg}_{h}")
                        P.op("act", lambda en: en.activation(out=sigf[:, 0:8], in_=ps2[:, 2040:2048], func=AF.Square),
                             reads=[pb2[3]], writes=[bsigf], tag=f"sg2{cg}")
                        for h in range(HA):
                            P.op("act", lambda en, h=h: en.activation(
                                out=o2d[:], in_=ps2[:, h * 128:(h + 1) * 128], func=AF.Square,
                                accum_out=sigf[:, 8 + h:9 + h]),
                                reads=[pb2[0], pb2[1]], writes=[bo2], update=[bsigf], tag=f"ss{cg}_{h}")
                        P.op("dve", lambda en: en.tensor_scalar_mul(sigf[:, 16:24], sigf[:, 8:16], 1.0 / DV),
                             reads=[bsigf], update=[bsigf], tag=f"f1{cg}")
                        P.op("dve", lambda en: en.scalar_tensor_tensor(
                            out=sigf[:, 24:32], in0=sigf[:, 0:8], scalar=A_EPS, in1=sigf[:, 16:24],
                            op0=ALU.mult, op1=ALU.add),
                            reads=[bsigf], update=[bsigf], tag=f"f2{cg}")
                        P.op("act", lambda en: en.activation(out=sigf[:, 32:40], in_=sigf[:, 24:32], func=AF.Ln),
                             reads=[bsigf], update=[bsigf], tag=f"f3{cg}")
                        P.op("act", lambda en: en.activation(out=sigf[:, 32:40], in_=sigf[:, 32:40], func=AF.Exp, scale=-0.5),
                             reads=[bsigf], update=[bsigf], tag=f"f4{cg}")
                        for h in range(HA):
                            P.op("dve", lambda en, h=h, tc4=tc4: en.scalar_tensor_tensor(
                                out=o_g[:, h * 128:(h + 1) * 128], in0=ps2[:, h * 128:(h + 1) * 128],
                                scalar=sigf[:, 32 + h:33 + h],
                                in1=g_win[:, tc4 * 1024 + h * 128:tc4 * 1024 + h * 128 + 128],
                                op0=ALU.mult, op1=ALU.mult),
                                reads=[pb2[0], pb2[1], bsigf, bgw],
                                writes=[bog] if h == 0 else (), update=[bog] if h == HA - 1 else (),
                                tag=f"og{cg}_{h}")
                        for h in range(HA):
                            P.op("pe", lambda en, h=h: en.matmul(
                                ps2[0:64, 1024 + h * 64:1024 + (h + 1) * 64], k_tm[:, h * 64:(h + 1) * 64],
                                P_tm[:, h * 64:(h + 1) * 64], start=True, stop=True),
                                reads=[bktm, bPtm],
                                writes=[pb2[2]] if h == 0 else (), update=[pb2[2]] if h == HA - 1 else (),
                                tag=f"dhk{cg}_{h}")
                            P.op("pe", lambda en, h=h, tc4=tc4: en.matmul(
                                ps1[0:64, 1024 + h * 128:1024 + (h + 1) * 128], P_tm[:, h * 64:(h + 1) * 64],
                                v_win[:, tc4 * 1024 + h * 128:tc4 * 1024 + h * 128 + 128],
                                start=True, stop=True),
                                reads=[bPtm, bvw],
                                writes=[pb1[2], pb1[3]] if h == 0 else (),
                                update=[pb1[2], pb1[3]] if h == HA - 1 else (), tag=f"dhv{cg}_{h}")
                        P.op("dve", lambda en: en.tensor_add(HK_f[:], HK_f[:], ps2[0:64, 1024:1536]),
                             reads=[pb2[2]], writes=[bHKf], tag=f"hkadd{cg}")
                        P.op("dve", lambda en: en.tensor_add(HV_f[:], HV_f[:], ps1[0:64, 1024:2048]),
                             reads=[pb1[2], pb1[3]], writes=[bHVf], tag=f"hvadd{cg}")
                        P.op("dve", lambda en: en.tensor_copy(HK_bf[:], HK_f[:]),
                             reads=[bHKf], writes=[bHKbf], tag=f"hkbf{cg}")
                        P.op("dve", lambda en: en.tensor_copy(HV_bf[:], HV_f[:]),
                             reads=[bHVf], writes=[bHVbf], tag=f"hvbf{cg}")
                        for h in range(HA):
                            P.op("pe", lambda en, h=h: en.transpose(
                                ps1[:, h * 128:(h + 1) * 128], o_g[:, h * 128:(h + 1) * 128], identf[:]),
                                reads=[bog, bconsts["identf"]],
                                writes=[pb1[0], pb1[1]] if h == 0 else (),
                                update=[pb1[0], pb1[1]] if h == HA - 1 else (),
                                tag=f"trO{cg}_{h}")
                        P.op("dve", lambda en, tc4=tc4: en.tensor_copy(
                            og_fm.ap().rearrange("p (h t) -> p h t", h=HA)[:, :, tc4 * 128:(tc4 + 1) * 128],
                            ps1.ap().rearrange("p (h t) -> p h t", h=16)[:, 0:8, :]),
                            reads=[pb1[0], pb1[1]],
                            writes=[bogfm] if tc4 == 0 else (), update=[bogfm] if tc4 == 3 else (),
                            tag=f"ogfm{cg}")
                    for half in range(2):
                        for hh_ in range(4):
                            hglob = half * 4 + hh_
                            P.op("pool", lambda en, hglob=hglob, hh_=hh_: en.dma_start(
                                out=wos[:, hh_ * 2048:(hh_ + 1) * 2048],
                                in_=din["wo_a"][hglob * 128:(hglob + 1) * 128, :]),
                                writes=[bwos[hh_]], dma=True, tag=f"ldwo{w}_{half}_{hh_}")
                        for hh_ in range(4):
                            P.op("dve", lambda en, hh_=hh_: en.tensor_scalar_mul(
                                wos[:, hh_ * 2048:(hh_ + 1) * 2048], wos[:, hh_ * 2048:(hh_ + 1) * 2048], gnw[:]),
                                reads=[bconsts["gnw"]], writes=[bwos[hh_]], tag=f"gnwf{w}_{half}_{hh_}")
                        for dc in range(16):
                            reg = ps2[:, (dc % 2) * 512:(dc % 2) * 512 + 512]
                            rb = pb2[dc % 2]
                            for hh_ in range(4):
                                hglob = half * 4 + hh_
                                P.op("pe", lambda en, dc=dc, hh_=hh_, hglob=hglob, reg=reg: en.matmul(
                                    reg, wos[:, hh_ * 2048 + dc * 128:hh_ * 2048 + (dc + 1) * 128],
                                    og_fm[:, hglob * 512:(hglob + 1) * 512], start=(hh_ == 0), stop=(hh_ == 3)),
                                    reads=[bwos[hh_], bogfm],
                                    writes=[rb] if hh_ == 0 else (), update=[rb] if hh_ == 3 else (),
                                    tag=f"wo{w}_{half}_{dc}_{hh_}")
                            sreg = stg[:]
                            if half == 0:
                                P.op("act", lambda en, reg=reg, sreg=sreg: en.activation(out=sreg, in_=reg, func=AF.Identity),
                                     reads=[rb], writes=[bstg[0]], tag=f"woev{w}_{half}_{dc}")
                                pt_toks[w % 2].append(P.op("sp", lambda en, w=w, dc=dc, sreg=sreg: en.dma_start(
                                    out=ag_pt_in2[w % 2][(w // 2) * D + dc * 128:(w // 2) * D + (dc + 1) * 128, :],
                                    in_=sreg),
                                    reads=[bstg[0]], writes=[bag_ptr[w][dc]], dma=True, tag=f"wodma{w}_{half}_{dc}"))
                            else:
                                P.op("act", lambda en, reg=reg, sreg=sreg: en.activation(out=sreg, in_=reg, func=AF.Identity),
                                     reads=[rb], writes=[bstg[0]], tag=f"woev{w}_{half}_{dc}")
                                pt_toks[w % 2].append(P.op("pool", lambda en, w=w, dc=dc, sreg=sreg: en.dma_start(
                                    out=ag_pt_in2[w % 2][(w // 2) * D + dc * 128:(w // 2) * D + (dc + 1) * 128, :],
                                    in_=sreg, accum_op=ALU.add),
                                    reads=[bstg[0]], writes=[bag_ptr[w][dc]], dma=True, tag=f"wodma{w}_{half}_{dc}"))
                    if w == 2:
                        emit_rs(0, "ag_pt0")

                P.fence([bqw, bkw, bvw, bgw, bEf, bEbf, bkf32, bDs, bRt, bokh, bpv, bph,
                         bPtm, bktm, bSTm, bWtm, bHKf, bHKbf, bHVf, bHVbf, bcarry, bones_s,
                         bsigf, bo2, bog, bogfm, bstg[0], bwos[0], bwos[1], bwos[2], bwos[3],
                         bsch[0], bzst, bzneg] + bwqk + bwvg + bxnw, tag="a2f")
            emit_rs(1, "ag_pt1")
            with ExitStack() as rctx:
                re_ = rctx.enter_context
                rb0 = re_(nc.sbuf_tensor("a_rb0", [128, 2 * TC], F32))
                brb0 = [[P.buf(f"rb0{i}_{ch}") for ch in range(2)] for i in range(2)]
                for c in range(NCH):
                    par = c % 2
                    for colh in range(2):
                        r0v = rb0[:, par * TC + colh * 512:par * TC + colh * 512 + 512]
                        P.op("sp" if colh == 0 else "pool", lambda en, c=c, colh=colh, r0v=r0v: en.dma_start(
                            out=r0v, in_=ag_pt_out2[colh][c * 128:(c + 1) * 128, :]),
                            reads=[bag_pt_out2[colh]], writes=[brb0[par][colh]], dma=True, tag=f"rb0_{c}_{colh}")
                        sl = slice(c * TC + colh * 512, c * TC + colh * 512 + 512)
                        P.op("dve", lambda en, sl=sl, r0v=r0v: en.tensor_add(x_res[:, sl], x_res[:, sl], r0v),
                             reads=[brb0[par][colh]], writes=[bxc[c]], tag=f"res0_{c}_{colh}")
                P.fence([b for row in brb0 for b in row], tag="rbf")

        def ffn_sublayer(tag):
            with ExitStack() as fctx:
                fe = fctx.enter_context
                if conv_state["i"] < len(conv_jobs) and conv_state["stg"] is None:
                    conv_state["stg"] = fe(nc.sbuf_tensor(tag + "_cstg", [128, 2 * 1024], BF16))

                def wsrc3(name, par):
                    if conv_done(name):
                        return dcache[name], ("sp" if par == 0 else "pool"), [bcache[name]]
                    return din[name], "pool", []
                xn_f = fe(nc.sbuf_tensor(tag + "_xn", [128, NCH * TC], BF16))
                xn1_f = fe(nc.sbuf_tensor(tag + "_xn1", [128, NCH * TC], BF16))
                bxnf = [P.buf(tag + f"xnf{c}") for c in range(NCH)]
                bxn1 = [P.buf(tag + f"xn1{c}") for c in range(NCH)]
                layernorm(tag + "ln", xn_f, bxnf)
                wg8 = fe(nc.sbuf_tensor(tag + "_wg8", [128, 256], BF16))
                bgc = fe(nc.sbuf_tensor(tag + "_bg", [8, 2], F32))
                bupc = fe(nc.sbuf_tensor(tag + "_bup", [128, 48], F32))
                bswc = fe(nc.sbuf_tensor(tag + "_bsw", [128, 48], F32))
                bsac = fe(nc.sbuf_tensor(tag + "_bsa", [128, 8], F32))
                bsuc = fe(nc.sbuf_tensor(tag + "_bsu", [128, 16], F32))
                bdn6 = fe(nc.sbuf_tensor(tag + "_bdn6", [6, 512], BF16))
                bsd1 = fe(nc.sbuf_tensor(tag + "_bsd1", [1, 512], BF16))
                ones512 = fe(nc.sbuf_tensor(tag + "_ones512", [1, 512], BF16))
                g8r = fe(nc.sbuf_tensor(tag + "_g8r", [8, 512], F32))
                gtm = fe(nc.sbuf_tensor(tag + "_gtm", [128, 8], F32))
                gsc = fe(nc.sbuf_tensor(tag + "_gsc", [128, 112], F32))
                w_fm = fe(nc.sbuf_tensor(tag + "_wfm", [8, TC], F32))
                w_bf = fe(nc.sbuf_tensor(tag + "_wbf", [8, TC], BF16))
                wb_n = fe(nc.sbuf_tensor(tag + "_wbn", [128, TC], BF16))
                u_n = fe(nc.sbuf_tensor(tag + "_un", [128, 8 * TC], BF16))
                silu_t = fe(nc.sbuf_tensor(tag + "_silu", [128, 512], BF16))
                t2b = fe(nc.sbuf_tensor(tag + "_t2", [128, 512], BF16))
                hst = fe(nc.sbuf_tensor(tag + "_hst", [128, 2 * 512], BF16))
                wups = fe(nc.sbuf_tensor(tag + "_wus", [128, 2 * 2048], BF16))
                wsws = wsus = wsas = wups
                wdns = fe(nc.sbuf_tensor(tag + "_wds", [128, 2 * 2048], BF16))
                wsds = wdns
                h_n = fe(nc.sbuf_tensor(tag + "_hn", [128, 8 * TC], BF16))
                shl = fe(nc.sbuf_tensor(tag + "_shl", [128, 2 * TC], BF16))
                xcs = fe(nc.sbuf_tensor(tag + "_xcs", [128, TC], F32)) if False else None

                bwg8, bbg = P.buf(tag + "wg8"), P.buf(tag + "bg")
                bbup, bbsw, bbsa, bbsu = P.buf(tag + "bup"), P.buf(tag + "bsw"), P.buf(tag + "bsa"), P.buf(tag + "bsu")
                bbdn6, bbsd1, bones512 = P.buf(tag + "bdn6"), P.buf(tag + "bsd1"), P.buf(tag + "o512")
                bg8r, bgtm, bgsc = P.buf(tag + "g8r"), P.buf(tag + "gtm"), P.buf(tag + "gsc")
                bwfm, bwbf, bwbn = P.buf(tag + "wfm"), P.buf(tag + "wbf"), P.buf(tag + "wbn")
                bun, bsilu, bt2, bhst = P.buf(tag + "un"), P.buf(tag + "silu"), P.buf(tag + "t2"), P.buf(tag + "hst")
                bwup = [P.buf(tag + "wus0"), P.buf(tag + "wus1")]
                bwsw = bwsu = bwsa = bwup
                bwdn = [P.buf(tag + "wds0"), P.buf(tag + "wds1")]
                bwsd = bwdn
                bhn = [P.buf(tag + "hn0"), P.buf(tag + "hn1")]
                bshl = [P.buf(tag + "shl0"), P.buf(tag + "shl1")]
                us_ctr = [0]
                ds_ctr = [0]
                bhsp = [[P.buf(tag + f"hsp{n}_{f}") for f in range(8)] for n in range(6)]
                bssp = [P.buf(tag + f"ssp{i}") for i in range(16)]

                P.op("pool", lambda en: en.dma_start(out=wg8[:], in_=din["f_wg_t"]), writes=[bwg8], dma=True)
                P.op("sp", lambda en: en.dma_start(out=bgc[:], in_=din["f_bg_t"]), writes=[bbg], dma=True)
                P.op("sp", lambda en: en.dma_start(out=bupc[:], in_=din["f_bup_t"]), writes=[bbup], dma=True)
                P.op("sp", lambda en: en.dma_start(out=bswc[:], in_=din["f_bsw_t"]), writes=[bbsw], dma=True)
                P.op("sp", lambda en: en.dma_start(out=bsac[:], in_=din["f_bsa_t"]), writes=[bbsa], dma=True)
                P.op("sp", lambda en: en.dma_start(out=bsuc[:], in_=din["f_bsu_t"]), writes=[bbsu], dma=True)

                P.op("dve", lambda en: en.memset(ones512[:], 1.0), writes=[bones512], tag=tag + "o512")

                for j in range(2):
                    xin = xn_f if j == 0 else xn1_f
                    bxin = bxnf if j == 0 else bxn1
                    for hf in range(2):
                        reg = ps1[0:8, hf * 512:(hf + 1) * 512]
                        for c in range(NCH):
                            P.op("pe", lambda en, j=j, hf=hf, c=c, reg=reg, xin=xin: en.matmul(
                                reg, wg8[:, j * 128 + c * 8:j * 128 + (c + 1) * 8],
                                xin[:, c * TC + hf * 512:c * TC + hf * 512 + 512],
                                start=(c == 0), stop=(c == NCH - 1)),
                                reads=[bwg8, bxin[c]],
                                writes=[pb1[hf]] if c == 0 else (), update=[pb1[hf]] if c == NCH - 1 else (),
                                tag=f"{tag}g{j}_{hf}_{c}")
                        P.op("act", lambda en, j=j, hf=hf, reg=reg: en.activation(
                            out=g8r[:], in_=reg, func=AF.Identity,
                            bias=bgc[:, j:j + 1]),
                            reads=[pb1[hf], bbg], writes=[bg8r],
                            tag=f"{tag}gev{j}_{hf}")
                        for sl4 in range(4):
                            sl8 = hf * 4 + sl4
                            P.op("pe", lambda en, sl4=sl4, sl8=sl8: en.transpose(
                                ps1[:, 1024 + sl8 * 8:1024 + (sl8 + 1) * 8],
                                g8r[0:8, sl4 * 128:(sl4 + 1) * 128], identf[0:8, 0:8]),
                                reads=[bg8r, bconsts["identf"]],
                                writes=[pb1[2]] if sl8 == 0 else (), update=[pb1[2]] if sl8 == 7 else (),
                                tag=f"{tag}gt{j}_{sl8}")
                    for sl8 in range(8):
                        P.op("dve", lambda en, sl8=sl8: en.tensor_copy(gtm[:], ps1[:, 1024 + sl8 * 8:1024 + (sl8 + 1) * 8]),
                             reads=[pb1[2]], writes=[bgtm], tag=f"{tag}gc{j}_{sl8}")
                        P.op("dve", lambda en: en.tensor_reduce(gsc[:, 0:1], gtm[:], axis=mybir.AxisListType.X, op=ALU.min),
                             reads=[bgtm], writes=[bgsc], tag=f"{tag}m1{j}_{sl8}")
                        P.op("dve", lambda en: en.tensor_single_scalar(gsc[:, 8:16], gtm[:], gsc[:, 0:1], ALU.is_equal),
                             reads=[bgtm, bgsc], update=[bgsc], tag=f"{tag}eq{j}_{sl8}")
                        P.op("dve", lambda en: en.scalar_tensor_tensor(
                            out=gsc[:, 16:24], in0=gsc[:, 8:16], scalar=1e30, in1=gtm[:], op0=ALU.mult, op1=ALU.add),
                            reads=[bgsc, bgtm], update=[bgsc], tag=f"{tag}gb{j}_{sl8}")
                        P.op("dve", lambda en: en.tensor_reduce(gsc[:, 1:2], gsc[:, 16:24], axis=mybir.AxisListType.X, op=ALU.min),
                             reads=[bgsc], update=[bgsc], tag=f"{tag}m2{j}_{sl8}")
                        P.op("dve", lambda en: en.tensor_single_scalar(gsc[:, 8:16], gtm[:], gsc[:, 1:2], ALU.is_gt),
                             reads=[bgtm, bgsc], update=[bgsc], tag=f"{tag}kp{j}_{sl8}")
                        P.op("dve", lambda en: en.tensor_scalar(
                            out=gsc[:, 16:24], in0=gsc[:, 8:16], scalar1=1.0, scalar2=60.0,
                            op0=ALU.subtract, op1=ALU.mult),
                            reads=[bgsc], update=[bgsc], tag=f"{tag}pn{j}_{sl8}")
                        P.op("dve", lambda en: en.tensor_add(gsc[:, 24:32], gtm[:], gsc[:, 16:24]),
                             reads=[bgtm, bgsc], update=[bgsc], tag=f"{tag}gl{j}_{sl8}")
                        P.op("dve", lambda en: en.reduce_max(gsc[:, 2:3], gsc[:, 24:32], axis=mybir.AxisListType.X),
                             reads=[bgsc], update=[bgsc], tag=f"{tag}rm{j}_{sl8}")
                        P.op("dve", lambda en: en.tensor_scalar_mul(gsc[:, 3:4], gsc[:, 2:3], -1.0),
                             reads=[bgsc], update=[bgsc], tag=f"{tag}rn{j}_{sl8}")
                        P.op("act", lambda en: en.activation(out=gsc[:, 32:40], in_=gsc[:, 24:32], func=AF.Exp,
                                                             bias=gsc[:, 3:4], accum_out=gsc[:, 4:5]),
                             reads=[bgsc], update=[bgsc], tag=f"{tag}ex{j}_{sl8}")
                        P.op("dve", lambda en: en.reciprocal(gsc[:, 5:6], gsc[:, 4:5]),
                             reads=[bgsc], update=[bgsc], tag=f"{tag}rc{j}_{sl8}")
                        P.op("dve", lambda en, sl8=sl8: en.tensor_single_scalar(
                            gsc[:, 48 + 8 * sl8:56 + 8 * sl8], gsc[:, 32:40], gsc[:, 5:6], ALU.mult),
                            reads=[bgsc], update=[bgsc], tag=f"{tag}wm{j}_{sl8}")

                    def gate_tail(sl8, j=j):
                        sl4 = sl8 % 4
                        P.op("pe", lambda en, sl4=sl4, sl8=sl8: en.transpose(
                            ps1[0:8, 1536 + sl4 * 128:1536 + (sl4 + 1) * 128], gsc[:, 48 + 8 * sl8:56 + 8 * sl8], identf[:]),
                            reads=[bgsc, bconsts["identf"]],
                            writes=[pb1[3]], tag=f"{tag}gbt{j}_{sl8}")
                        P.op("dve", lambda en, sl4=sl4, sl8=sl8: en.tensor_copy(
                            w_fm[:, sl8 * 128:(sl8 + 1) * 128], ps1[0:8, 1536 + sl4 * 128:1536 + (sl4 + 1) * 128]),
                            reads=[pb1[3]],
                            writes=[bwfm] if sl8 == 0 else (), update=[bwfm] if sl8 == 7 else (),
                            tag=f"{tag}wfm{j}_{sl8}")
                        if sl8 == 7:
                            P.op("dve", lambda en: en.tensor_copy(w_bf[:], w_fm[:]),
                                 reads=[bwfm], writes=[bwbf], tag=f"{tag}wbf{j}")
                    for s_ in range(2):
                        for sci in range(8):
                            sc = s_ * 8 + sci
                            reg2 = ps2[:, (sci % 2) * 1024:(sci % 2) * 1024 + 512]
                            reg2b = ps2[:, (sci % 2) * 1024 + 512:(sci % 2) * 1024 + 1024]
                            rbs = [pb2[(sci % 2) * 2], pb2[(sci % 2) * 2 + 1]]
                            sp_ = us_ctr[0] % 2
                            us_ctr[0] += 1
                            wsrc, wq, wrd = wsrc3("f_Wsu_t2", sp_)
                            P.op(wq, lambda en, sc=sc, sp_=sp_, wsrc=wsrc: en.dma_start(
                                out=wsus[:, sp_ * 2048:(sp_ + 1) * 2048],
                                in_=wsrc[:, sc * 2048:(sc + 1) * 2048]),
                                reads=wrd, writes=[bwsu[sp_]], dma=True, tag=f"{tag}wsu{j}_{sc}")
                            feed_conv(2)
                            for tf in range(2):
                                reg = reg2 if tf == 0 else reg2b
                                for c in range(NCH):
                                    P.op("pe", lambda en, c=c, tf=tf, sp_=sp_, reg=reg, xin=xin: en.matmul(
                                        reg, wsus[:, sp_ * 2048 + c * 128:sp_ * 2048 + (c + 1) * 128],
                                        xin[:, c * TC + tf * 512:c * TC + tf * 512 + 512],
                                        start=(c == 0), stop=(c == NCH - 1)),
                                        reads=[bwsu[sp_], bxin[c]],
                                        writes=[rbs[tf]] if c == 0 else (), update=[rbs[tf]] if c == NCH - 1 else (),
                                        tag=f"{tag}su{j}_{sc}_{tf}_{c}")
                                P.op("act", lambda en, sci=sci, tf=tf, sc=sc, reg=reg: en.activation(
                                    out=u_n[:, sci * TC + tf * 512:sci * TC + tf * 512 + 512], in_=reg,
                                    func=AF.Identity, bias=bsuc[:, sc:sc + 1]),
                                    reads=[rbs[tf], bbsu],
                                    writes=[bun] if (sci == 0 and tf == 0) else (), update=[bun],
                                    tag=f"{tag}suev{j}_{sc}_{tf}")
                            if sc < 8:
                                gate_tail(sc)
                        for fch in range(8):
                            reg2 = ps1[:, (fch % 2) * 1024:(fch % 2) * 1024 + 512]
                            reg2b = ps1[:, (fch % 2) * 1024 + 512:(fch % 2) * 1024 + 1024]
                            rbs = [pb1[(fch % 2) * 2], pb1[(fch % 2) * 2 + 1]]
                            fp_ = us_ctr[0] % 2
                            us_ctr[0] += 1
                            wsrc, wq, wrd = wsrc3("f_Wsa_t2", fp_)
                            P.op(wq, lambda en, fch=fch, fp_=fp_, wsrc=wsrc: en.dma_start(
                                out=wsas[:, fp_ * 2048:fp_ * 2048 + 1024],
                                in_=wsrc[:, fch * 1024:(fch + 1) * 1024]),
                                reads=wrd, writes=[bwsa[fp_]], dma=True, tag=f"{tag}wsa{j}_{s_}_{fch}")
                            feed_conv(2)
                            for tf in range(2):
                                reg = reg2 if tf == 0 else reg2b
                                for ec in range(8):
                                    P.op("pe", lambda en, ec=ec, tf=tf, fp_=fp_, reg=reg: en.matmul(
                                        reg, wsas[:, fp_ * 2048 + ec * 128:fp_ * 2048 + (ec + 1) * 128],
                                        u_n[:, ec * TC + tf * 512:ec * TC + tf * 512 + 512],
                                        start=(ec == 0), stop=(ec == 7)),
                                        reads=[bwsa[fp_], bun],
                                        writes=[rbs[tf]] if ec == 0 else (), update=[rbs[tf]] if ec == 7 else (),
                                        tag=f"{tag}sa{j}_{s_}_{fch}_{tf}_{ec}")
                                P.op("act", lambda en, fch=fch, tf=tf, reg=reg: en.activation(
                                    out=silu_t[:], in_=reg, func=AF.Silu, bias=bsac[:, fch:fch + 1]),
                                    reads=[rbs[tf], bbsa], writes=[bsilu], tag=f"{tag}sas{j}_{s_}_{fch}_{tf}")
                                P.op("dve", lambda en, fch=fch, tf=tf: en.tensor_mul(
                                    hst[:, tf * 512:(tf + 1) * 512], silu_t[:],
                                    u_n[:, fch * TC + tf * 512:fch * TC + tf * 512 + 512]),
                                    reads=[bsilu, bun], writes=[bhst], tag=f"{tag}sam{j}_{s_}_{fch}_{tf}")
                            sc = s_ * 8 + fch
                            P.op("sp", lambda en, sc=sc: en.dma_start(
                                out=sh_spill[:, sc * TC:(sc + 1) * TC], in_=hst[:]),
                                reads=[bhst], writes=[bssp[sc]], dma=True, tag=f"{tag}shsp{j}_{sc}")
                    for n in range(6):
                        for tf in range(2):
                            P.op("pe", lambda en, n=n, tf=tf: en.matmul(
                                ps1[:, 1024 + tf * 512:1024 + (tf + 1) * 512],
                                bc8t[:, n * 128:(n + 1) * 128],
                                w_bf[0:8, tf * 512:(tf + 1) * 512], start=True, stop=True),
                                reads=[bwbf, bconsts["bc8"]],
                                writes=[pb1[2 + tf]], tag=f"{tag}wb{j}_{n}_{tf}")
                            P.op("dve", lambda en, tf=tf: en.tensor_copy(
                                wb_n[:, tf * 512:(tf + 1) * 512], ps1[:, 1024 + tf * 512:1024 + (tf + 1) * 512]),
                                reads=[pb1[2 + tf]],
                                writes=[bwbn] if tf == 0 else (), update=[bwbn] if tf else (),
                                tag=f"{tag}wbc{j}_{n}_{tf}")
                        for ech in range(8):
                            reg2 = ps2[:, (ech % 2) * 1024:(ech % 2) * 1024 + 512]
                            reg2b = ps2[:, (ech % 2) * 1024 + 512:(ech % 2) * 1024 + 1024]
                            rbs = [pb2[(ech % 2) * 2], pb2[(ech % 2) * 2 + 1]]
                            up_ = us_ctr[0] % 2
                            us_ctr[0] += 1
                            wsrc, wq, wrd = wsrc3("f_Wup_t", up_)
                            P.op(wq, lambda en, n=n, ech=ech, up_=up_, wsrc=wsrc: en.dma_start(
                                out=wups[:, up_ * 2048:(up_ + 1) * 2048],
                                in_=wsrc[:, (n * 8 + ech) * 2048:(n * 8 + ech + 1) * 2048]),
                                reads=wrd, writes=[bwup[up_]], dma=True, tag=f"{tag}wup{j}_{n}_{ech}")
                            feed_conv(2)
                            for tf in range(2):
                                reg = reg2 if tf == 0 else reg2b
                                for c in range(NCH):
                                    P.op("pe", lambda en, c=c, tf=tf, up_=up_, reg=reg, xin=xin: en.matmul(
                                        reg, wups[:, up_ * 2048 + c * 128:up_ * 2048 + (c + 1) * 128],
                                        xin[:, c * TC + tf * 512:c * TC + tf * 512 + 512],
                                        start=(c == 0), stop=(c == NCH - 1)),
                                        reads=[bwup[up_], bxin[c]],
                                        writes=[rbs[tf]] if c == 0 else (), update=[rbs[tf]] if c == NCH - 1 else (),
                                        tag=f"{tag}up{j}_{n}_{ech}_{tf}_{c}")
                                P.op("act", lambda en, ech=ech, tf=tf, n=n, reg=reg: en.activation(
                                    out=u_n[:, ech * TC + tf * 512:ech * TC + tf * 512 + 512], in_=reg,
                                    func=AF.Identity, bias=bupc[:, n * 8 + ech:n * 8 + ech + 1]),
                                    reads=[rbs[tf], bbup],
                                    writes=[bun] if (ech == 0 and tf == 0) else (), update=[bun],
                                    tag=f"{tag}upev{j}_{n}_{ech}_{tf}")
                        for fch in range(8):
                            reg2 = ps1[:, (fch % 2) * 1024:(fch % 2) * 1024 + 512]
                            reg2b = ps1[:, (fch % 2) * 1024 + 512:(fch % 2) * 1024 + 1024]
                            rbs = [pb1[(fch % 2) * 2], pb1[(fch % 2) * 2 + 1]]
                            fp_ = us_ctr[0] % 2
                            us_ctr[0] += 1
                            wsrc, wq, wrd = wsrc3("f_Wsw_t", fp_)
                            P.op(wq, lambda en, n=n, fch=fch, fp_=fp_, wsrc=wsrc: en.dma_start(
                                out=wsws[:, fp_ * 2048:fp_ * 2048 + 1024],
                                in_=wsrc[:, (n * 8 + fch) * 1024:(n * 8 + fch + 1) * 1024]),
                                reads=wrd, writes=[bwsw[fp_]], dma=True, tag=f"{tag}wswl{j}_{n}_{fch}")
                            feed_conv(2)
                            for tf in range(2):
                                reg = reg2 if tf == 0 else reg2b
                                for ec in range(8):
                                    P.op("pe", lambda en, ec=ec, tf=tf, fp_=fp_, reg=reg: en.matmul(
                                        reg, wsws[:, fp_ * 2048 + ec * 128:fp_ * 2048 + (ec + 1) * 128],
                                        u_n[:, ec * TC + tf * 512:ec * TC + tf * 512 + 512],
                                        start=(ec == 0), stop=(ec == 7)),
                                        reads=[bwsw[fp_], bun],
                                        writes=[rbs[tf]] if ec == 0 else (), update=[rbs[tf]] if ec == 7 else (),
                                        tag=f"{tag}sw{j}_{n}_{fch}_{tf}_{ec}")
                                P.op("act", lambda en, n=n, fch=fch, tf=tf, reg=reg: en.activation(
                                    out=silu_t[:], in_=reg, func=AF.Silu, bias=bswc[:, n * 8 + fch:n * 8 + fch + 1]),
                                    reads=[rbs[tf], bbsw], writes=[bsilu], tag=f"{tag}sws{j}_{n}_{fch}_{tf}")
                                P.op("dve", lambda en, fch=fch, tf=tf: en.tensor_mul(
                                    t2b[:], silu_t[:],
                                    u_n[:, fch * TC + tf * 512:fch * TC + tf * 512 + 512]),
                                    reads=[bsilu, bun], writes=[bt2], tag=f"{tag}swm{j}_{n}_{fch}_{tf}")
                                P.op("dve", lambda en, tf=tf: en.tensor_mul(
                                    hst[:, tf * 512:(tf + 1) * 512], t2b[:], wb_n[:, tf * 512:(tf + 1) * 512]),
                                    reads=[bt2, bwbn], writes=[bhst], tag=f"{tag}swh{j}_{n}_{fch}_{tf}")
                            P.op("sp", lambda en, n=n, fch=fch: en.dma_start(
                                out=h_spill[:, (n * 8 + fch) * TC:(n * 8 + fch + 1) * TC], in_=hst[:]),
                                reads=[bhst], writes=[bhsp[n][fch]], dma=True, tag=f"{tag}hsp{j}_{n}_{fch}")
                    for dcg in range(4):
                        P.op("pool", lambda en, dcg=dcg: en.dma_start(
                            out=bdn6[:], in_=din["f_bdn"][:, dcg * 512:(dcg + 1) * 512]),
                            writes=[bbdn6], dma=True, tag=f"{tag}bdnl{j}_{dcg}")
                        P.op("pool", lambda en, dcg=dcg: en.dma_start(
                            out=bsd1[:], in_=din["f_bsd"].rearrange("d -> () d")[:, dcg * 512:(dcg + 1) * 512]),
                            writes=[bbsd1], dma=True, tag=f"{tag}bsdl{j}_{dcg}")
                        regs = {}
                        for dci in range(4):
                            for tf in range(2):
                                ridx = dci * 2 + tf
                                regs[(dci, tf)] = ((ps1 if ridx < 4 else ps2)[:, (ridx % 4) * 512:(ridx % 4 + 1) * 512],
                                                  (pb1 if ridx < 4 else pb2)[ridx % 4])
                        first_mm = {k: True for k in regs}
                        for dci in range(4):
                            dc = dcg * 4 + dci
                            for tf in range(2):
                                reg, rb = regs[(dci, tf)]
                                P.op("pe", lambda en, dci=dci, tf=tf, reg=reg: en.matmul(
                                    reg, bsd1[0:1, dci * 128:(dci + 1) * 128], ones512[0:1, :],
                                    start=True, stop=False),
                                    reads=[bbsd1, bones512],
                                    writes=[rb], tag=f"{tag}bsd{j}_{dc}_{tf}")
                                P.op("pe", lambda en, dci=dci, tf=tf, reg=reg: en.matmul(
                                    reg, bdn6[:, dci * 128:(dci + 1) * 128],
                                    w_bf[0:6, tf * 512:(tf + 1) * 512], start=False, stop=False),
                                    reads=[bbdn6, bwbf], tag=f"{tag}bdn{j}_{dc}_{tf}")
                        for n in range(6):
                            for ecg in range(2):
                                hp = (n * 2 + ecg) % 2
                                P.op("sp", lambda en, n=n, ecg=ecg, hp=hp: en.dma_start(
                                    out=h_n[:, hp * 4 * TC:(hp + 1) * 4 * TC],
                                    in_=h_spill[:, (n * 8 + ecg * 4) * TC:(n * 8 + ecg * 4 + 4) * TC]),
                                    reads=[bhsp[n][f] for f in range(ecg * 4, ecg * 4 + 4)],
                                    writes=[bhn[hp]], dma=True, tag=f"{tag}hld{j}_{dcg}_{n}_{ecg}")
                                wp_ = ds_ctr[0] % 2
                                ds_ctr[0] += 1
                                wsrc, wq, wrd = wsrc3("f_Wdn_t", wp_)
                                P.op(wq, lambda en, n=n, dcg=dcg, ecg=ecg, wp_=wp_, wsrc=wsrc: en.dma_start(
                                    out=wdns[:, wp_ * 2048:(wp_ + 1) * 2048],
                                    in_=wsrc[:, ((n * 2 + ecg) * 4 + dcg) * 2048:((n * 2 + ecg) * 4 + dcg + 1) * 2048]),
                                    reads=wrd, writes=[bwdn[wp_]], dma=True, tag=f"{tag}wdnl{j}_{dcg}_{n}_{ecg}")
                                feed_conv(2)
                                for dci in range(4):
                                    for tf in range(2):
                                        reg, rb = regs[(dci, tf)]
                                        for eci in range(4):
                                            P.op("pe", lambda en, dci=dci, tf=tf, eci=eci, hp=hp, wp_=wp_, reg=reg: en.matmul(
                                                reg, wdns[:, wp_ * 2048 + eci * 512 + dci * 128:wp_ * 2048 + eci * 512 + dci * 128 + 128],
                                                h_n[:, hp * 4 * TC + eci * TC + tf * 512:hp * 4 * TC + eci * TC + tf * 512 + 512],
                                                start=False, stop=False),
                                                reads=[bwdn[wp_], bhn[hp]], tag=f"{tag}dn{j}_{dcg}_{n}_{dci}_{tf}_{eci}")

                        for scp in range(8):
                            wp = ds_ctr[0] % 2
                            ds_ctr[0] += 1
                            wsrc, wq, wrd = wsrc3("f_Wsd_t2", wp)
                            P.op(wq, lambda en, dcg=dcg, scp=scp, wp=wp, wsrc=wsrc: en.dma_start(
                                out=wsds[:, wp * 2048:wp * 2048 + 1024],
                                in_=wsrc[:, (dcg * 8 + scp) * 1024:(dcg * 8 + scp + 1) * 1024]),
                                reads=wrd, writes=[bwsd[wp]], dma=True, tag=f"{tag}wsdl{j}_{dcg}_{scp}")
                            feed_conv(2)
                            for sci in range(2):
                                sc = scp * 2 + sci
                                sp_ = sc % 2
                                P.op("sp", lambda en, sc=sc, sp_=sp_: en.dma_start(
                                    out=shl[:, sp_ * TC:(sp_ + 1) * TC],
                                    in_=sh_spill[:, sc * TC:(sc + 1) * TC]),
                                    reads=[bssp[sc]], writes=[bshl[sp_]], dma=True,
                                    tag=f"{tag}shl{j}_{dcg}_{sc}")
                                for dci in range(4):
                                    for tf in range(2):
                                        reg, rb = regs[(dci, tf)]
                                        P.op("pe", lambda en, sci=sci, sp_=sp_, wp=wp, dci=dci, tf=tf, sc=sc, reg=reg: en.matmul(
                                            reg, wsds[:, wp * 2048 + sci * 512 + dci * 128:wp * 2048 + sci * 512 + dci * 128 + 128],
                                            shl[:, sp_ * TC + tf * 512:sp_ * TC + tf * 512 + 512],
                                            start=False, stop=(sc == 15)),
                                            reads=[bwsd[wp], bshl[sp_]],
                                            update=[regs[(dci, tf)][1]] if sc == 15 else (),
                                            tag=f"{tag}sd{j}_{dcg}_{sc}_{dci}_{tf}")
                        for dci in range(4):
                            dc = dcg * 4 + dci
                            for tf in range(2):
                                reg, rb = regs[(dci, tf)]
                                xs = slice(dc * TC + tf * 512, dc * TC + tf * 512 + 512)
                                if j == 0:
                                    P.op("dve", lambda en, reg=reg, xs=xs: en.tensor_add(
                                        xn1_f[:, xs], reg, xn_f[:, xs]),
                                        reads=[rb, bxnf[dc]],
                                        writes=[bxn1[dc]] if tf == 0 else (), update=[bxn1[dc]] if tf else (),
                                        tag=f"{tag}ev0_{dc}_{tf}")
                                else:
                                    P.op("dve", lambda en, reg=reg, xs=xs: en.tensor_add(
                                        x_res[:, xs], x_res[:, xs], reg),
                                        reads=[rb], writes=[bxc[dc]], tag=f"{tag}ev1a_{dc}_{tf}")
                                    P.op("dve", lambda en, xs=xs: en.tensor_add(
                                        x_res[:, xs], x_res[:, xs], xn_f[:, xs]),
                                        reads=[bxnf[dc]], writes=[bxc[dc]], tag=f"{tag}ev1b_{dc}_{tf}")
                    if j == 0 and "xt0" in dbg_out:
                        P.op("pool", lambda en: en.dma_start(
                            out=dbg_out["xt0"].rearrange("(c p) t -> p c t", p=128),
                            in_=xn1_f.ap().rearrange("p (c t) -> p c t", c=NCH)),
                            reads=bxn1, dma=True, tag="dbg_xt0")
                    if "w" + str(j) in dbg_out:
                        P.op("pool", lambda en, j=j: en.dma_start(
                            out=dbg_out["w" + str(j)], in_=w_fm[:]),
                            reads=[bwfm], dma=True, tag=f"dbg_w{j}")
                    if j == 0 and "sh0" in dbg_out:
                        P.op("pool", lambda en: en.dma_start(out=dbg_out["sh0"], in_=sh_spill),
                             reads=[bssp[i] for i in range(16)], dma=True, tag="dbg_sh0")
                    if j == 0 and "h0" in dbg_out:
                        P.op("pool", lambda en: en.dma_start(out=dbg_out["h0"], in_=h_spill),
                             reads=[b for row in bhsp for b in row], dma=True, tag="dbg_h0")
                feed_conv(len(conv_jobs))
                P.fence([bwg8, bbg, bbup, bbsw, bbsa, bbsu, bbdn6, bbsd1, bones512,
                         bg8r, bgtm, bgsc, bwfm, bwbf, bwbn, bun, bsilu, bt2, bhst]
                        + bwup + bwsw + bwsu + bwsa + bwdn + bwsd + bhn + bshl + bcstg
                        + bxnf + bxn1 + [b for row in bhsp for b in row] + bssp, tag=tag + "f")
                if conv_state["stg"] is not None and conv_state["i"] >= len(conv_jobs):
                    conv_state["stg"] = None

        QSB = 1.0 / (128.0 ** 0.5)

        def attn_b():
            stage_toks = layernorm("ln2", None, None)
            tcc = P.op("pool", lambda en: en.collective_compute(
                "AllGather", ALU.bypass, ins=[ag_xn_in.opt()], outs=[ag_xn_out.opt()],
                replica_groups=PAIRS),
                writes=[bag_xn_out], cc=True, tag="ag_xn_b")
            th = P.thunks["pool"][-1]
            for t in stage_toks:
                P._need(th, t)

            pt_toks = {0: [], 1: []}
            rs_emitted = [False, False]

            def emit_rs(colh, tg):
                rs_emitted[colh] = True
                P.op("pool", lambda en, colh=colh: en.collective_compute(
                    "ReduceScatter", ALU.add, ins=[ag_pt_in2[colh].opt()], outs=[ag_pt_out2[colh].opt()],
                    replica_groups=PAIRS),
                    writes=[bag_pt_out2[colh]], cc=True, tag=tg)
                thpt = P.thunks["pool"][-1]
                for t in pt_toks[colh]:
                    P._need(thpt, t)
            with ExitStack() as bctx:
                be = bctx.enter_context
                xnw = be(nc.sbuf_tensor("b_xnw", [128, NCH * 512], BF16))
                bxnw = [P.buf(f"bxnw{c}") for c in range(NCH)]
                q_fm = be(nc.sbuf_tensor("b_qfm", [128, 8 * T], BF16))
                k_fm = be(nc.sbuf_tensor("b_kfm", [128, 2 * T], BF16))
                v_tm = be(nc.sbuf_tensor("b_vtm", [128, NCH * 256], BF16))
                wslab = be(nc.sbuf_tensor("b_wslab", [128, 2 * NCH * 128], BF16))
                wvb = be(nc.sbuf_tensor("b_wvb", [128, NCH * 256], BF16))
                strip = be(nc.sbuf_tensor("b_strip", [128, 2 * STRIP_W], BF16))
                Ptmp = be(nc.sbuf_tensor("b_Ptmp", [128, 2 * 512], F32))
                Pbf = be(nc.sbuf_tensor("b_Pbf", [128, 2 * 512], BF16))
                sq = be(nc.sbuf_tensor("b_sq", [128, T], BF16))
                nrm = be(nc.sbuf_tensor("b_nrm", [1, 64], F32))
                mneg = be(nc.sbuf_tensor("b_mneg", [128, 8], F32))
                rrow = be(nc.sbuf_tensor("b_rrow", [1, 1024], F32))
                bcst = be(nc.sbuf_tensor("b_bcst", [128, 512], F32))
                o_fm = be(nc.sbuf_tensor("b_ofm", [128, 8 * 512], BF16))
                wob = be(nc.sbuf_tensor("b_wob", [128, 4 * 2048], BF16))
                stg = be(nc.sbuf_tensor("b_stg", [128, 512], F32))

                bq, bk, bv = P.buf("bq"), P.buf("bk"), P.buf("bv")
                bws = [P.buf("bws0"), P.buf("bws1")]
                bwvb = P.buf("bwvb")
                bstrip = [P.buf("bstrip0"), P.buf("bstrip1")]
                bPt = [P.buf("bPt0"), P.buf("bPt1")]
                bPb = [P.buf("bPb0"), P.buf("bPb1")]
                bsq, bnrm, bmneg = P.buf("bsq"), P.buf("bnrm"), P.buf("bmneg")
                brr, bbc = P.buf("brr"), P.buf("bbc2")
                bof = P.buf("bof")
                bwob = [P.buf(f"bwob{i}") for i in range(4)]
                bstg2 = P.buf("bstg2")

                for w in range(4):
                    for c in range(NCH):
                        def fn(en, c=c, w=w):
                            return en.dma_start(
                                out=xnw[:, c * 512:(c + 1) * 512],
                                in_=ag_xn_out[(w // 2) * D + c * 128:(w // 2) * D + (c + 1) * 128,
                                              (w % 2) * 512:(w % 2) * 512 + 512])
                        P.op("pool", fn, reads=[bag_xn_out],
                             writes=[bxnw[c]], dma=True, tag=f"bx{w}_{c}")
                    for hl in range(8):
                        par = hl % 2
                        P.op("sp" if par == 0 else "pool", lambda en, hl=hl, par=par: en.dma_start(
                            out=wslab[:, par * 2048:(par + 1) * 2048],
                            in_=dcache["wq_b"][:, hl * 2048:(hl + 1) * 2048]),
                            reads=[bcache["wq_b"]], writes=[bws[par]], dma=True, tag=f"bwq{w}_{hl}")
                        reg = ps2[:, (hl % 4) * 512:(hl % 4 + 1) * 512]
                        rb = pb2[hl % 4]
                        for c in range(NCH):
                            P.op("pe", lambda en, hl=hl, c=c, reg=reg, par=par: en.matmul(
                                reg, wslab[:, par * NCH * 128 + c * 128:par * NCH * 128 + (c + 1) * 128],
                                xnw[:, c * 512:(c + 1) * 512],
                                start=(c == 0), stop=(c == NCH - 1)),
                                reads=[bws[par], bxnw[c]],
                                writes=[rb] if c == 0 else (), update=[rb] if c == NCH - 1 else (),
                                tag=f"bq{w}_{hl}_{c}")
                        P.op("dve", lambda en, hl=hl, w=w, reg=reg: en.tensor_copy(
                            q_fm[:, hl * T + w * 512:hl * T + (w + 1) * 512], reg),
                            reads=[rb],
                            writes=[bq] if (w == 0 and hl == 0) else (), update=[bq],
                            tag=f"bqev{w}_{hl}")
                    for gl in range(2):
                        par = gl % 2
                        P.op("sp" if par == 0 else "pool", lambda en, gl=gl, par=par: en.dma_start(
                            out=wslab[:, par * 2048:(par + 1) * 2048],
                            in_=dcache["wk_b"][:, gl * 2048:(gl + 1) * 2048]),
                            reads=[bcache["wk_b"]], writes=[bws[par]], dma=True, tag=f"bwk{w}_{gl}")
                        reg = ps2[:, gl * 512:(gl + 1) * 512]
                        rb = pb2[gl]
                        for c in range(NCH):
                            P.op("pe", lambda en, gl=gl, c=c, reg=reg, par=par: en.matmul(
                                reg, wslab[:, par * NCH * 128 + c * 128:par * NCH * 128 + (c + 1) * 128],
                                xnw[:, c * 512:(c + 1) * 512],
                                start=(c == 0), stop=(c == NCH - 1)),
                                reads=[bws[par], bxnw[c]],
                                writes=[rb] if c == 0 else (), update=[rb] if c == NCH - 1 else (),
                                tag=f"bk{w}_{gl}_{c}")
                        P.op("dve", lambda en, gl=gl, w=w, reg=reg: en.tensor_copy(
                            k_fm[:, gl * T + w * 512:gl * T + (w + 1) * 512], reg),
                            reads=[rb],
                            writes=[bk] if (w == 0 and gl == 0) else (), update=[bk],
                            tag=f"bkev{w}_{gl}")
                    if w == 0:
                        P.op("pool", lambda en: en.dma_start(
                            out=wvb[:], in_=din["wv_b"]),
                            writes=[bwvb], dma=True, tag=f"bwv{w}")
                    for tc4 in range(4):
                        tcg = w * 4 + tc4
                        reg = ps1[:, (tc4 % 4) * 512:(tc4 % 4) * 512 + 256]
                        rb = pb1[tc4 % 4]
                        for c in range(NCH):
                            P.op("pe", lambda en, c=c, tc4=tc4, reg=reg: en.matmul(
                                reg, xnw[:, c * 512 + tc4 * 128:c * 512 + tc4 * 128 + 128],
                                wvb[:, c * 256:(c + 1) * 256], start=(c == 0), stop=(c == NCH - 1)),
                                reads=[bwvb, bxnw[c]],
                                writes=[rb] if c == 0 else (), update=[rb] if c == NCH - 1 else (),
                                tag=f"bv{w}_{tc4}_{c}")
                        P.op("dve", lambda en, tcg=tcg, reg=reg: en.tensor_copy(
                            v_tm[:, tcg * 256:(tcg + 1) * 256], reg),
                            reads=[rb],
                            writes=[bv] if tcg == 0 else (), update=[bv],
                            tag=f"bvev{w}_{tc4}")
                for hl in range(10):
                    src = q_fm[:, hl * T:(hl + 1) * T] if hl < 8 else k_fm[:, (hl - 8) * T:(hl - 7) * T]
                    rdep = [bq] if hl < 8 else [bk]
                    P.op("act", lambda en, src=src: en.activation(out=sq[:], in_=src, func=AF.Square),
                         reads=rdep, writes=[bsq], tag=f"bsqr{hl}")
                    for qq in range(4):
                        P.op("pe", lambda en, qq=qq: en.matmul(
                            ps1[0:1, 1024:1536],
                            ones_bf[:], sq[:, qq * 512:(qq + 1) * 512],
                            start=True, stop=True),
                            reads=[bsq, bconsts["ones_bf"]],
                            writes=[pb1[2]], tag=f"bnm{hl}_{qq}")
                        P.op("dve", lambda en, hl=hl, qq=qq: en.reduce_max(
                            nrm[0:1, 32 + qq:33 + qq], ps1[0:1, 1024:1536], axis=mybir.AxisListType.X),
                            reads=[pb1[2]],
                            writes=[bnrm] if (hl == 0 and qq == 0) else (), update=[bnrm], tag=f"bnr{hl}_{qq}")
                    P.op("dve", lambda en, hl=hl: en.reduce_max(
                        nrm[0:1, hl:hl + 1], nrm[0:1, 32:36], axis=mybir.AxisListType.X),
                        reads=[bnrm], update=[bnrm], tag=f"bnrf{hl}")
                P.op("dve", lambda en: en.tensor_single_scalar(
                    nrm[0:1, 16:20], nrm[0:1, 0:4], nrm[0:1, 8:9], ALU.mult),
                    reads=[bnrm], update=[bnrm], tag="bmm0")
                P.op("dve", lambda en: en.tensor_single_scalar(
                    nrm[0:1, 20:24], nrm[0:1, 4:8], nrm[0:1, 9:10], ALU.mult),
                    reads=[bnrm], update=[bnrm], tag="bmm1")
                P.op("act", lambda en: en.activation(out=nrm[0:1, 24:32], in_=nrm[0:1, 16:24], func=AF.Ln),
                     reads=[bnrm], update=[bnrm], tag="bln")
                P.op("act", lambda en: en.activation(out=nrm[0:1, 16:24], in_=nrm[0:1, 24:32], func=AF.Exp, scale=0.5),
                     reads=[bnrm], update=[bnrm], tag="bexp")
                P.op("dve", lambda en: en.tensor_scalar_mul(nrm[0:1, 24:32], nrm[0:1, 16:24], -QSB),
                     reads=[bnrm], update=[bnrm], tag="bneg")
                P.op("pe", lambda en: en.matmul(
                    ps1[:, 1024:1032], ones_f[:], nrm[0:1, 24:32], start=True, stop=True),
                    reads=[bnrm, bconsts["ones_f"]], writes=[pb1[2]], tag="bmnbc")
                P.op("dve", lambda en: en.tensor_copy(mneg[:], ps1[:, 1024:1032]),
                     reads=[pb1[2]], writes=[bmneg], tag="bmnc")
                def emit_S(qb, hl, kb):
                    gl = hl // 4
                    regC = ps1[:, (kb % 2) * 512:(kb % 2) * 512 + 512]
                    rbC = pb1[kb % 2]
                    P.op("pe", lambda en, hl=hl, gl=gl, qb=qb, kb=kb, regC=regC: en.matmul(
                        regC, k_fm[:, gl * T + kb * 128:gl * T + kb * 128 + 128],
                        q_fm[:, hl * T + qb * 512:hl * T + (qb + 1) * 512],
                        start=True, stop=True),
                        reads=[bk, bq], writes=[rbC], tag=f"bS{qb}_{hl}_{kb}")

                def emit_strip(qb, hl):
                    sp_ = hl % 2
                    P.op("sp", lambda en, hl=hl, sp_=sp_: en.dma_start(
                        out=strip[:, sp_ * STRIP_W:(sp_ + 1) * STRIP_W],
                        in_=din["strips"][hl]),
                        writes=[bstrip[sp_]], dma=True, tag=f"bstl{qb}_{hl}")

                for qb in range(4):
                    nkb = (qb + 1) * 4
                    psO = ps2[:, 0:512]
                    psS = ps2[0:1, 512:1024]
                    emit_strip(qb, 0)
                    emit_S(qb, 0, 0)
                    emit_S(qb, 0, 1)
                    for hl in range(8):
                        gl = hl // 4
                        sp_ = hl % 2
                        if hl + 1 < 8:
                            emit_strip(qb, hl + 1)
                        for kb in range(nkb):
                            kp = kb % 2
                            rbC = pb1[kp]
                            regC = ps1[:, kp * 512:kp * 512 + 512]
                            P.op("act", lambda en, hl=hl, kp=kp, regC=regC: en.activation(
                                out=Ptmp[:, kp * 512:(kp + 1) * 512], in_=regC, func=AF.Exp, scale=QSB, bias=mneg[:, hl:hl + 1]),
                                reads=[rbC, bmneg], writes=[bPt[kp]], tag=f"bE{qb}_{hl}_{kb}")
                            off = qb * 512 - kb * 128 + 384
                            P.op("dve", lambda en, kp=kp, sp_=sp_, off=off: en.tensor_mul(
                                Pbf[:, kp * 512:kp * 512 + 512], Ptmp[:, kp * 512:(kp + 1) * 512],
                                strip[:, sp_ * STRIP_W + off:sp_ * STRIP_W + off + 512]),
                                reads=[bPt[kp], bstrip[sp_]], writes=[bPb[kp]], tag=f"bW{qb}_{hl}_{kb}")
                            if kb + 2 < nkb:
                                emit_S(qb, hl, kb + 2)
                            elif hl + 1 < 8:
                                emit_S(qb, hl + 1, kb + 2 - nkb)
                            P.op("pe", lambda en, kb=kb, kp=kp, gl=gl, first=(kb == 0), last=(kb == nkb - 1): en.matmul(
                                psO, v_tm[:, kb * 256 + gl * 128:kb * 256 + gl * 128 + 128],
                                Pbf[:, kp * 512:kp * 512 + 512],
                                start=first, stop=last),
                                reads=[bPb[kp], bv],
                                writes=[pb2[0]] if kb == 0 else (), update=[pb2[0]] if kb == nkb - 1 else (),
                                tag=f"bPV{qb}_{hl}_{kb}")
                            P.op("pe", lambda en, kb=kb, kp=kp, first=(kb == 0), last=(kb == nkb - 1): en.matmul(
                                psS, ones_bf[:], Pbf[:, kp * 512:kp * 512 + 512],
                                start=first, stop=last),
                                reads=[bPb[kp], bconsts["ones_bf"]],
                                writes=[pb2[1]] if kb == 0 else (), update=[pb2[1]] if kb == nkb - 1 else (),
                                tag=f"bSS{qb}_{hl}_{kb}")
                        P.op("act", lambda en: en.activation(out=rrow[0:1, 0:512], in_=psS, func=AF.Ln),
                             reads=[pb2[1]], writes=[brr], tag=f"bln{qb}_{hl}")
                        P.op("act", lambda en: en.activation(out=rrow[0:1, 512:1024], in_=rrow[0:1, 0:512], func=AF.Exp, scale=-1.0),
                             reads=[brr], update=[brr], tag=f"bre{qb}_{hl}")
                        P.op("pe", lambda en: en.matmul(
                            ps1[:, 1024:1536], ones_f[:], rrow[0:1, 512:1024], start=True, stop=True),
                            reads=[brr, bconsts["ones_f"]], writes=[pb1[2]], tag=f"bbc{qb}_{hl}")
                        P.op("dve", lambda en: en.tensor_copy(bcst[:], ps1[:, 1024:1536]),
                             reads=[pb1[2]], writes=[bbc], tag=f"bbcc{qb}_{hl}")
                        P.op("dve", lambda en, hl=hl: en.tensor_mul(
                            o_fm[:, hl * 512:(hl + 1) * 512], psO, bcst[:]),
                            reads=[pb2[0], bbc],
                            writes=[bof] if hl == 0 else (), update=[bof] if hl == 7 else (),
                            tag=f"bon{qb}_{hl}")
                    for half in range(2):
                        for hh_ in range(4):
                            hglob = half * 4 + hh_
                            P.op("sp" if hh_ % 2 == 0 else "pool", lambda en, hglob=hglob, hh_=hh_: en.dma_start(
                                out=wob[:, hh_ * 2048:(hh_ + 1) * 2048],
                                in_=dcache["wo_b"][hglob * 128:(hglob + 1) * 128, :]),
                                reads=[bcache["wo_b"]], writes=[bwob[hh_]], dma=True, tag=f"bwo{qb}_{half}_{hh_}")
                        for dc in range(16):
                            reg = ps2[:, 1024 + (dc % 2) * 512:1024 + (dc % 2) * 512 + 512]
                            rb = pb2[2 + dc % 2]
                            for hh_ in range(4):
                                hglob = half * 4 + hh_
                                P.op("pe", lambda en, dc=dc, hh_=hh_, hglob=hglob, reg=reg: en.matmul(
                                    reg, wob[:, hh_ * 2048 + dc * 128:hh_ * 2048 + (dc + 1) * 128],
                                    o_fm[:, hglob * 512:(hglob + 1) * 512], start=(hh_ == 0), stop=(hh_ == 3)),
                                    reads=[bwob[hh_], bof],
                                    writes=[rb] if hh_ == 0 else (), update=[rb] if hh_ == 3 else (),
                                    tag=f"bwomm{qb}_{half}_{dc}_{hh_}")
                            P.op("act", lambda en, reg=reg: en.activation(out=stg[:], in_=reg, func=AF.Identity),
                                 reads=[rb], writes=[bstg2], tag=f"bwoev{qb}_{half}_{dc}")
                            if half == 0:
                                pt_toks[qb % 2].append(P.op("sp", lambda en, qb=qb, dc=dc: en.dma_start(
                                    out=ag_pt_in2[qb % 2][(qb // 2) * D + dc * 128:(qb // 2) * D + (dc + 1) * 128, :],
                                    in_=stg[:]),
                                    reads=[bstg2], writes=[bag_ptr[qb][dc]], dma=True, tag=f"bwod{qb}_{half}_{dc}"))
                            else:
                                pt_toks[qb % 2].append(P.op("pool", lambda en, qb=qb, dc=dc: en.dma_start(
                                    out=ag_pt_in2[qb % 2][(qb // 2) * D + dc * 128:(qb // 2) * D + (dc + 1) * 128, :],
                                    in_=stg[:], accum_op=ALU.add),
                                    reads=[bstg2], writes=[bag_ptr[qb][dc]], dma=True, tag=f"bwod{qb}_{half}_{dc}"))
                    if qb == 2:
                        emit_rs(0, "ag_pt_b0")
                P.fence([bq, bk, bv, bws[0], bws[1], bwvb, bstrip[0], bstrip[1],
                         bsq, bnrm, bmneg, brr, bbc, bof, bstg2] + bPt + bPb + bwob + bxnw, tag="bf")

            emit_rs(1, "ag_pt_b1")
            with ExitStack() as rctx:
                re_ = rctx.enter_context
                rb0 = re_(nc.sbuf_tensor("b_rb0", [128, 2 * TC], F32))
                brb0 = [[P.buf(f"brb0{i}_{ch}") for ch in range(2)] for i in range(2)]
                for c in range(NCH):
                    par = c % 2
                    for colh in range(2):
                        r0v = rb0[:, par * TC + colh * 512:par * TC + colh * 512 + 512]
                        P.op("sp" if colh == 0 else "pool", lambda en, c=c, colh=colh, r0v=r0v: en.dma_start(
                            out=r0v, in_=ag_pt_out2[colh][c * 128:(c + 1) * 128, :]),
                            reads=[bag_pt_out2[colh]], writes=[brb0[par][colh]], dma=True, tag=f"brb0_{c}_{colh}")
                        sl = slice(c * TC + colh * 512, c * TC + colh * 512 + 512)
                        P.op("dve", lambda en, sl=sl, r0v=r0v: en.tensor_add(x_res[:, sl], x_res[:, sl], r0v),
                             reads=[brb0[par][colh]], writes=[bxc[c]], tag=f"bres0_{c}_{colh}")
                P.fence([b for row in brb0 for b in row], tag="brbf")

        if nsub >= 1:
            attn_a()
        if nsub >= 2:
            ffn_sublayer("f1")
        if nsub >= 3:
            attn_b()
        if nsub >= 4:
            ffn_sublayer("f2")

        out_toks = []

        def ox(en):
            return en.dma_start(out=out_x.rearrange("(c p) t -> p c t", p=128),
                                in_=x_res.ap().rearrange("p (c t) -> p c t", c=NCH))
        out_toks.append(P.op("sp", ox, reads=list(bxc), dma=True, tag="out_x"))

        P.final_barrier("sp", out_toks)
        blk = e(nc.Block())
        P.finalize(blk)

    return nc, [k for k in dbg_out]


def kernel(**inputs):
    in_maps = make_in_maps(inputs)
    nc, dbg = build_program()
    res = run_bass_kernel_spmd(nc, in_maps, core_ids=list(range(NC_ACT)))
    out = np.empty((B, T, D), np.float32)
    for c in range(NC_ACT):
        b, hh = c // 2, c % 2
        out[b, hh * TC:(hh + 1) * TC, :] = res.results[c]["out_x"].T
    return out


# version 07 (final)
# speedup vs baseline: 1.0071x; 1.0071x over previous
import numpy as np
import ml_dtypes
from contextlib import ExitStack

import concourse.bass as bass
import concourse.mybir as mybir
from concourse.bass_utils import run_bass_kernel_spmd

F32 = mybir.dt.float32
BF16 = mybir.dt.bfloat16
I32 = mybir.dt.int32
AF = mybir.ActivationFunctionType
ALU = mybir.AluOpType
BF = ml_dtypes.bfloat16

NCORE = 8
NC_ACT = 8
D = 2048
TC = 1024
NCH = 16
B = 4
T = 2048

HA = 8
DK = 64
DV = 128
MSLOT = 64
CLAMP = 32.0
A_EPS = 1e-5
QSCALE = float(DK) ** -0.5
LN_EPS = 1e-5

CFG = {"n_sub": 4, "debug": [], "debug_shapes": {}}

ENGS = ("pe", "act", "dve", "pool", "sp")
BLKNAME = {"pe": "tensor", "act": "scalar", "dve": "vector", "pool": "gpsimd", "sp": "sync"}
QSEM = {"sp": "q_sp", "act": "q_act", "pool": "q_pool"}
CSEM = {e: "c_" + e for e in ENGS}
NDMASEM = 8


class Buf:
    __slots__ = ("name", "writer", "readers")

    def __init__(self, name):
        self.name = name
        self.writer = None
        self.readers = []


class Prog:
    def __init__(self, nc, semh):
        self.nc = nc
        self.semh = semh
        self.thunks = {e: [] for e in ENGS}
        self.cinc = {e: [] for e in ENGS}
        self.drain_cover = {e: 0 for e in ENGS}
        self.dma_idx = {q: 0 for q in QSEM}
        self.cc_idx = 0

    def buf(self, name):
        return Buf(name)

    def op(self, eng, fn, reads=(), writes=(), update=(), dma=False, cc=False, tag=None):
        if update and eng != "pe":
            writes = list(writes) + [b for b in update if b not in writes]
            update = ()
        seq = len(self.thunks[eng])
        kind = "d" if dma else ("x" if cc else "c")
        tok = (eng, seq, kind)
        th = {"fn": fn, "waits": [], "drain": False, "incs": [], "eng": eng,
              "seq": seq, "tag": tag, "xwaits": [], "dsem": None}
        if dma:
            assert eng in QSEM, f"dma on {eng}"
            i = self.dma_idx[eng]
            self.dma_idx[eng] = i + 1
            slot, gen = i % NDMASEM, i // NDMASEM + 1
            sem = f"{QSEM[eng]}_{slot}"
            th["dsem"] = (sem, 16 * gen)
            th["incs"].append((sem, 16))
            if gen > 1:
                th["xwaits"].append((sem, 16 * (gen - 1)))
        if cc:
            assert eng == "pool"
            self.cc_idx += 1
            th["dsem"] = ("cc", self.cc_idx)
            th["incs"].append(("cc", 1))
            if self.cc_idx > 1:
                th["xwaits"].append(("cc", self.cc_idx - 1))
        deps = []
        for b in reads:
            if b.writer is not None:
                deps.append(b.writer)
        for b in writes:
            if b.writer is not None:
                deps.append(b.writer)
            deps.extend(b.readers)
        seen = set()
        for d in deps:
            if d in seen:
                continue
            seen.add(d)
            self._need(th, d)
        self.thunks[eng].append(th)
        for b in reads:
            b.readers.append(tok)
        for b in writes:
            b.writer = tok
            b.readers = []
        for b in update:
            b.writer = tok
        return tok

    def _need(self, th, tok):
        eng, seq, kind = tok
        me = th["eng"]
        if eng == me and kind == "c":
            if me == "pe":
                return
            if seq >= self.drain_cover[me]:
                th["drain"] = True
                self.drain_cover[me] = th["seq"]
            return
        if kind == "c":
            self._force_inc(tok)
        th["waits"].append(tok)

    def _force_inc(self, tok):
        eng, seq, kind = tok
        import bisect
        lst = self.cinc[eng]
        i = bisect.bisect_left(lst, seq)
        if i < len(lst):
            return
        self.thunks[eng][seq]["incs"].append((CSEM[eng], 1))
        bisect.insort(lst, seq)

    def final_barrier(self, eng, toks):
        finals = {}
        for q, n in self.dma_idx.items():
            for i in range(n):
                sem = f"{QSEM[q]}_{i % NDMASEM}"
                finals[sem] = max(finals.get(sem, 0), 16 * (i // NDMASEM + 1))
        if self.cc_idx:
            finals["cc"] = self.cc_idx
        tok = self.barrier(eng, toks, tag="final")
        self.thunks[eng][-1]["xwaits"].extend(finals.items())
        return tok

    def fence(self, bufs, tag="fence"):
        toks = []
        for b in bufs:
            if b.writer is not None:
                toks.append(b.writer)
            toks.extend(b.readers)
        toks = list(dict.fromkeys(toks))
        for eng in ENGS:
            self.barrier(eng, toks, tag=f"{tag}_{eng}")

    def barrier(self, eng, toks, tag=None):
        def fn(e):
            return e.nop()
        th_tok = self.op(eng, fn, tag=tag or "barrier")
        th = self.thunks[eng][-1]
        for t in toks:
            self._need(th, t)
        return th_tok

    def finalize(self, blk):
        incpts = {}
        for eng in ENGS:
            cums = {}
            for th in self.thunks[eng]:
                for (sem, amt) in th["incs"]:
                    cums.setdefault(sem, [0, []])
                    cums[sem][0] += amt
                    cums[sem][1].append((th["seq"], cums[sem][0]))
            for sem, (_, lst) in cums.items():
                if sem.startswith("c_"):
                    assert sem not in incpts
                    incpts[sem] = lst
        import bisect

        def resolve(tok):
            eng, seq, kind = tok
            if kind in ("d", "x"):
                return self.thunks[eng][seq]["dsem"]
            sem = CSEM[eng]
            lst = incpts.get(sem)
            assert lst, f"no incs on {sem} for tok {tok}"
            seqs = [s for s, _ in lst]
            i = bisect.bisect_left(seqs, seq)
            assert i < len(lst), f"no inc covering {tok}"
            return sem, lst[i][1]

        for eng in ENGS:
            hw = {}
            for th in self.thunks[eng]:
                rw = {}
                for tok in th["waits"]:
                    sem, val = resolve(tok)
                    if val > rw.get(sem, -1):
                        rw[sem] = val
                for sem, val in th["xwaits"]:
                    if val > rw.get(sem, -1):
                        rw[sem] = val
                th["rwaits"] = []
                for sem, val in rw.items():
                    if val > hw.get(sem, -1):
                        th["rwaits"].append((sem, val))
                        hw[sem] = val

        semh = self.semh
        for eng in ENGS:
            ths = self.thunks[eng]
            if not ths:
                continue

            def runner(e, ths=ths):
                for th in ths:
                    if th["drain"]:
                        e.drain()
                    for (sem, val) in th["rwaits"]:
                        e.wait_ge(semh[sem], val)
                    inst = th["fn"](e)
                    for (sem, amt) in th["incs"]:
                        inst.then_inc(semh[sem], amt)
            getattr(blk, BLKNAME[eng])(runner)



def _alibi_slopes(n_heads):
    import math
    n = 2 ** int(math.floor(math.log2(n_heads)))
    m0 = 2.0 ** (-8.0 / n)
    slopes = m0 ** np.arange(1, n + 1)
    if n < n_heads:
        m1 = 2.0 ** (-4.0 / n)
        slopes = np.concatenate([slopes, m1 ** np.arange(1, 1 + 2 * (n_heads - n), 2)])
    return slopes.astype(np.float64)

STRIP_W = 2432


def _alibi_strips(hh):
    slopes = _alibi_slopes(16)
    heads = range(hh * 8, hh * 8 + 8)
    u = np.arange(STRIP_W)[None, :]
    p = np.arange(128)[:, None]
    d = (u - 384 - p).astype(np.float64)
    mask = d >= 0
    sq = np.sqrt(np.maximum(d, 0))
    out = np.empty((8, 128, STRIP_W), np.float32)
    for i, h in enumerate(heads):
        out[i] = np.where(mask, np.exp(-slopes[h] * sq), 0.0).astype(np.float32)
    return out.astype(BF)


def make_in_maps(inputs):
    x = np.asarray(inputs["L0_attn_a_hidden_states"])
    in_maps = []
    ffn_names = ["L0_ffn_Wg", "L0_ffn_bg", "L0_ffn_Wup", "L0_ffn_bup", "L0_ffn_Wsw",
                 "L0_ffn_bsw", "L0_ffn_Wdn", "L0_ffn_bdn", "L0_ffn_Wsu", "L0_ffn_bsu",
                 "L0_ffn_Wsa", "L0_ffn_bsa", "L0_ffn_Wsd", "L0_ffn_bsd"]
    ffn = {n.replace("L0_ffn_", "f_"): np.ascontiguousarray(np.asarray(inputs[n]), dtype=np.float32)
           for n in ffn_names}
    ffn["f_Wup_t"] = np.ascontiguousarray(
        ffn.pop("f_Wup").reshape(6, 16, 128, 8, 128).transpose(2, 0, 3, 1, 4).reshape(128, 6 * 8 * 2048))
    ffn["f_Wsw_t"] = np.ascontiguousarray(
        ffn.pop("f_Wsw").reshape(6, 8, 128, 8, 128).transpose(2, 0, 3, 1, 4).reshape(128, 6 * 8 * 1024))
    ffn["f_Wdn_t"] = np.ascontiguousarray(
        ffn.pop("f_Wdn").reshape(6, 2, 4, 128, 4, 512).transpose(3, 0, 1, 4, 2, 5).reshape(128, 6 * 2 * 4 * 2048))
    ffn["f_Wsu_t2"] = np.ascontiguousarray(
        ffn.pop("f_Wsu").reshape(16, 128, 16, 128).transpose(1, 2, 0, 3).reshape(128, 16 * 2048))
    ffn["f_Wsa_t2"] = np.ascontiguousarray(
        ffn.pop("f_Wsa").reshape(8, 128, 8, 128).transpose(1, 2, 0, 3).reshape(128, 8 * 1024))
    ffn["f_Wsd_t2"] = np.ascontiguousarray(
        ffn.pop("f_Wsd").reshape(8, 2, 128, 4, 512).transpose(2, 3, 0, 1, 4).reshape(128, 4 * 8 * 1024))
    wqbT = np.ascontiguousarray(np.asarray(inputs["L0_attn_b_Wq"]).T)
    wkbT = np.ascontiguousarray(np.asarray(inputs["L0_attn_b_Wk"]).T)
    wvbT = np.ascontiguousarray(np.asarray(inputs["L0_attn_b_Wv"]).T)
    wobT = np.ascontiguousarray(np.asarray(inputs["L0_attn_b_Wo"]).T)
    for c in range(NC_ACT):
        b, hh = c // 2, c % 2
        t0 = TC * hh
        im = {}
        im["x_core"] = np.ascontiguousarray(x[b, t0:t0 + TC, :].T)
        im["wq_a"] = np.ascontiguousarray(
            np.asarray(inputs["L0_attn_a_Wq"])[:, hh * 512:(hh + 1) * 512]
            .reshape(16, 128, 4, 128).transpose(1, 2, 0, 3).reshape(128, 4 * 2048))
        im["wk_a"] = np.ascontiguousarray(
            np.asarray(inputs["L0_attn_a_Wk"])[:, hh * 512:(hh + 1) * 512]
            .reshape(16, 128, 4, 128).transpose(1, 2, 0, 3).reshape(128, 4 * 2048))
        im["ws_a"] = np.ascontiguousarray(
            np.asarray(inputs["L0_attn_a_Ws"])[:, hh * 512:(hh + 1) * 512]
            .reshape(16, 128, 512).transpose(1, 0, 2).reshape(128, 16 * 512))
        im["wv_a"] = np.ascontiguousarray(np.asarray(inputs["L0_attn_a_Wv"])[:, hh * 1024:(hh + 1) * 1024])
        im["wg_a"] = np.ascontiguousarray(np.asarray(inputs["L0_attn_a_Wg"])[:, hh * 1024:(hh + 1) * 1024])
        im["wo_a"] = np.ascontiguousarray(np.asarray(inputs["L0_attn_a_Wo"])[hh * 1024:(hh + 1) * 1024, :])
        im["gnw_a"] = np.ascontiguousarray(np.asarray(inputs["L0_attn_a_g_norm_weight"]).reshape(128, 1))
        im["wq_b"] = np.ascontiguousarray(
            wqbT[:, hh * 1024:(hh + 1) * 1024]
            .reshape(16, 128, 8, 128).transpose(1, 2, 0, 3).reshape(128, 8 * 2048))
        im["wk_b"] = np.ascontiguousarray(
            wkbT[:, hh * 256:(hh + 1) * 256]
            .reshape(16, 128, 2, 128).transpose(1, 2, 0, 3).reshape(128, 2 * 2048))
        im["wv_b"] = np.ascontiguousarray(
            wvbT[:, hh * 256:(hh + 1) * 256]
            .reshape(16, 128, 256).transpose(1, 0, 2).reshape(128, 16 * 256))
        im["wo_b"] = np.ascontiguousarray(wobT[hh * 1024:(hh + 1) * 1024, :])
        im.update(ffn)
        im["f_bup_t"] = np.ascontiguousarray(ffn["f_bup"].reshape(6, 8, 128).transpose(2, 0, 1).reshape(128, 48))
        im["f_bsw_t"] = np.ascontiguousarray(ffn["f_bsw"].reshape(6, 8, 128).transpose(2, 0, 1).reshape(128, 48))
        im["f_bsa_t"] = np.ascontiguousarray(ffn["f_bsa"].reshape(8, 128).T)
        im["f_bsu_t"] = np.ascontiguousarray(ffn["f_bsu"].reshape(16, 128).T)
        im["f_bg_t"] = np.ascontiguousarray(ffn["f_bg"].T)
        im["f_wg_t"] = np.ascontiguousarray(
            ffn["f_Wg"].reshape(2, 16, 128, 8).transpose(2, 0, 1, 3).reshape(128, 2 * 16 * 8))
        im["strips"] = _alibi_strips(hh)
        lm = (np.arange(128)[:, None] <= np.arange(128)[None, :]).astype(np.float32)
        im["lmask"] = lm
        im["lmask8"] = np.ascontiguousarray(np.tile(lm, (1, 8)).astype(BF))
        im["ident_bf"] = np.eye(128, dtype=BF)
        im["ident_f"] = np.eye(128, dtype=np.float32)
        bc8 = np.zeros((8, 6 * 128), np.float32)
        for n_ in range(6):
            bc8[n_, n_ * 128:(n_ + 1) * 128] = 1.0
        im["bcast8"] = bc8.astype(BF)
        im["ones_bf"] = np.ones((128, 1), dtype=BF)
        im["ones_f"] = np.ones((1, 128), np.float32)
        in_maps.append(im)
    return in_maps



PAIRS = [[0, 1], [2, 3], [4, 5], [6, 7]]


def build_program():
    nsub = CFG["n_sub"]
    nc = bass.Bass("TRN2", target_bir_lowering=False)

    din = {}

    def dt_in(name, shape, dtype=F32):
        din[name] = nc.dram_tensor(name, list(shape), dtype, kind="ExternalInput").ap()
        return din[name]

    dt_in("x_core", [D, TC])
    for n, sh in [("wq_a", [128, 4 * 2048]), ("wk_a", [128, 4 * 2048]), ("ws_a", [128, 16 * 512]),
                  ("wv_a", [D, 1024]), ("wg_a", [D, 1024]), ("wo_a", [1024, D]),
                  ("gnw_a", [128, 1]),
                  ("wq_b", [128, 8 * 2048]), ("wk_b", [128, 2 * 2048]), ("wv_b", [128, 16 * 256]),
                  ("wo_b", [1024, D]),
                  ("f_Wg", [2, D, 8]), ("f_bg", [2, 8]),
                  ("f_Wup_t", [128, 6 * 8 * 2048]), ("f_bup", [6, 1024]),
                  ("f_Wsw_t", [128, 6 * 8 * 1024]), ("f_bsw", [6, 1024]),
                  ("f_Wdn_t", [128, 6 * 2 * 4 * 2048]), ("f_bdn", [6, D]),
                  ("f_Wsu_t2", [128, 16 * 2048]), ("f_bsu", [D]),
                  ("f_Wsa_t2", [128, 8 * 1024]), ("f_bsa", [1024]),
                  ("f_Wsd_t2", [128, 4 * 8 * 1024]), ("f_bsd", [D]),
                  ("lmask", [128, 128]),
                  ("ones_f", [1, 128]), ("ident_f", [128, 128]),
                  ("f_bup_t", [128, 48]), ("f_bsw_t", [128, 48]), ("f_bsa_t", [128, 8]),
                  ("f_bsu_t", [128, 16]), ("f_bg_t", [8, 2]), ("f_wg_t", [128, 256])]:
        dt_in(n, sh)
    dt_in("ident_bf", [128, 128], BF16)
    dt_in("lmask8", [128, 1024], BF16)
    dt_in("strips", [8, 128, STRIP_W], BF16)
    dt_in("bcast8", [8, 6 * 128], BF16)
    dt_in("ones_bf", [128, 1], BF16)

    out_x = nc.dram_tensor("out_x", [D, TC], F32, kind="ExternalOutput").ap()
    dbg_out = {}
    for name, shape in CFG.get("debug_shapes", {}).items():
        dbg_out[name] = nc.dram_tensor("dbg_" + name, list(shape), F32, kind="ExternalOutput").ap()

    cache_specs = [
        ("wq_b", [128, 16384]), ("wk_b", [128, 4096]), ("wo_b", [1024, D]),
    ]
    dcache = {n: nc.dram_tensor("c_" + n, list(sh), BF16, kind="Internal").ap()
              for n, sh in cache_specs}

    s_spill = nc.dram_tensor("s_spill", [64, HA * T], F32, kind="Internal").ap()
    ag_xn_in = nc.dram_tensor("ag_xn_in", [D, TC], BF16, kind="Internal").ap()
    ag_xn_out = nc.dram_tensor("ag_xn_out", [2 * D, TC], BF16, kind="Internal").ap()
    ag_pt_in2 = [nc.dram_tensor(f"ag_pt_in{i}", [2 * D, 512], F32, kind="Internal").ap() for i in range(2)]
    h_spill = nc.dram_tensor("h_spill", [128, 48 * TC], BF16, kind="Internal").ap()
    sh_spill = nc.dram_tensor("sh_spill", [128, 16 * TC], BF16, kind="Internal").ap()
    ag_pt_out2 = [nc.dram_tensor(f"ag_pt_out{i}", [D, 512], F32, kind="Internal").ap() for i in range(2)]

    with ExitStack() as ctx:
        e = ctx.enter_context
        semh = {}
        semnames = ["cc"] + [CSEM[x] for x in ENGS]
        for q in QSEM.values():
            semnames += [f"{q}_{i}" for i in range(NDMASEM)]
        for s in semnames:
            semh[s] = e(nc.semaphore(s))

        x_res = e(nc.sbuf_tensor("x_res", [128, NCH * TC], F32))
        lmask8 = e(nc.sbuf_tensor("sb_lmask8", [128, 1024], BF16))
        identf = e(nc.sbuf_tensor("sb_identf", [128, 128], F32))
        bc8t = e(nc.sbuf_tensor("sb_bc8", [8, 6 * 128], BF16))
        ones_bf = e(nc.sbuf_tensor("sb_ones_bf", [128, 1], BF16))
        ones_f = e(nc.sbuf_tensor("sb_ones_f", [1, 128], F32))
        gnw = e(nc.sbuf_tensor("sb_gnw", [128, 1], F32))
        epsc = e(nc.sbuf_tensor("sb_epsc", [128, 1], F32))

        P = Prog(nc, semh)

        bxc = [P.buf(f"x_res{c}") for c in range(NCH)]
        bconsts = {n: P.buf("const_" + n) for n in
                   ["lmask", "lmask8", "ident", "ones_bf", "ones_f", "gnw", "eps",
                    "identf", "bc8"]}

        bcstg = [P.buf("cstg0"), P.buf("cstg1")]
        bcache = {n: P.buf("cache_" + n) for n, _ in cache_specs}
        conv_jobs = []
        conv_last = {}
        for n_, sh_ in cache_specs:
            R_, CC_ = sh_
            for r_ in range(max(1, R_ // 128)):
                for i_ in range(max(1, CC_ // 1024)):
                    w_ = min(1024, CC_)
                    conv_jobs.append((n_, din[n_][r_ * 128:(r_ + 1) * 128, i_ * w_:(i_ + 1) * w_],
                                      dcache[n_][r_ * 128:(r_ + 1) * 128, i_ * w_:(i_ + 1) * w_], w_))
            conv_last[n_] = len(conv_jobs)
        conv_state = {"i": 0, "stg": None}

        def conv_done(name):
            return name in conv_last and conv_state["i"] >= conv_last[name]

        def feed_conv(k):
            stg = conv_state["stg"]
            while k > 0 and conv_state["i"] < len(conv_jobs):
                assert stg is not None
                idx = conv_state["i"]
                conv_state["i"] += 1
                nme, src, dst, w_ = conv_jobs[idx]
                par = idx % 2
                stg_v = stg[:, par * 1024:par * 1024 + w_]
                P.op("pool", lambda en, src=src, stg_v=stg_v: en.dma_start(out=stg_v, in_=src),
                     writes=[bcstg[par]], dma=True, tag=f"cvr{idx}")
                P.op("sp", lambda en, dst=dst, stg_v=stg_v: en.dma_start(out=dst, in_=stg_v),
                     reads=[bcstg[par]], update=[bcache[nme]], dma=True, tag=f"cvw{idx}")
                k -= 1

        for c_ in range(NCH):
            P.op("sp" if c_ % 2 == 0 else "pool", lambda en, c_=c_: en.dma_start(
                out=x_res[:, c_ * TC:(c_ + 1) * TC],
                in_=din["x_core"][c_ * 128:(c_ + 1) * 128, :]),
                writes=[bxc[c_]], dma=True, tag=f"ldx{c_}")
        for nm, dst in [("lmask8", lmask8), ("ones_f", ones_f),
                        ("gnw_a", gnw)]:
            key = "gnw" if nm == "gnw_a" else nm
            P.op("sp", lambda en, dst=dst, nm=nm: en.dma_start(out=dst[:], in_=din[nm]),
                 writes=[bconsts[key]], dma=True)
        P.op("sp", lambda en: en.dma_start(out=identf[:], in_=din["ident_f"]), writes=[bconsts["identf"]], dma=True)
        P.op("sp", lambda en: en.dma_start(out=bc8t[:], in_=din["bcast8"]), writes=[bconsts["bc8"]], dma=True)
        P.op("sp", lambda en: en.dma_start(out=ones_bf[:], in_=din["ones_bf"]), writes=[bconsts["ones_bf"]], dma=True)
        P.op("dve", lambda en: en.memset(epsc[:], LN_EPS), writes=[bconsts["eps"]], tag="eps")

        ps1 = e(nc.psum_tensor("ps1", [128, 2048], F32))
        ps2 = e(nc.psum_tensor("ps2", [128, 2048], F32))
        pb1 = [P.buf(f"ps1b{i}") for i in range(4)]
        pb2 = [P.buf(f"ps2b{i}") for i in range(4)]

        def layernorm(tag, xn_t, bxn_t):
            with ExitStack() as lctx:
                le = lctx.enter_context
                bca = le(nc.sbuf_tensor(tag + "_bca", [128, TC], F32))
                bcm = le(nc.sbuf_tensor(tag + "_bcm", [128, TC], F32))
                sc_a = le(nc.sbuf_tensor(tag + "_sca", [128, TC], F32))
                sc_b = le(nc.sbuf_tensor(tag + "_scb", [128, TC], F32))
                srow = le(nc.sbuf_tensor(tag + "_srow", [1, 3 * TC], F32))
                x2bf = le(nc.sbuf_tensor(tag + "_x2bf", [128, 2 * TC], BF16))
                xbf = le(nc.sbuf_tensor(tag + "_xbf", [128, 2 * TC], BF16))
                xnc = le(nc.sbuf_tensor(tag + "_xnc", [128, 2 * TC], BF16)) if xn_t is None else None
                bbc = P.buf(tag + "bcast")
                bsc = [P.buf(tag + "sca"), P.buf(tag + "scb")]
                bxbf = [P.buf(tag + "xbf0"), P.buf(tag + "xbf1")]
                bx2bf = [P.buf(tag + "x2bf0"), P.buf(tag + "x2bf1")]
                bsrow = P.buf(tag + "srow")
                bxnc = [P.buf(tag + "xnc0"), P.buf(tag + "xnc1")]
                stage_toks = []

                for c in range(NCH):
                    sl = slice(c * TC, (c + 1) * TC)
                    par = c % 2
                    xb = xbf[:, par * TC:(par + 1) * TC]
                    x2 = x2bf[:, par * TC:(par + 1) * TC]
                    P.op("dve", lambda en, sl=sl, xb=xb: en.tensor_copy(xb, x_res[:, sl]),
                         reads=[bxc[c]], writes=[bxbf[par]], tag=f"{tag}_xbf{c}")
                    P.op("act", lambda en, sl=sl, x2=x2: en.activation(out=x2, in_=x_res[:, sl], func=AF.Square),
                         reads=[bxc[c]], writes=[bx2bf[par]], tag=f"{tag}_x2{c}")
                    for h in range(2):
                        hs = slice(h * 512, (h + 1) * 512)
                        first, last = c == 0, c == NCH - 1
                        P.op("pe", lambda en, xb=xb, hs=hs, first=first, last=last, h=h:
                             en.matmul(ps1[0:1, h * 512:(h + 1) * 512], ones_bf[:], xb[:, hs], start=first, stop=last),
                             reads=[bxbf[par], bconsts["ones_bf"]],
                             writes=[pb1[h]] if first else (), update=[pb1[h]] if last else (),
                             tag=f"{tag}_s1_{c}_{h}")
                        P.op("pe", lambda en, x2=x2, hs=hs, first=first, last=last, h=h:
                             en.matmul(ps1[0:1, 1024 + h * 512:1024 + (h + 1) * 512], ones_bf[:], x2[:, hs], start=first, stop=last),
                             reads=[bx2bf[par], bconsts["ones_bf"]],
                             writes=[pb1[2 + h]] if first else (), update=[pb1[2 + h]] if last else (),
                             tag=f"{tag}_s2_{c}_{h}")
                m_row = srow[0:1, 0:TC]
                a_row = srow[0:1, TC:2 * TC]
                ss_row = srow[0:1, 2 * TC:3 * TC]
                v_row = a_row
                for h in range(2):
                    hs = slice(h * 512, (h + 1) * 512)
                    P.op("dve", lambda en, hs=hs, h=h:
                         en.tensor_scalar_mul(srow[0:1, hs], ps1[0:1, h * 512:(h + 1) * 512], 1.0 / D),
                         reads=[pb1[h]], writes=[bsrow] if h == 0 else (), update=[bsrow] if h else (),
                         tag=f"{tag}_m{h}")
                    P.op("dve", lambda en, h=h:
                         en.tensor_scalar_mul(srow[0:1, 2 * TC + h * 512:2 * TC + (h + 1) * 512],
                                              ps1[0:1, 1024 + h * 512:1024 + (h + 1) * 512], 1.0 / D),
                         reads=[pb1[2 + h]], update=[bsrow], tag=f"{tag}_ss{h}")
                P.op("dve", lambda en: en.scalar_tensor_tensor(
                    out=v_row, in0=m_row, scalar=-1.0, in1=m_row, op0=ALU.mult, op1=ALU.mult),
                    reads=[bsrow], update=[bsrow], tag=f"{tag}_mm")
                P.op("dve", lambda en: en.tensor_add(v_row, v_row, ss_row),
                     reads=[bsrow], update=[bsrow], tag=f"{tag}_var")
                P.op("act", lambda en: en.activation(out=ss_row, in_=v_row, func=AF.Ln, bias=epsc[0:1, :]),
                     reads=[bsrow, bconsts["eps"]], update=[bsrow], tag=f"{tag}_ln")
                P.op("act", lambda en: en.activation(out=a_row, in_=ss_row, func=AF.Exp, scale=-0.5),
                     reads=[bsrow], update=[bsrow], tag=f"{tag}_exp")
                bbcm = P.buf(tag + "bbcm")
                for h in range(2):
                    hs = slice(h * 512, (h + 1) * 512)
                    P.op("pe", lambda en, hs=hs:
                         en.matmul(ps2[:, 0:512], ones_f[:], m_row[0:1, hs], start=True, stop=True),
                         reads=[bsrow, bconsts["ones_f"]], writes=[pb2[0]], tag=f"{tag}_bcm{h}")
                    P.op("dve", lambda en, hs=hs: en.tensor_copy(bcm[:, hs], ps2[:, 0:512]),
                         reads=[pb2[0]], writes=[bbcm] if h == 0 else (), update=[bbcm] if h else (),
                         tag=f"{tag}_bcmc{h}")
                    P.op("pe", lambda en, hs=hs:
                         en.matmul(ps2[:, 512:1024], ones_f[:], a_row[0:1, hs], start=True, stop=True),
                         reads=[bsrow, bconsts["ones_f"]], writes=[pb2[1]], tag=f"{tag}_bca{h}")
                    P.op("dve", lambda en, hs=hs: en.tensor_copy(bca[:, hs], ps2[:, 512:1024]),
                         reads=[pb2[1]], writes=[bbc] if h == 0 else (), update=[bbc] if h else (),
                         tag=f"{tag}_bcac{h}")
                for c in range(NCH):
                    sl = slice(c * TC, (c + 1) * TC)
                    par = c % 2
                    sc = (sc_a, sc_b)[par]
                    P.op("dve", lambda en, sl=sl, sc=sc: en.tensor_sub(sc[:], x_res[:, sl], bcm[:]),
                         reads=[bxc[c], bbcm], writes=[bsc[par]], tag=f"{tag}_n1_{c}")
                    if xn_t is not None:
                        P.op("dve", lambda en, sl=sl, sc=sc: en.tensor_mul(xn_t[:, sl], sc[:], bca[:]),
                             reads=[bsc[par], bbc], writes=[bxn_t[c]], tag=f"{tag}_n2_{c}")
                    else:
                        dst = xnc[:, par * TC:(par + 1) * TC]
                        P.op("dve", lambda en, sc=sc, dst=dst: en.tensor_mul(dst, sc[:], bca[:]),
                             reads=[bsc[par], bbc], writes=[bxnc[par]], tag=f"{tag}_n2_{c}")
                        tkn = P.op("sp", lambda en, dst=dst, c=c: en.dma_start(
                            out=ag_xn_in.rearrange("(c p) t -> p c t", p=128)[:, c, :], in_=dst),
                            reads=[bxnc[par]], dma=True, tag=f"{tag}_st{c}")
                        stage_toks.append(tkn)
                P.fence([bbc, bbcm, bsc[0], bsc[1], bxbf[0], bxbf[1], bx2bf[0], bx2bf[1],
                         bsrow, bxnc[0], bxnc[1]], tag=tag + "f")
                return stage_toks

        bag_xn_in = P.buf("ag_xn_in")
        bag_xn_out = P.buf("ag_xn_out")
        bag_ptr = [[P.buf(f"agpt{w}_{dc}") for dc in range(16)] for w in range(4)]
        bag_pt_out2 = [P.buf("ag_pt_out0"), P.buf("ag_pt_out1")]
        bspill = [P.buf(f"s_spill{i}") for i in range(4)]

        def attn_a():
            stage_toks = layernorm("ln0", None, None)
            tcc = P.op("pool", lambda en: en.collective_compute(
                "AllGather", ALU.bypass, ins=[ag_xn_in.opt()], outs=[ag_xn_out.opt()],
                replica_groups=PAIRS),
                writes=[bag_xn_out], cc=True, tag="ag_xn")
            th = P.thunks["pool"][-1]
            for t in stage_toks:
                P._need(th, t)

            pt_toks = {0: [], 1: []}
            rs_emitted = [False, False]

            def emit_rs(colh, tg):
                rs_emitted[colh] = True
                P.op("pool", lambda en, colh=colh: en.collective_compute(
                    "ReduceScatter", ALU.add, ins=[ag_pt_in2[colh].opt()], outs=[ag_pt_out2[colh].opt()],
                    replica_groups=PAIRS),
                    writes=[bag_pt_out2[colh]], cc=True, tag=tg)
                thpt = P.thunks["pool"][-1]
                for t in pt_toks[colh]:
                    P._need(thpt, t)
            with ExitStack() as actx:
                ae = actx.enter_context
                xnw = ae(nc.sbuf_tensor("a_xnw", [128, NCH * 512], BF16))
                bxnw = [P.buf(f"xnw{c}") for c in range(NCH)]

                def gather_xnw(w, tag):
                    for c in range(NCH):
                        def fn(en, c=c, w=w):
                            return en.dma_start(
                                out=xnw[:, c * 512:(c + 1) * 512],
                                in_=ag_xn_out[(w // 2) * D + c * 128:(w // 2) * D + (c + 1) * 128,
                                              (w % 2) * 512:(w % 2) * 512 + 512])
                        P.op("pool", fn, reads=[bag_xn_out],
                             writes=[bxnw[c]], dma=True, tag=f"{tag}_g{c}")

                zst = ae(nc.sbuf_tensor("a_zst", [64, 8 * 8], F32))
                bzst = P.buf("zst")
                zneg8 = ae(nc.sbuf_tensor("a_zneg", [64, 8], F32))
                bzneg = P.buf("zneg8")
                with ExitStack() as p1ctx:
                    p1 = p1ctx.enter_context
                    wst = p1(nc.sbuf_tensor("a_wst", [128, NCH * 512], BF16))
                    s_win = p1(nc.sbuf_tensor("a_swin", [64, HA * 512], F32))
                    zdump = p1(nc.sbuf_tensor("a_zdump", [64, 512], F32))
                    bwst = [P.buf(f"wst{i}") for i in range(4)]
                    bswin = P.buf("swin")
                    bzdump = P.buf("zdump")
                    P.op("pool", lambda en: en.dma_start(out=wst[:], in_=din["ws_a"]),
                         writes=list(bwst), dma=True, tag="ws_all")
                    for w in range(4):
                        gather_xnw(w, f"p1x{w}")
                        for h in range(HA):
                            hl = h % 4
                            reg = ps2[0:64, hl * 512:(hl + 1) * 512]
                            rb = pb2[hl]
                            for c in range(NCH):
                                P.op("pe", lambda en, h=h, c=c, reg=reg:
                                     en.matmul(reg, wst[:, c * 512 + h * 64:c * 512 + h * 64 + 64],
                                               xnw[:, c * 512:(c + 1) * 512],
                                               start=(c == 0), stop=(c == NCH - 1)),
                                     reads=[bwst[h // 2], bxnw[c]],
                                     writes=[rb] if c == 0 else (), update=[rb] if c == NCH - 1 else (),
                                     tag=f"sproj{w}_{h}_{c}")
                            P.op("dve", lambda en, h=h, reg=reg, rb=rb: en.tensor_scalar(
                                out=s_win[:, h * 512:(h + 1) * 512], in0=reg,
                                scalar1=-CLAMP, scalar2=CLAMP, op0=ALU.max, op1=ALU.min),
                                reads=[rb],
                                writes=[bswin] if h == 0 else (), update=[bswin] if h == HA - 1 else (),
                                tag=f"sclip{w}_{h}")
                        P.op("sp", lambda en, w=w: en.dma_start(
                            out=s_spill.rearrange("m (h t) -> m h t", h=HA)[:, :, w * 512:(w + 1) * 512],
                            in_=s_win.ap().rearrange("m (h t) -> m h t", h=HA)),
                            reads=[bswin], writes=[bspill[w]], dma=True, tag=f"sspill{w}")
                        for h in range(HA):
                            P.op("dve", lambda en, h=h: en.reduce_max(
                                zst[:, h:h + 1], s_win[:, h * 512:(h + 1) * 512], axis=mybir.AxisListType.X),
                                reads=[bswin],
                                writes=[bzst] if (w == 0 and h == 0) else (), update=[bzst],
                                tag=f"zmax{w}_{h}")
                        P.op("dve", lambda en: en.tensor_scalar_mul(zst[:, 48:56], zst[:, 0:8], -1.0),
                             reads=[bzst], update=[bzst], tag=f"znegw{w}")
                        for h in range(HA):
                            P.op("act", lambda en, h=h: en.activation(
                                out=zdump[:], in_=s_win[:, h * 512:(h + 1) * 512], func=AF.Exp,
                                bias=zst[:, 48 + h:49 + h], accum_out=zst[:, 8 + h:9 + h]),
                                reads=[bswin, bzst], writes=[bzdump], update=[bzst], tag=f"zexp{w}_{h}")
                        if w == 0:
                            P.op("dve", lambda en: en.tensor_copy(zst[:, 16:24], zst[:, 0:8]),
                                 reads=[bzst], update=[bzst], tag="zm0")
                            P.op("dve", lambda en: en.tensor_copy(zst[:, 24:32], zst[:, 8:16]),
                                 reads=[bzst], update=[bzst], tag="zs0")
                        else:
                            P.op("dve", lambda en: en.tensor_max(zst[:, 32:40], zst[:, 16:24], zst[:, 0:8]),
                                 reads=[bzst], update=[bzst], tag=f"zmg{w}a")
                            P.op("dve", lambda en: en.tensor_sub(zst[:, 48:56], zst[:, 16:24], zst[:, 32:40]),
                                 reads=[bzst], update=[bzst], tag=f"zmg{w}b")
                            P.op("act", lambda en: en.activation(out=zst[:, 48:56], in_=zst[:, 48:56], func=AF.Exp),
                                 reads=[bzst], update=[bzst], tag=f"zmg{w}c")
                            P.op("dve", lambda en: en.tensor_mul(zst[:, 24:32], zst[:, 24:32], zst[:, 48:56]),
                                 reads=[bzst], update=[bzst], tag=f"zmg{w}d")
                            P.op("dve", lambda en: en.tensor_sub(zst[:, 56:64], zst[:, 0:8], zst[:, 32:40]),
                                 reads=[bzst], update=[bzst], tag=f"zmg{w}e")
                            P.op("act", lambda en: en.activation(out=zst[:, 56:64], in_=zst[:, 56:64], func=AF.Exp),
                                 reads=[bzst], update=[bzst], tag=f"zmg{w}f")
                            P.op("dve", lambda en: en.tensor_mul(zst[:, 48:56], zst[:, 8:16], zst[:, 56:64]),
                                 reads=[bzst], update=[bzst], tag=f"zmg{w}g")
                            P.op("dve", lambda en: en.tensor_add(zst[:, 24:32], zst[:, 24:32], zst[:, 48:56]),
                                 reads=[bzst], update=[bzst], tag=f"zmg{w}h")
                            P.op("dve", lambda en: en.tensor_copy(zst[:, 16:24], zst[:, 32:40]),
                                 reads=[bzst], update=[bzst], tag=f"zmg{w}i")
                    P.op("act", lambda en: en.activation(out=zst[:, 56:64], in_=zst[:, 24:32], func=AF.Ln),
                         reads=[bzst], update=[bzst], tag="zlog")
                    P.op("dve", lambda en: en.tensor_add(zst[:, 0:8], zst[:, 16:24], zst[:, 56:64]),
                         reads=[bzst], update=[bzst], tag="zfin")
                    P.op("dve", lambda en: en.tensor_scalar_mul(zneg8[:], zst[:, 0:8], -1.0),
                         reads=[bzst], writes=[bzneg], tag="znegf")
                    P.fence([bwst[0], bwst[1], bwst[2], bwst[3], bswin, bzdump] + bxnw, tag="p1f")

                wqk = ae(nc.sbuf_tensor("a_wqk", [128, 2 * 2048], BF16))
                wos = ae(nc.sbuf_tensor("a_wos", [128, 4 * 2048], BF16))
                wvg = ae(nc.sbuf_tensor("a_wvg", [128, 2 * 1024], BF16))
                q_win = ae(nc.sbuf_tensor("a_qwin", [64, HA * 512], BF16))
                k_win = ae(nc.sbuf_tensor("a_kwin", [64, HA * 512], BF16))
                v_win = ae(nc.sbuf_tensor("a_vwin", [128, 4 * 1024], BF16))
                g_win = ae(nc.sbuf_tensor("a_gwin", [128, 4 * 1024], BF16))
                s_ch = ae(nc.sbuf_tensor("a_sch", [64, HA * 128], F32))
                E_f = ae(nc.sbuf_tensor("a_Ef", [64, HA * 128], F32))
                E_bf = ae(nc.sbuf_tensor("a_Ebf", [64, HA * 128], BF16))
                k_f32 = ae(nc.sbuf_tensor("a_kf32", [64, HA * 128], F32))
                D_s = ae(nc.sbuf_tensor("a_Ds", [64, HA * 128], F32))
                R_t = ae(nc.sbuf_tensor("a_Rt", [64, HA * 128], F32))
                okh = ae(nc.sbuf_tensor("a_okh", [64, HA * 128], F32))
                pv_bf = ae(nc.sbuf_tensor("a_pv", [64, HA * 128], BF16))
                p_hat = ae(nc.sbuf_tensor("a_phat", [64, HA * 128], BF16))
                P_tm = ae(nc.sbuf_tensor("a_Ptm", [128, HA * 64], BF16))
                k_tm = ae(nc.sbuf_tensor("a_ktm", [128, HA * 64], BF16))
                ST_m = ae(nc.sbuf_tensor("a_STm", [128, HA * 128], BF16))
                Wt_m = ae(nc.sbuf_tensor("a_Wtm", [128, HA * 128], BF16))
                HK_f = ae(nc.sbuf_tensor("a_HKf", [64, HA * 64], F32))
                HK_bf = ae(nc.sbuf_tensor("a_HKbf", [64, HA * 64], BF16))
                HV_f = ae(nc.sbuf_tensor("a_HVf", [64, HA * 128], F32))
                HV_bf = ae(nc.sbuf_tensor("a_HVbf", [64, HA * 128], BF16))
                carry = ae(nc.sbuf_tensor("a_carry", [64, 8], F32))
                ones_scan = ae(nc.sbuf_tensor("a_ones", [64, 128], F32))
                sigf = ae(nc.sbuf_tensor("a_sigf", [128, 48], F32))
                o2d = ae(nc.sbuf_tensor("a_o2d", [128, 128], BF16))
                o_g = ae(nc.sbuf_tensor("a_og", [128, HA * 128], F32))
                og_fm = ae(nc.sbuf_tensor("a_ogfm", [128, HA * 512], BF16))
                stg = ae(nc.sbuf_tensor("a_stg", [128, 512], F32))

                bwqk = [P.buf("wqk0"), P.buf("wqk1")]
                bwos = [P.buf(f"wos{i}") for i in range(4)]
                bwvg = [P.buf("wvg0"), P.buf("wvg1")]
                bqw, bkw, bvw, bgw = P.buf("qwin"), P.buf("kwin"), P.buf("vwin"), P.buf("gwin")
                bsch = [P.buf("sch0")]
                bEf, bEbf, bkf32 = P.buf("Ef"), P.buf("Ebf"), P.buf("kf32")
                bDs, bRt = P.buf("Ds"), P.buf("Rt")
                bokh, bpv, bph = P.buf("okh"), P.buf("pv"), P.buf("phat")
                bPtm, bktm, bSTm, bWtm = P.buf("Ptm"), P.buf("ktm"), P.buf("STm"), P.buf("Wtm")
                bHKf, bHKbf, bHVf, bHVbf = P.buf("HKf"), P.buf("HKbf"), P.buf("HVf"), P.buf("HVbf")
                bcarry, bones_s, bsigf = P.buf("carry"), P.buf("ones_s"), P.buf("sigf")
                bo2, bog, bogfm = P.buf("o2"), P.buf("og"), P.buf("ogfm")
                bstg = [P.buf("stg0")]

                P.op("dve", lambda en: en.memset(ones_scan[:], 1.0), writes=[bones_s], tag="amemset_ones")
                P.op("dve", lambda en: en.memset(carry[:], 0.0), writes=[bcarry], tag="amemset_c")
                P.op("dve", lambda en: en.memset(HK_f[:], 0.0), writes=[bHKf], tag="amemset_hkf")
                P.op("dve", lambda en: en.memset(HK_bf[:], 0.0), writes=[bHKbf], tag="amemset_hkb")
                P.op("dve", lambda en: en.memset(HV_f[:], 0.0), writes=[bHVf], tag="amemset_hvf")
                P.op("dve", lambda en: en.memset(HV_bf[:], 0.0), writes=[bHVbf], tag="amemset_hvb")

                for w in range(4):
                    gather_xnw(w, f"p2x{w}")
                    for (wname, wslab, bslab, dwin, bwin, act_silu) in [
                            ("wv_a", wvg, bwvg, v_win, bvw, False),
                            ("wg_a", wvg, bwvg, g_win, bgw, True)]:
                        for c in range(NCH):
                            cp = c % 2
                            P.op("pool", lambda en, c=c, cp=cp, wname=wname, wslab=wslab: en.dma_start(
                                out=wslab[:, cp * 1024:(cp + 1) * 1024],
                                in_=din[wname][c * 128:(c + 1) * 128, :]),
                                writes=[bslab[cp]], dma=True, tag=f"{wname}s{w}_{c}")
                            for ridx in range(8):
                                tc4, hf = ridx // 2, ridx % 2
                                reg = (ps1 if ridx < 4 else ps2)[:, (ridx % 4) * 512:(ridx % 4 + 1) * 512]
                                rb = (pb1 if ridx < 4 else pb2)[ridx % 4]
                                P.op("pe", lambda en, c=c, cp=cp, tc4=tc4, hf=hf, reg=reg, wslab=wslab:
                                     en.matmul(reg, xnw[:, c * 512 + tc4 * 128:c * 512 + tc4 * 128 + 128],
                                               wslab[:, cp * 1024 + hf * 512:cp * 1024 + hf * 512 + 512],
                                               start=(c == 0), stop=(c == NCH - 1)),
                                     reads=[bxnw[c], bslab[cp]],
                                     writes=[rb] if c == 0 else (), update=[rb] if c == NCH - 1 else (),
                                     tag=f"{wname}mm{w}_{c}_{ridx}")
                        for ridx in range(8):
                            tc4, hf = ridx // 2, ridx % 2
                            reg = (ps1 if ridx < 4 else ps2)[:, (ridx % 4) * 512:(ridx % 4 + 1) * 512]
                            rb = (pb1 if ridx < 4 else pb2)[ridx % 4]
                            dst = dwin[:, tc4 * 1024 + hf * 512:tc4 * 1024 + hf * 512 + 512]
                            if act_silu:
                                P.op("act", lambda en, reg=reg, dst=dst: en.activation(out=dst, in_=reg, func=AF.Silu),
                                     reads=[rb], writes=[bwin] if ridx == 0 else (), update=[bwin] if ridx == 7 else (),
                                     tag=f"gsilu{w}_{ridx}")
                            else:
                                P.op("dve", lambda en, reg=reg, dst=dst: en.tensor_copy(dst, reg),
                                     reads=[rb], writes=[bwin] if ridx == 0 else (), update=[bwin] if ridx == 7 else (),
                                     tag=f"vevac{w}_{ridx}")
                    for (wname, wslab, bslab, dwin, bwin, scl) in [
                            ("wq_a", wqk, bwqk, q_win, bqw, QSCALE),
                            ("wk_a", wqk, bwqk, k_win, bkw, 1.0)]:
                        for pp in range(4):
                            par = pp % 2
                            P.op("pool", lambda en, pp=pp, par=par, wname=wname, wslab=wslab: en.dma_start(
                                out=wslab[:, par * 2048:(par + 1) * 2048],
                                in_=din[wname][:, pp * 2048:(pp + 1) * 2048]),
                                writes=[bslab[par]], dma=True, tag=f"{wname}l{w}_{pp}")
                            for hl in range(2):
                                h = pp * 2 + hl
                                reg = ps2[0:64, (h % 4) * 512:(h % 4 + 1) * 512]
                                rb = pb2[h % 4]
                                for c in range(NCH):
                                    P.op("pe", lambda en, h=h, c=c, par=par, reg=reg, wslab=wslab, hl=hl:
                                         en.matmul(reg, wslab[:, par * 2048 + c * 128 + hl * 64:par * 2048 + c * 128 + hl * 64 + 64],
                                                   xnw[:, c * 512:(c + 1) * 512],
                                                   start=(c == 0), stop=(c == NCH - 1)),
                                         reads=[bslab[par], bxnw[c]],
                                         writes=[rb] if c == 0 else (), update=[rb] if c == NCH - 1 else (),
                                         tag=f"{wname}mm{w}_{h}_{c}")
                                P.op("act", lambda en, h=h, reg=reg, dwin=dwin, scl=scl: en.activation(
                                    out=dwin[:, h * 512:(h + 1) * 512], in_=reg, func=AF.Identity, scale=scl),
                                    reads=[rb], writes=[bwin] if h == 0 else (), update=[bwin] if h == 7 else (),
                                    tag=f"{wname}ev{w}_{h}")

                    for tc4 in range(4):
                        cg = w * 4 + tc4
                        par = 0
                        schv = s_ch[:]
                        P.op("sp", lambda en, cg=cg, schv=schv: en.dma_start(
                            out=schv.rearrange("m (h t) -> m h t", h=HA),
                            in_=s_spill.rearrange("m (h t) -> m h t", h=HA)[:, :, cg * 128:(cg + 1) * 128]),
                            reads=[bspill[cg // 4]], writes=[bsch[par]], dma=True, tag=f"sld{cg}")
                        for h in range(HA):
                            P.op("act", lambda en, h=h, schv=schv: en.activation(
                                out=E_f[:, h * 128:(h + 1) * 128], in_=schv[:, h * 128:(h + 1) * 128],
                                func=AF.Exp, bias=zneg8[:, h:h + 1]),
                                reads=[bsch[par], bzneg],
                                writes=[bEf] if h == 0 else (), update=[bEf] if h == HA - 1 else (),
                                tag=f"E{cg}_{h}")
                        P.op("dve", lambda en: en.tensor_copy(E_bf[:], E_f[:]),
                             reads=[bEf], writes=[bEbf], tag=f"Ebf{cg}")
                        P.op("dve", lambda en, tc4=tc4: en.tensor_copy(
                            k_f32.ap().rearrange("m (h t) -> m h t", h=HA),
                            k_win.ap().rearrange("m (h t) -> m h t", h=HA)[:, :, tc4 * 128:(tc4 + 1) * 128]),
                            reads=[bkw], writes=[bkf32], tag=f"kf32{cg}")
                        for h in range(HA):
                            P.op("dve", lambda en, h=h: en.tensor_tensor_scan(
                                out=D_s[:, h * 128:(h + 1) * 128], data0=ones_scan[:],
                                data1=E_f[:, h * 128:(h + 1) * 128], initial=carry[:, h:h + 1],
                                op0=ALU.mult, op1=ALU.add),
                                reads=[bEf, bones_s, bcarry],
                                writes=[bDs] if h == 0 else (), update=[bDs] if h == HA - 1 else (),
                                tag=f"scan{cg}_{h}")
                        P.op("dve", lambda en: en.tensor_copy(
                            carry[:], D_s.ap().rearrange("m (h t) -> m h t", h=HA)[:, :, 127]),
                            reads=[bDs], writes=[bcarry], tag=f"carry{cg}")
                        P.op("act", lambda en: en.activation(out=okh[:], in_=D_s[:], func=AF.Ln),
                             reads=[bDs], writes=[bokh], tag=f"lnD{cg}")
                        P.op("act", lambda en: en.activation(out=R_t[:], in_=okh[:], func=AF.Exp, scale=-1.0),
                             reads=[bokh], writes=[bRt], tag=f"R{cg}")
                        for h in range(HA):
                            P.op("pe", lambda en, h=h: en.transpose(
                                ps1[:, h * 64:(h + 1) * 64], E_f[:, h * 128:(h + 1) * 128], identf[0:64, 0:64]),
                                reads=[bEf, bconsts["identf"]],
                                writes=[pb1[0]] if h == 0 else (),
                                update=[pb1[0]] if h == HA - 1 else (), tag=f"trP{cg}_{h}")
                        for h in range(HA):
                            P.op("pe", lambda en, h=h: en.transpose(
                                ps1[:, 512 + h * 64:512 + (h + 1) * 64], k_f32[:, h * 128:(h + 1) * 128], identf[0:64, 0:64]),
                                reads=[bkf32, bconsts["identf"]],
                                writes=[pb1[1]] if h == 0 else (),
                                update=[pb1[1]] if h == HA - 1 else (), tag=f"trK{cg}_{h}")
                        P.op("dve", lambda en: en.tensor_copy(P_tm[:], ps1[:, 0:512]),
                             reads=[pb1[0]], writes=[bPtm], tag=f"Ptm{cg}")
                        P.op("dve", lambda en: en.tensor_copy(k_tm[:], ps1[:, 512:1024]),
                             reads=[pb1[1]], writes=[bktm], tag=f"ktm{cg}")
                        for h in range(HA):
                            P.op("pe", lambda en, h=h, tc4=tc4: en.matmul(
                                ps1[:, h * 128:(h + 1) * 128],
                                k_win[:, h * 512 + tc4 * 128:h * 512 + tc4 * 128 + 128],
                                q_win[:, h * 512 + tc4 * 128:h * 512 + tc4 * 128 + 128],
                                start=True, stop=True),
                                reads=[bkw, bqw],
                                writes=[pb1[0], pb1[1]] if h == 0 else (),
                                update=[pb1[0], pb1[1]] if h == HA - 1 else (), tag=f"ST{cg}_{h}")
                        P.op("dve", lambda en: en.tensor_mul(ST_m[:], ps1[:, 0:1024], lmask8[:]),
                             reads=[pb1[0], pb1[1], bconsts["lmask8"]], writes=[bSTm], tag=f"STm{cg}")
                        for h in range(HA):
                            P.op("pe", lambda en, h=h: en.matmul(
                                ps2[0:64, h * 128:(h + 1) * 128], P_tm[:, h * 64:(h + 1) * 64],
                                ST_m[:, h * 128:(h + 1) * 128], start=True, stop=False),
                                reads=[bPtm, bSTm],
                                writes=[pb2[0], pb2[1]] if h == 0 else (), tag=f"oki{cg}_{h}")
                            P.op("pe", lambda en, h=h, tc4=tc4: en.matmul(
                                ps2[0:64, h * 128:(h + 1) * 128], HK_bf[:, h * 64:(h + 1) * 64],
                                q_win[:, h * 512 + tc4 * 128:h * 512 + tc4 * 128 + 128],
                                start=False, stop=True),
                                reads=[bHKbf, bqw],
                                update=[pb2[0], pb2[1]] if h == HA - 1 else (), tag=f"okc{cg}_{h}")
                        P.op("dve", lambda en: en.tensor_mul(okh[:], ps2[0:64, 0:1024], R_t[:]),
                             reads=[pb2[0], pb2[1], bRt], writes=[bokh], tag=f"okh{cg}")
                        P.op("act", lambda en: en.activation(out=pv_bf[:], in_=okh[:], func=AF.Exp),
                             reads=[bokh], writes=[bpv], tag=f"pv{cg}")
                        P.op("dve", lambda en: en.tensor_mul(p_hat[:], pv_bf[:], R_t[:]),
                             reads=[bpv, bRt], writes=[bph], tag=f"ph{cg}")
                        for h in range(HA):
                            P.op("pe", lambda en, h=h: en.matmul(
                                ps2[:, 2040 + h:2041 + h], pv_bf[:, h * 128:(h + 1) * 128],
                                ones_bf[0:64, :], start=True, stop=True),
                                reads=[bpv, bconsts["ones_bf"]],
                                writes=[pb2[3]] if h == 0 else (), update=[pb2[3]] if h == HA - 1 else (),
                                tag=f"sig{cg}_{h}")
                        for h in range(HA):
                            P.op("pe", lambda en, h=h: en.matmul(
                                ps1[:, h * 128:(h + 1) * 128], E_bf[:, h * 128:(h + 1) * 128],
                                p_hat[:, h * 128:(h + 1) * 128], start=True, stop=True),
                                reads=[bEbf, bph],
                                writes=[pb1[0], pb1[1]] if h == 0 else (),
                                update=[pb1[0], pb1[1]] if h == HA - 1 else (), tag=f"W{cg}_{h}")
                        P.op("dve", lambda en: en.tensor_mul(Wt_m[:], ps1[:, 0:1024], lmask8[:]),
                             reads=[pb1[0], pb1[1], bconsts["lmask8"]], writes=[bWtm], tag=f"Wtm{cg}")
                        for h in range(HA):
                            P.op("pe", lambda en, h=h, tc4=tc4: en.matmul(
                                ps2[:, h * 128:(h + 1) * 128], Wt_m[:, h * 128:(h + 1) * 128],
                                v_win[:, tc4 * 1024 + h * 128:tc4 * 1024 + h * 128 + 128],
                                start=True, stop=False),
                                reads=[bWtm, bvw],
                                writes=[pb2[0], pb2[1]] if h == 0 else (), tag=f"oi{cg}_{h}")
                            P.op("pe", lambda en, h=h: en.matmul(
                                ps2[:, h * 128:(h + 1) * 128], p_hat[:, h * 128:(h + 1) * 128],
                                HV_bf[:, h * 128:(h + 1) * 128], start=False, stop=True),
                                reads=[bph, bHVbf],
                                update=[pb2[0], pb2[1]] if h == HA - 1 else (), tag=f"oc{cg}_{h}")
                        P.op("act", lambda en: en.activation(out=sigf[:, 0:8], in_=ps2[:, 2040:2048], func=AF.Square),
                             reads=[pb2[3]], writes=[bsigf], tag=f"sg2{cg}")
                        for h in range(HA):
                            P.op("act", lambda en, h=h: en.activation(
                                out=o2d[:], in_=ps2[:, h * 128:(h + 1) * 128], func=AF.Square,
                                accum_out=sigf[:, 8 + h:9 + h]),
                                reads=[pb2[0], pb2[1]], writes=[bo2], update=[bsigf], tag=f"ss{cg}_{h}")
                        P.op("dve", lambda en: en.tensor_scalar_mul(sigf[:, 16:24], sigf[:, 8:16], 1.0 / DV),
                             reads=[bsigf], update=[bsigf], tag=f"f1{cg}")
                        P.op("dve", lambda en: en.scalar_tensor_tensor(
                            out=sigf[:, 24:32], in0=sigf[:, 0:8], scalar=A_EPS, in1=sigf[:, 16:24],
                            op0=ALU.mult, op1=ALU.add),
                            reads=[bsigf], update=[bsigf], tag=f"f2{cg}")
                        P.op("act", lambda en: en.activation(out=sigf[:, 32:40], in_=sigf[:, 24:32], func=AF.Ln),
                             reads=[bsigf], update=[bsigf], tag=f"f3{cg}")
                        P.op("act", lambda en: en.activation(out=sigf[:, 32:40], in_=sigf[:, 32:40], func=AF.Exp, scale=-0.5),
                             reads=[bsigf], update=[bsigf], tag=f"f4{cg}")
                        for h in range(HA):
                            P.op("dve", lambda en, h=h, tc4=tc4: en.scalar_tensor_tensor(
                                out=o_g[:, h * 128:(h + 1) * 128], in0=ps2[:, h * 128:(h + 1) * 128],
                                scalar=sigf[:, 32 + h:33 + h],
                                in1=g_win[:, tc4 * 1024 + h * 128:tc4 * 1024 + h * 128 + 128],
                                op0=ALU.mult, op1=ALU.mult),
                                reads=[pb2[0], pb2[1], bsigf, bgw],
                                writes=[bog] if h == 0 else (), update=[bog] if h == HA - 1 else (),
                                tag=f"og{cg}_{h}")
                        for h in range(HA):
                            P.op("pe", lambda en, h=h: en.matmul(
                                ps2[0:64, 1024 + h * 64:1024 + (h + 1) * 64], k_tm[:, h * 64:(h + 1) * 64],
                                P_tm[:, h * 64:(h + 1) * 64], start=True, stop=True),
                                reads=[bktm, bPtm],
                                writes=[pb2[2]] if h == 0 else (), update=[pb2[2]] if h == HA - 1 else (),
                                tag=f"dhk{cg}_{h}")
                            P.op("pe", lambda en, h=h, tc4=tc4: en.matmul(
                                ps1[0:64, 1024 + h * 128:1024 + (h + 1) * 128], P_tm[:, h * 64:(h + 1) * 64],
                                v_win[:, tc4 * 1024 + h * 128:tc4 * 1024 + h * 128 + 128],
                                start=True, stop=True),
                                reads=[bPtm, bvw],
                                writes=[pb1[2], pb1[3]] if h == 0 else (),
                                update=[pb1[2], pb1[3]] if h == HA - 1 else (), tag=f"dhv{cg}_{h}")
                        P.op("dve", lambda en: en.tensor_add(HK_f[:], HK_f[:], ps2[0:64, 1024:1536]),
                             reads=[pb2[2]], writes=[bHKf], tag=f"hkadd{cg}")
                        P.op("dve", lambda en: en.tensor_add(HV_f[:], HV_f[:], ps1[0:64, 1024:2048]),
                             reads=[pb1[2], pb1[3]], writes=[bHVf], tag=f"hvadd{cg}")
                        P.op("dve", lambda en: en.tensor_copy(HK_bf[:], HK_f[:]),
                             reads=[bHKf], writes=[bHKbf], tag=f"hkbf{cg}")
                        P.op("dve", lambda en: en.tensor_copy(HV_bf[:], HV_f[:]),
                             reads=[bHVf], writes=[bHVbf], tag=f"hvbf{cg}")
                        for h in range(HA):
                            P.op("pe", lambda en, h=h: en.transpose(
                                ps1[:, h * 128:(h + 1) * 128], o_g[:, h * 128:(h + 1) * 128], identf[:]),
                                reads=[bog, bconsts["identf"]],
                                writes=[pb1[0], pb1[1]] if h == 0 else (),
                                update=[pb1[0], pb1[1]] if h == HA - 1 else (),
                                tag=f"trO{cg}_{h}")
                        P.op("dve", lambda en, tc4=tc4: en.tensor_copy(
                            og_fm.ap().rearrange("p (h t) -> p h t", h=HA)[:, :, tc4 * 128:(tc4 + 1) * 128],
                            ps1.ap().rearrange("p (h t) -> p h t", h=16)[:, 0:8, :]),
                            reads=[pb1[0], pb1[1]],
                            writes=[bogfm] if tc4 == 0 else (), update=[bogfm] if tc4 == 3 else (),
                            tag=f"ogfm{cg}")
                    for half in range(2):
                        for hh_ in range(4):
                            hglob = half * 4 + hh_
                            P.op("pool", lambda en, hglob=hglob, hh_=hh_: en.dma_start(
                                out=wos[:, hh_ * 2048:(hh_ + 1) * 2048],
                                in_=din["wo_a"][hglob * 128:(hglob + 1) * 128, :]),
                                writes=[bwos[hh_]], dma=True, tag=f"ldwo{w}_{half}_{hh_}")
                        for hh_ in range(4):
                            P.op("dve", lambda en, hh_=hh_: en.tensor_scalar_mul(
                                wos[:, hh_ * 2048:(hh_ + 1) * 2048], wos[:, hh_ * 2048:(hh_ + 1) * 2048], gnw[:]),
                                reads=[bconsts["gnw"]], writes=[bwos[hh_]], tag=f"gnwf{w}_{half}_{hh_}")
                        for dc in range(16):
                            reg = ps2[:, (dc % 2) * 512:(dc % 2) * 512 + 512]
                            rb = pb2[dc % 2]
                            for hh_ in range(4):
                                hglob = half * 4 + hh_
                                P.op("pe", lambda en, dc=dc, hh_=hh_, hglob=hglob, reg=reg: en.matmul(
                                    reg, wos[:, hh_ * 2048 + dc * 128:hh_ * 2048 + (dc + 1) * 128],
                                    og_fm[:, hglob * 512:(hglob + 1) * 512], start=(hh_ == 0), stop=(hh_ == 3)),
                                    reads=[bwos[hh_], bogfm],
                                    writes=[rb] if hh_ == 0 else (), update=[rb] if hh_ == 3 else (),
                                    tag=f"wo{w}_{half}_{dc}_{hh_}")
                            sreg = stg[:]
                            if half == 0:
                                P.op("act", lambda en, reg=reg, sreg=sreg: en.activation(out=sreg, in_=reg, func=AF.Identity),
                                     reads=[rb], writes=[bstg[0]], tag=f"woev{w}_{half}_{dc}")
                                pt_toks[w % 2].append(P.op("sp", lambda en, w=w, dc=dc, sreg=sreg: en.dma_start(
                                    out=ag_pt_in2[w % 2][(w // 2) * D + dc * 128:(w // 2) * D + (dc + 1) * 128, :],
                                    in_=sreg),
                                    reads=[bstg[0]], writes=[bag_ptr[w][dc]], dma=True, tag=f"wodma{w}_{half}_{dc}"))
                            else:
                                P.op("act", lambda en, reg=reg, sreg=sreg: en.activation(out=sreg, in_=reg, func=AF.Identity),
                                     reads=[rb], writes=[bstg[0]], tag=f"woev{w}_{half}_{dc}")
                                pt_toks[w % 2].append(P.op("pool", lambda en, w=w, dc=dc, sreg=sreg: en.dma_start(
                                    out=ag_pt_in2[w % 2][(w // 2) * D + dc * 128:(w // 2) * D + (dc + 1) * 128, :],
                                    in_=sreg, accum_op=ALU.add),
                                    reads=[bstg[0]], writes=[bag_ptr[w][dc]], dma=True, tag=f"wodma{w}_{half}_{dc}"))
                    if w == 2:
                        emit_rs(0, "ag_pt0")

                P.fence([bqw, bkw, bvw, bgw, bEf, bEbf, bkf32, bDs, bRt, bokh, bpv, bph,
                         bPtm, bktm, bSTm, bWtm, bHKf, bHKbf, bHVf, bHVbf, bcarry, bones_s,
                         bsigf, bo2, bog, bogfm, bstg[0], bwos[0], bwos[1], bwos[2], bwos[3],
                         bsch[0], bzst, bzneg] + bwqk + bwvg + bxnw, tag="a2f")
            emit_rs(1, "ag_pt1")
            with ExitStack() as rctx:
                re_ = rctx.enter_context
                rb0 = re_(nc.sbuf_tensor("a_rb0", [128, 2 * TC], F32))
                brb0 = [[P.buf(f"rb0{i}_{ch}") for ch in range(2)] for i in range(2)]
                for c in range(NCH):
                    par = c % 2
                    for colh in range(2):
                        r0v = rb0[:, par * TC + colh * 512:par * TC + colh * 512 + 512]
                        P.op("sp" if colh == 0 else "pool", lambda en, c=c, colh=colh, r0v=r0v: en.dma_start(
                            out=r0v, in_=ag_pt_out2[colh][c * 128:(c + 1) * 128, :]),
                            reads=[bag_pt_out2[colh]], writes=[brb0[par][colh]], dma=True, tag=f"rb0_{c}_{colh}")
                        sl = slice(c * TC + colh * 512, c * TC + colh * 512 + 512)
                        P.op("dve", lambda en, sl=sl, r0v=r0v: en.tensor_add(x_res[:, sl], x_res[:, sl], r0v),
                             reads=[brb0[par][colh]], writes=[bxc[c]], tag=f"res0_{c}_{colh}")
                P.fence([b for row in brb0 for b in row], tag="rbf")

        def ffn_sublayer(tag):
            with ExitStack() as fctx:
                fe = fctx.enter_context
                if conv_state["i"] < len(conv_jobs) and conv_state["stg"] is None:
                    conv_state["stg"] = fe(nc.sbuf_tensor(tag + "_cstg", [128, 2 * 1024], BF16))

                def wsrc3(name, par):
                    if conv_done(name):
                        return dcache[name], ("sp" if par == 0 else "pool"), [bcache[name]]
                    return din[name], "pool", []
                xn_f = fe(nc.sbuf_tensor(tag + "_xn", [128, NCH * TC], BF16))
                xn1_f = fe(nc.sbuf_tensor(tag + "_xn1", [128, NCH * TC], BF16))
                bxnf = [P.buf(tag + f"xnf{c}") for c in range(NCH)]
                bxn1 = [P.buf(tag + f"xn1{c}") for c in range(NCH)]
                layernorm(tag + "ln", xn_f, bxnf)
                wg8 = fe(nc.sbuf_tensor(tag + "_wg8", [128, 256], BF16))
                bgc = fe(nc.sbuf_tensor(tag + "_bg", [8, 2], F32))
                bupc = fe(nc.sbuf_tensor(tag + "_bup", [128, 48], F32))
                bswc = fe(nc.sbuf_tensor(tag + "_bsw", [128, 48], F32))
                bsac = fe(nc.sbuf_tensor(tag + "_bsa", [128, 8], F32))
                bsuc = fe(nc.sbuf_tensor(tag + "_bsu", [128, 16], F32))
                bdn6 = fe(nc.sbuf_tensor(tag + "_bdn6", [6, 512], BF16))
                bsd1 = fe(nc.sbuf_tensor(tag + "_bsd1", [1, 512], BF16))
                ones512 = fe(nc.sbuf_tensor(tag + "_ones512", [1, 512], BF16))
                g8r = fe(nc.sbuf_tensor(tag + "_g8r", [8, 512], F32))
                gtm = fe(nc.sbuf_tensor(tag + "_gtm", [128, 8], F32))
                gsc = fe(nc.sbuf_tensor(tag + "_gsc", [128, 112], F32))
                w_fm = fe(nc.sbuf_tensor(tag + "_wfm", [8, TC], F32))
                w_bf = fe(nc.sbuf_tensor(tag + "_wbf", [8, TC], BF16))
                wb_n = fe(nc.sbuf_tensor(tag + "_wbn", [128, TC], BF16))
                u_n = fe(nc.sbuf_tensor(tag + "_un", [128, 8 * TC], BF16))
                silu_t = fe(nc.sbuf_tensor(tag + "_silu", [128, 512], BF16))
                t2b = fe(nc.sbuf_tensor(tag + "_t2", [128, 512], BF16))
                hst = fe(nc.sbuf_tensor(tag + "_hst", [128, 2 * 512], BF16))
                wups = fe(nc.sbuf_tensor(tag + "_wus", [128, 2 * 2048], BF16))
                wsws = wsus = wsas = wups
                wdns = fe(nc.sbuf_tensor(tag + "_wds", [128, 2 * 2048], BF16))
                wsds = wdns
                h_n = fe(nc.sbuf_tensor(tag + "_hn", [128, 8 * TC], BF16))
                shl = fe(nc.sbuf_tensor(tag + "_shl", [128, 2 * TC], BF16))
                xcs = fe(nc.sbuf_tensor(tag + "_xcs", [128, TC], F32)) if False else None

                bwg8, bbg = P.buf(tag + "wg8"), P.buf(tag + "bg")
                bbup, bbsw, bbsa, bbsu = P.buf(tag + "bup"), P.buf(tag + "bsw"), P.buf(tag + "bsa"), P.buf(tag + "bsu")
                bbdn6, bbsd1, bones512 = P.buf(tag + "bdn6"), P.buf(tag + "bsd1"), P.buf(tag + "o512")
                bg8r, bgtm, bgsc = P.buf(tag + "g8r"), P.buf(tag + "gtm"), P.buf(tag + "gsc")
                bwfm, bwbf, bwbn = P.buf(tag + "wfm"), P.buf(tag + "wbf"), P.buf(tag + "wbn")
                bun, bsilu, bt2, bhst = P.buf(tag + "un"), P.buf(tag + "silu"), P.buf(tag + "t2"), P.buf(tag + "hst")
                bwup = [P.buf(tag + "wus0"), P.buf(tag + "wus1")]
                bwsw = bwsu = bwsa = bwup
                bwdn = [P.buf(tag + "wds0"), P.buf(tag + "wds1")]
                bwsd = bwdn
                bhn = [P.buf(tag + "hn0"), P.buf(tag + "hn1")]
                bshl = [P.buf(tag + "shl0"), P.buf(tag + "shl1")]
                us_ctr = [0]
                ds_ctr = [0]
                bhsp = [[P.buf(tag + f"hsp{n}_{f}") for f in range(8)] for n in range(6)]
                bssp = [P.buf(tag + f"ssp{i}") for i in range(16)]

                P.op("pool", lambda en: en.dma_start(out=wg8[:], in_=din["f_wg_t"]), writes=[bwg8], dma=True)
                P.op("sp", lambda en: en.dma_start(out=bgc[:], in_=din["f_bg_t"]), writes=[bbg], dma=True)
                P.op("sp", lambda en: en.dma_start(out=bupc[:], in_=din["f_bup_t"]), writes=[bbup], dma=True)
                P.op("sp", lambda en: en.dma_start(out=bswc[:], in_=din["f_bsw_t"]), writes=[bbsw], dma=True)
                P.op("sp", lambda en: en.dma_start(out=bsac[:], in_=din["f_bsa_t"]), writes=[bbsa], dma=True)
                P.op("sp", lambda en: en.dma_start(out=bsuc[:], in_=din["f_bsu_t"]), writes=[bbsu], dma=True)

                P.op("dve", lambda en: en.memset(ones512[:], 1.0), writes=[bones512], tag=tag + "o512")

                for j in range(2):
                    xin = xn_f if j == 0 else xn1_f
                    bxin = bxnf if j == 0 else bxn1
                    for hf in range(2):
                        reg = ps1[0:8, hf * 512:(hf + 1) * 512]
                        for c in range(NCH):
                            P.op("pe", lambda en, j=j, hf=hf, c=c, reg=reg, xin=xin: en.matmul(
                                reg, wg8[:, j * 128 + c * 8:j * 128 + (c + 1) * 8],
                                xin[:, c * TC + hf * 512:c * TC + hf * 512 + 512],
                                start=(c == 0), stop=(c == NCH - 1)),
                                reads=[bwg8, bxin[c]],
                                writes=[pb1[hf]] if c == 0 else (), update=[pb1[hf]] if c == NCH - 1 else (),
                                tag=f"{tag}g{j}_{hf}_{c}")
                        P.op("act", lambda en, j=j, hf=hf, reg=reg: en.activation(
                            out=g8r[:], in_=reg, func=AF.Identity,
                            bias=bgc[:, j:j + 1]),
                            reads=[pb1[hf], bbg], writes=[bg8r],
                            tag=f"{tag}gev{j}_{hf}")
                        for sl4 in range(4):
                            sl8 = hf * 4 + sl4
                            P.op("pe", lambda en, sl4=sl4, sl8=sl8: en.transpose(
                                ps1[:, 1024 + sl8 * 8:1024 + (sl8 + 1) * 8],
                                g8r[0:8, sl4 * 128:(sl4 + 1) * 128], identf[0:8, 0:8]),
                                reads=[bg8r, bconsts["identf"]],
                                writes=[pb1[2]] if sl8 == 0 else (), update=[pb1[2]] if sl8 == 7 else (),
                                tag=f"{tag}gt{j}_{sl8}")
                    for sl8 in range(8):
                        P.op("dve", lambda en, sl8=sl8: en.tensor_copy(gtm[:], ps1[:, 1024 + sl8 * 8:1024 + (sl8 + 1) * 8]),
                             reads=[pb1[2]], writes=[bgtm], tag=f"{tag}gc{j}_{sl8}")
                        P.op("dve", lambda en: en.tensor_reduce(gsc[:, 0:1], gtm[:], axis=mybir.AxisListType.X, op=ALU.min),
                             reads=[bgtm], writes=[bgsc], tag=f"{tag}m1{j}_{sl8}")
                        P.op("dve", lambda en: en.tensor_single_scalar(gsc[:, 8:16], gtm[:], gsc[:, 0:1], ALU.is_equal),
                             reads=[bgtm, bgsc], update=[bgsc], tag=f"{tag}eq{j}_{sl8}")
                        P.op("dve", lambda en: en.scalar_tensor_tensor(
                            out=gsc[:, 16:24], in0=gsc[:, 8:16], scalar=1e30, in1=gtm[:], op0=ALU.mult, op1=ALU.add),
                            reads=[bgsc, bgtm], update=[bgsc], tag=f"{tag}gb{j}_{sl8}")
                        P.op("dve", lambda en: en.tensor_reduce(gsc[:, 1:2], gsc[:, 16:24], axis=mybir.AxisListType.X, op=ALU.min),
                             reads=[bgsc], update=[bgsc], tag=f"{tag}m2{j}_{sl8}")
                        P.op("dve", lambda en: en.tensor_single_scalar(gsc[:, 8:16], gtm[:], gsc[:, 1:2], ALU.is_gt),
                             reads=[bgtm, bgsc], update=[bgsc], tag=f"{tag}kp{j}_{sl8}")
                        P.op("dve", lambda en: en.tensor_scalar(
                            out=gsc[:, 16:24], in0=gsc[:, 8:16], scalar1=1.0, scalar2=60.0,
                            op0=ALU.subtract, op1=ALU.mult),
                            reads=[bgsc], update=[bgsc], tag=f"{tag}pn{j}_{sl8}")
                        P.op("dve", lambda en: en.tensor_add(gsc[:, 24:32], gtm[:], gsc[:, 16:24]),
                             reads=[bgtm, bgsc], update=[bgsc], tag=f"{tag}gl{j}_{sl8}")
                        P.op("dve", lambda en: en.reduce_max(gsc[:, 2:3], gsc[:, 24:32], axis=mybir.AxisListType.X),
                             reads=[bgsc], update=[bgsc], tag=f"{tag}rm{j}_{sl8}")
                        P.op("dve", lambda en: en.tensor_scalar_mul(gsc[:, 3:4], gsc[:, 2:3], -1.0),
                             reads=[bgsc], update=[bgsc], tag=f"{tag}rn{j}_{sl8}")
                        P.op("act", lambda en: en.activation(out=gsc[:, 32:40], in_=gsc[:, 24:32], func=AF.Exp,
                                                             bias=gsc[:, 3:4], accum_out=gsc[:, 4:5]),
                             reads=[bgsc], update=[bgsc], tag=f"{tag}ex{j}_{sl8}")
                        P.op("dve", lambda en: en.reciprocal(gsc[:, 5:6], gsc[:, 4:5]),
                             reads=[bgsc], update=[bgsc], tag=f"{tag}rc{j}_{sl8}")
                        P.op("dve", lambda en, sl8=sl8: en.tensor_single_scalar(
                            gsc[:, 48 + 8 * sl8:56 + 8 * sl8], gsc[:, 32:40], gsc[:, 5:6], ALU.mult),
                            reads=[bgsc], update=[bgsc], tag=f"{tag}wm{j}_{sl8}")

                    def gate_tail(sl8, j=j):
                        sl4 = sl8 % 4
                        P.op("pe", lambda en, sl4=sl4, sl8=sl8: en.transpose(
                            ps1[0:8, 1536 + sl4 * 128:1536 + (sl4 + 1) * 128], gsc[:, 48 + 8 * sl8:56 + 8 * sl8], identf[:]),
                            reads=[bgsc, bconsts["identf"]],
                            writes=[pb1[3]], tag=f"{tag}gbt{j}_{sl8}")
                        P.op("dve", lambda en, sl4=sl4, sl8=sl8: en.tensor_copy(
                            w_fm[:, sl8 * 128:(sl8 + 1) * 128], ps1[0:8, 1536 + sl4 * 128:1536 + (sl4 + 1) * 128]),
                            reads=[pb1[3]],
                            writes=[bwfm] if sl8 == 0 else (), update=[bwfm] if sl8 == 7 else (),
                            tag=f"{tag}wfm{j}_{sl8}")
                        if sl8 == 7:
                            P.op("dve", lambda en: en.tensor_copy(w_bf[:], w_fm[:]),
                                 reads=[bwfm], writes=[bwbf], tag=f"{tag}wbf{j}")
                    for s_ in range(2):
                        for sci in range(8):
                            sc = s_ * 8 + sci
                            reg2 = ps2[:, (sci % 2) * 1024:(sci % 2) * 1024 + 512]
                            reg2b = ps2[:, (sci % 2) * 1024 + 512:(sci % 2) * 1024 + 1024]
                            rbs = [pb2[(sci % 2) * 2], pb2[(sci % 2) * 2 + 1]]
                            sp_ = us_ctr[0] % 2
                            us_ctr[0] += 1
                            wsrc, wq, wrd = wsrc3("f_Wsu_t2", sp_)
                            P.op(wq, lambda en, sc=sc, sp_=sp_, wsrc=wsrc: en.dma_start(
                                out=wsus[:, sp_ * 2048:(sp_ + 1) * 2048],
                                in_=wsrc[:, sc * 2048:(sc + 1) * 2048]),
                                reads=wrd, writes=[bwsu[sp_]], dma=True, tag=f"{tag}wsu{j}_{sc}")
                            feed_conv(2)
                            for tf in range(2):
                                reg = reg2 if tf == 0 else reg2b
                                for c in range(NCH):
                                    P.op("pe", lambda en, c=c, tf=tf, sp_=sp_, reg=reg, xin=xin: en.matmul(
                                        reg, wsus[:, sp_ * 2048 + c * 128:sp_ * 2048 + (c + 1) * 128],
                                        xin[:, c * TC + tf * 512:c * TC + tf * 512 + 512],
                                        start=(c == 0), stop=(c == NCH - 1)),
                                        reads=[bwsu[sp_], bxin[c]],
                                        writes=[rbs[tf]] if c == 0 else (), update=[rbs[tf]] if c == NCH - 1 else (),
                                        tag=f"{tag}su{j}_{sc}_{tf}_{c}")
                                P.op("act", lambda en, sci=sci, tf=tf, sc=sc, reg=reg: en.activation(
                                    out=u_n[:, sci * TC + tf * 512:sci * TC + tf * 512 + 512], in_=reg,
                                    func=AF.Identity, bias=bsuc[:, sc:sc + 1]),
                                    reads=[rbs[tf], bbsu],
                                    writes=[bun] if (sci == 0 and tf == 0) else (), update=[bun],
                                    tag=f"{tag}suev{j}_{sc}_{tf}")
                            if sc < 8:
                                gate_tail(sc)
                        for fch in range(8):
                            reg2 = ps1[:, (fch % 2) * 1024:(fch % 2) * 1024 + 512]
                            reg2b = ps1[:, (fch % 2) * 1024 + 512:(fch % 2) * 1024 + 1024]
                            rbs = [pb1[(fch % 2) * 2], pb1[(fch % 2) * 2 + 1]]
                            fp_ = us_ctr[0] % 2
                            us_ctr[0] += 1
                            wsrc, wq, wrd = wsrc3("f_Wsa_t2", fp_)
                            P.op(wq, lambda en, fch=fch, fp_=fp_, wsrc=wsrc: en.dma_start(
                                out=wsas[:, fp_ * 2048:fp_ * 2048 + 1024],
                                in_=wsrc[:, fch * 1024:(fch + 1) * 1024]),
                                reads=wrd, writes=[bwsa[fp_]], dma=True, tag=f"{tag}wsa{j}_{s_}_{fch}")
                            feed_conv(2)
                            for tf in range(2):
                                reg = reg2 if tf == 0 else reg2b
                                for ec in range(8):
                                    P.op("pe", lambda en, ec=ec, tf=tf, fp_=fp_, reg=reg: en.matmul(
                                        reg, wsas[:, fp_ * 2048 + ec * 128:fp_ * 2048 + (ec + 1) * 128],
                                        u_n[:, ec * TC + tf * 512:ec * TC + tf * 512 + 512],
                                        start=(ec == 0), stop=(ec == 7)),
                                        reads=[bwsa[fp_], bun],
                                        writes=[rbs[tf]] if ec == 0 else (), update=[rbs[tf]] if ec == 7 else (),
                                        tag=f"{tag}sa{j}_{s_}_{fch}_{tf}_{ec}")
                                P.op("act", lambda en, fch=fch, tf=tf, reg=reg: en.activation(
                                    out=silu_t[:], in_=reg, func=AF.Silu, bias=bsac[:, fch:fch + 1]),
                                    reads=[rbs[tf], bbsa], writes=[bsilu], tag=f"{tag}sas{j}_{s_}_{fch}_{tf}")
                                P.op("dve", lambda en, fch=fch, tf=tf: en.tensor_mul(
                                    hst[:, tf * 512:(tf + 1) * 512], silu_t[:],
                                    u_n[:, fch * TC + tf * 512:fch * TC + tf * 512 + 512]),
                                    reads=[bsilu, bun], writes=[bhst], tag=f"{tag}sam{j}_{s_}_{fch}_{tf}")
                            sc = s_ * 8 + fch
                            P.op("sp", lambda en, sc=sc: en.dma_start(
                                out=sh_spill[:, sc * TC:(sc + 1) * TC], in_=hst[:]),
                                reads=[bhst], writes=[bssp[sc]], dma=True, tag=f"{tag}shsp{j}_{sc}")
                    for n in range(6):
                        for tf in range(2):
                            P.op("pe", lambda en, n=n, tf=tf: en.matmul(
                                ps1[:, 1024 + tf * 512:1024 + (tf + 1) * 512],
                                bc8t[:, n * 128:(n + 1) * 128],
                                w_bf[0:8, tf * 512:(tf + 1) * 512], start=True, stop=True),
                                reads=[bwbf, bconsts["bc8"]],
                                writes=[pb1[2 + tf]], tag=f"{tag}wb{j}_{n}_{tf}")
                            P.op("dve", lambda en, tf=tf: en.tensor_copy(
                                wb_n[:, tf * 512:(tf + 1) * 512], ps1[:, 1024 + tf * 512:1024 + (tf + 1) * 512]),
                                reads=[pb1[2 + tf]],
                                writes=[bwbn] if tf == 0 else (), update=[bwbn] if tf else (),
                                tag=f"{tag}wbc{j}_{n}_{tf}")
                        for ech in range(8):
                            reg2 = ps2[:, (ech % 2) * 1024:(ech % 2) * 1024 + 512]
                            reg2b = ps2[:, (ech % 2) * 1024 + 512:(ech % 2) * 1024 + 1024]
                            rbs = [pb2[(ech % 2) * 2], pb2[(ech % 2) * 2 + 1]]
                            up_ = us_ctr[0] % 2
                            us_ctr[0] += 1
                            wsrc, wq, wrd = wsrc3("f_Wup_t", up_)
                            P.op(wq, lambda en, n=n, ech=ech, up_=up_, wsrc=wsrc: en.dma_start(
                                out=wups[:, up_ * 2048:(up_ + 1) * 2048],
                                in_=wsrc[:, (n * 8 + ech) * 2048:(n * 8 + ech + 1) * 2048]),
                                reads=wrd, writes=[bwup[up_]], dma=True, tag=f"{tag}wup{j}_{n}_{ech}")
                            feed_conv(2)
                            for tf in range(2):
                                reg = reg2 if tf == 0 else reg2b
                                for c in range(NCH):
                                    P.op("pe", lambda en, c=c, tf=tf, up_=up_, reg=reg, xin=xin: en.matmul(
                                        reg, wups[:, up_ * 2048 + c * 128:up_ * 2048 + (c + 1) * 128],
                                        xin[:, c * TC + tf * 512:c * TC + tf * 512 + 512],
                                        start=(c == 0), stop=(c == NCH - 1)),
                                        reads=[bwup[up_], bxin[c]],
                                        writes=[rbs[tf]] if c == 0 else (), update=[rbs[tf]] if c == NCH - 1 else (),
                                        tag=f"{tag}up{j}_{n}_{ech}_{tf}_{c}")
                                P.op("act", lambda en, ech=ech, tf=tf, n=n, reg=reg: en.activation(
                                    out=u_n[:, ech * TC + tf * 512:ech * TC + tf * 512 + 512], in_=reg,
                                    func=AF.Identity, bias=bupc[:, n * 8 + ech:n * 8 + ech + 1]),
                                    reads=[rbs[tf], bbup],
                                    writes=[bun] if (ech == 0 and tf == 0) else (), update=[bun],
                                    tag=f"{tag}upev{j}_{n}_{ech}_{tf}")
                        for fch in range(8):
                            reg2 = ps1[:, (fch % 2) * 1024:(fch % 2) * 1024 + 512]
                            reg2b = ps1[:, (fch % 2) * 1024 + 512:(fch % 2) * 1024 + 1024]
                            rbs = [pb1[(fch % 2) * 2], pb1[(fch % 2) * 2 + 1]]
                            fp_ = us_ctr[0] % 2
                            us_ctr[0] += 1
                            wsrc, wq, wrd = wsrc3("f_Wsw_t", fp_)
                            P.op(wq, lambda en, n=n, fch=fch, fp_=fp_, wsrc=wsrc: en.dma_start(
                                out=wsws[:, fp_ * 2048:fp_ * 2048 + 1024],
                                in_=wsrc[:, (n * 8 + fch) * 1024:(n * 8 + fch + 1) * 1024]),
                                reads=wrd, writes=[bwsw[fp_]], dma=True, tag=f"{tag}wswl{j}_{n}_{fch}")
                            feed_conv(2)
                            for tf in range(2):
                                reg = reg2 if tf == 0 else reg2b
                                for ec in range(8):
                                    P.op("pe", lambda en, ec=ec, tf=tf, fp_=fp_, reg=reg: en.matmul(
                                        reg, wsws[:, fp_ * 2048 + ec * 128:fp_ * 2048 + (ec + 1) * 128],
                                        u_n[:, ec * TC + tf * 512:ec * TC + tf * 512 + 512],
                                        start=(ec == 0), stop=(ec == 7)),
                                        reads=[bwsw[fp_], bun],
                                        writes=[rbs[tf]] if ec == 0 else (), update=[rbs[tf]] if ec == 7 else (),
                                        tag=f"{tag}sw{j}_{n}_{fch}_{tf}_{ec}")
                                P.op("act", lambda en, n=n, fch=fch, tf=tf, reg=reg: en.activation(
                                    out=silu_t[:], in_=reg, func=AF.Silu, bias=bswc[:, n * 8 + fch:n * 8 + fch + 1]),
                                    reads=[rbs[tf], bbsw], writes=[bsilu], tag=f"{tag}sws{j}_{n}_{fch}_{tf}")
                                P.op("dve", lambda en, fch=fch, tf=tf: en.tensor_mul(
                                    t2b[:], silu_t[:],
                                    u_n[:, fch * TC + tf * 512:fch * TC + tf * 512 + 512]),
                                    reads=[bsilu, bun], writes=[bt2], tag=f"{tag}swm{j}_{n}_{fch}_{tf}")
                                P.op("dve", lambda en, tf=tf: en.tensor_mul(
                                    hst[:, tf * 512:(tf + 1) * 512], t2b[:], wb_n[:, tf * 512:(tf + 1) * 512]),
                                    reads=[bt2, bwbn], writes=[bhst], tag=f"{tag}swh{j}_{n}_{fch}_{tf}")
                            P.op("sp", lambda en, n=n, fch=fch: en.dma_start(
                                out=h_spill[:, (n * 8 + fch) * TC:(n * 8 + fch + 1) * TC], in_=hst[:]),
                                reads=[bhst], writes=[bhsp[n][fch]], dma=True, tag=f"{tag}hsp{j}_{n}_{fch}")
                    for dcg in range(4):
                        P.op("pool", lambda en, dcg=dcg: en.dma_start(
                            out=bdn6[:], in_=din["f_bdn"][:, dcg * 512:(dcg + 1) * 512]),
                            writes=[bbdn6], dma=True, tag=f"{tag}bdnl{j}_{dcg}")
                        P.op("pool", lambda en, dcg=dcg: en.dma_start(
                            out=bsd1[:], in_=din["f_bsd"].rearrange("d -> () d")[:, dcg * 512:(dcg + 1) * 512]),
                            writes=[bbsd1], dma=True, tag=f"{tag}bsdl{j}_{dcg}")
                        regs = {}
                        for dci in range(4):
                            for tf in range(2):
                                ridx = dci * 2 + tf
                                regs[(dci, tf)] = ((ps1 if ridx < 4 else ps2)[:, (ridx % 4) * 512:(ridx % 4 + 1) * 512],
                                                  (pb1 if ridx < 4 else pb2)[ridx % 4])
                        first_mm = {k: True for k in regs}
                        for dci in range(4):
                            dc = dcg * 4 + dci
                            for tf in range(2):
                                reg, rb = regs[(dci, tf)]
                                P.op("pe", lambda en, dci=dci, tf=tf, reg=reg: en.matmul(
                                    reg, bsd1[0:1, dci * 128:(dci + 1) * 128], ones512[0:1, :],
                                    start=True, stop=False),
                                    reads=[bbsd1, bones512],
                                    writes=[rb], tag=f"{tag}bsd{j}_{dc}_{tf}")
                                P.op("pe", lambda en, dci=dci, tf=tf, reg=reg: en.matmul(
                                    reg, bdn6[:, dci * 128:(dci + 1) * 128],
                                    w_bf[0:6, tf * 512:(tf + 1) * 512], start=False, stop=False),
                                    reads=[bbdn6, bwbf], tag=f"{tag}bdn{j}_{dc}_{tf}")
                        for n in range(6):
                            for ecg in range(2):
                                hp = (n * 2 + ecg) % 2
                                P.op("sp", lambda en, n=n, ecg=ecg, hp=hp: en.dma_start(
                                    out=h_n[:, hp * 4 * TC:(hp + 1) * 4 * TC],
                                    in_=h_spill[:, (n * 8 + ecg * 4) * TC:(n * 8 + ecg * 4 + 4) * TC]),
                                    reads=[bhsp[n][f] for f in range(ecg * 4, ecg * 4 + 4)],
                                    writes=[bhn[hp]], dma=True, tag=f"{tag}hld{j}_{dcg}_{n}_{ecg}")
                                wp_ = ds_ctr[0] % 2
                                ds_ctr[0] += 1
                                wsrc, wq, wrd = wsrc3("f_Wdn_t", wp_)
                                P.op(wq, lambda en, n=n, dcg=dcg, ecg=ecg, wp_=wp_, wsrc=wsrc: en.dma_start(
                                    out=wdns[:, wp_ * 2048:(wp_ + 1) * 2048],
                                    in_=wsrc[:, ((n * 2 + ecg) * 4 + dcg) * 2048:((n * 2 + ecg) * 4 + dcg + 1) * 2048]),
                                    reads=wrd, writes=[bwdn[wp_]], dma=True, tag=f"{tag}wdnl{j}_{dcg}_{n}_{ecg}")
                                feed_conv(2)
                                for dci in range(4):
                                    for tf in range(2):
                                        reg, rb = regs[(dci, tf)]
                                        for eci in range(4):
                                            P.op("pe", lambda en, dci=dci, tf=tf, eci=eci, hp=hp, wp_=wp_, reg=reg: en.matmul(
                                                reg, wdns[:, wp_ * 2048 + eci * 512 + dci * 128:wp_ * 2048 + eci * 512 + dci * 128 + 128],
                                                h_n[:, hp * 4 * TC + eci * TC + tf * 512:hp * 4 * TC + eci * TC + tf * 512 + 512],
                                                start=False, stop=False),
                                                reads=[bwdn[wp_], bhn[hp]], tag=f"{tag}dn{j}_{dcg}_{n}_{dci}_{tf}_{eci}")

                        for scp in range(8):
                            wp = ds_ctr[0] % 2
                            ds_ctr[0] += 1
                            wsrc, wq, wrd = wsrc3("f_Wsd_t2", wp)
                            P.op(wq, lambda en, dcg=dcg, scp=scp, wp=wp, wsrc=wsrc: en.dma_start(
                                out=wsds[:, wp * 2048:wp * 2048 + 1024],
                                in_=wsrc[:, (dcg * 8 + scp) * 1024:(dcg * 8 + scp + 1) * 1024]),
                                reads=wrd, writes=[bwsd[wp]], dma=True, tag=f"{tag}wsdl{j}_{dcg}_{scp}")
                            feed_conv(2)
                            for sci in range(2):
                                sc = scp * 2 + sci
                                sp_ = sc % 2
                                P.op("sp", lambda en, sc=sc, sp_=sp_: en.dma_start(
                                    out=shl[:, sp_ * TC:(sp_ + 1) * TC],
                                    in_=sh_spill[:, sc * TC:(sc + 1) * TC]),
                                    reads=[bssp[sc]], writes=[bshl[sp_]], dma=True,
                                    tag=f"{tag}shl{j}_{dcg}_{sc}")
                                for dci in range(4):
                                    for tf in range(2):
                                        reg, rb = regs[(dci, tf)]
                                        P.op("pe", lambda en, sci=sci, sp_=sp_, wp=wp, dci=dci, tf=tf, sc=sc, reg=reg: en.matmul(
                                            reg, wsds[:, wp * 2048 + sci * 512 + dci * 128:wp * 2048 + sci * 512 + dci * 128 + 128],
                                            shl[:, sp_ * TC + tf * 512:sp_ * TC + tf * 512 + 512],
                                            start=False, stop=(sc == 15)),
                                            reads=[bwsd[wp], bshl[sp_]],
                                            update=[regs[(dci, tf)][1]] if sc == 15 else (),
                                            tag=f"{tag}sd{j}_{dcg}_{sc}_{dci}_{tf}")
                        for dci in range(4):
                            dc = dcg * 4 + dci
                            for tf in range(2):
                                reg, rb = regs[(dci, tf)]
                                xs = slice(dc * TC + tf * 512, dc * TC + tf * 512 + 512)
                                if j == 0:
                                    P.op("dve", lambda en, reg=reg, xs=xs: en.tensor_add(
                                        xn1_f[:, xs], reg, xn_f[:, xs]),
                                        reads=[rb, bxnf[dc]],
                                        writes=[bxn1[dc]] if tf == 0 else (), update=[bxn1[dc]] if tf else (),
                                        tag=f"{tag}ev0_{dc}_{tf}")
                                else:
                                    P.op("dve", lambda en, reg=reg, xs=xs: en.tensor_add(
                                        x_res[:, xs], x_res[:, xs], reg),
                                        reads=[rb], writes=[bxc[dc]], tag=f"{tag}ev1a_{dc}_{tf}")
                                    P.op("dve", lambda en, xs=xs: en.tensor_add(
                                        x_res[:, xs], x_res[:, xs], xn_f[:, xs]),
                                        reads=[bxnf[dc]], writes=[bxc[dc]], tag=f"{tag}ev1b_{dc}_{tf}")
                    if j == 0 and "xt0" in dbg_out:
                        P.op("pool", lambda en: en.dma_start(
                            out=dbg_out["xt0"].rearrange("(c p) t -> p c t", p=128),
                            in_=xn1_f.ap().rearrange("p (c t) -> p c t", c=NCH)),
                            reads=bxn1, dma=True, tag="dbg_xt0")
                    if "w" + str(j) in dbg_out:
                        P.op("pool", lambda en, j=j: en.dma_start(
                            out=dbg_out["w" + str(j)], in_=w_fm[:]),
                            reads=[bwfm], dma=True, tag=f"dbg_w{j}")
                    if j == 0 and "sh0" in dbg_out:
                        P.op("pool", lambda en: en.dma_start(out=dbg_out["sh0"], in_=sh_spill),
                             reads=[bssp[i] for i in range(16)], dma=True, tag="dbg_sh0")
                    if j == 0 and "h0" in dbg_out:
                        P.op("pool", lambda en: en.dma_start(out=dbg_out["h0"], in_=h_spill),
                             reads=[b for row in bhsp for b in row], dma=True, tag="dbg_h0")
                feed_conv(len(conv_jobs))
                P.fence([bwg8, bbg, bbup, bbsw, bbsa, bbsu, bbdn6, bbsd1, bones512,
                         bg8r, bgtm, bgsc, bwfm, bwbf, bwbn, bun, bsilu, bt2, bhst]
                        + bwup + bwsw + bwsu + bwsa + bwdn + bwsd + bhn + bshl + bcstg
                        + bxnf + bxn1 + [b for row in bhsp for b in row] + bssp, tag=tag + "f")
                if conv_state["stg"] is not None and conv_state["i"] >= len(conv_jobs):
                    conv_state["stg"] = None

        QSB = 1.0 / (128.0 ** 0.5)

        def attn_b():
            stage_toks = layernorm("ln2", None, None)
            tcc = P.op("pool", lambda en: en.collective_compute(
                "AllGather", ALU.bypass, ins=[ag_xn_in.opt()], outs=[ag_xn_out.opt()],
                replica_groups=PAIRS),
                writes=[bag_xn_out], cc=True, tag="ag_xn_b")
            th = P.thunks["pool"][-1]
            for t in stage_toks:
                P._need(th, t)

            pt_toks = {0: [], 1: []}
            rs_emitted = [False, False]

            def emit_rs(colh, tg):
                rs_emitted[colh] = True
                P.op("pool", lambda en, colh=colh: en.collective_compute(
                    "ReduceScatter", ALU.add, ins=[ag_pt_in2[colh].opt()], outs=[ag_pt_out2[colh].opt()],
                    replica_groups=PAIRS),
                    writes=[bag_pt_out2[colh]], cc=True, tag=tg)
                thpt = P.thunks["pool"][-1]
                for t in pt_toks[colh]:
                    P._need(thpt, t)
            with ExitStack() as bctx:
                be = bctx.enter_context
                xnw = be(nc.sbuf_tensor("b_xnw", [128, NCH * 512], BF16))
                bxnw = [P.buf(f"bxnw{c}") for c in range(NCH)]
                q_fm = be(nc.sbuf_tensor("b_qfm", [128, 8 * T], BF16))
                k_fm = be(nc.sbuf_tensor("b_kfm", [128, 2 * T], BF16))
                v_tm = be(nc.sbuf_tensor("b_vtm", [128, NCH * 256], BF16))
                wslab = be(nc.sbuf_tensor("b_wslab", [128, 2 * NCH * 128], BF16))
                wvb = be(nc.sbuf_tensor("b_wvb", [128, NCH * 256], BF16))
                strip = be(nc.sbuf_tensor("b_strip", [128, 2 * STRIP_W], BF16))
                Ptmp = be(nc.sbuf_tensor("b_Ptmp", [128, 3 * 512], F32))
                Pbf = be(nc.sbuf_tensor("b_Pbf", [128, 3 * 512], BF16))
                sq = be(nc.sbuf_tensor("b_sq", [128, T], BF16))
                nrm = be(nc.sbuf_tensor("b_nrm", [1, 64], F32))
                mneg = be(nc.sbuf_tensor("b_mneg", [128, 8], F32))
                rrow = be(nc.sbuf_tensor("b_rrow", [1, 1024], F32))
                bcst = be(nc.sbuf_tensor("b_bcst", [128, 512], F32))
                o_fm = be(nc.sbuf_tensor("b_ofm", [128, 8 * 512], BF16))
                wob = be(nc.sbuf_tensor("b_wob", [128, 4 * 2048], BF16))
                stg = be(nc.sbuf_tensor("b_stg", [128, 512], F32))

                bq, bk, bv = P.buf("bq"), P.buf("bk"), P.buf("bv")
                bws = [P.buf("bws0"), P.buf("bws1")]
                bwvb = P.buf("bwvb")
                bstrip = [P.buf("bstrip0"), P.buf("bstrip1")]
                bPt = [P.buf("bPt0"), P.buf("bPt1"), P.buf("bPt2")]
                bPb = [P.buf("bPb0"), P.buf("bPb1"), P.buf("bPb2")]
                bsq, bnrm, bmneg = P.buf("bsq"), P.buf("bnrm"), P.buf("bmneg")
                brr, bbc = P.buf("brr"), P.buf("bbc2")
                bof = P.buf("bof")
                bwob = [P.buf(f"bwob{i}") for i in range(4)]
                bstg2 = P.buf("bstg2")

                for w in range(4):
                    for c in range(NCH):
                        def fn(en, c=c, w=w):
                            return en.dma_start(
                                out=xnw[:, c * 512:(c + 1) * 512],
                                in_=ag_xn_out[(w // 2) * D + c * 128:(w // 2) * D + (c + 1) * 128,
                                              (w % 2) * 512:(w % 2) * 512 + 512])
                        P.op("pool", fn, reads=[bag_xn_out],
                             writes=[bxnw[c]], dma=True, tag=f"bx{w}_{c}")
                    for hl in range(8):
                        par = hl % 2
                        P.op("sp" if par == 0 else "pool", lambda en, hl=hl, par=par: en.dma_start(
                            out=wslab[:, par * 2048:(par + 1) * 2048],
                            in_=dcache["wq_b"][:, hl * 2048:(hl + 1) * 2048]),
                            reads=[bcache["wq_b"]], writes=[bws[par]], dma=True, tag=f"bwq{w}_{hl}")
                        reg = ps2[:, (hl % 4) * 512:(hl % 4 + 1) * 512]
                        rb = pb2[hl % 4]
                        for c in range(NCH):
                            P.op("pe", lambda en, hl=hl, c=c, reg=reg, par=par: en.matmul(
                                reg, wslab[:, par * NCH * 128 + c * 128:par * NCH * 128 + (c + 1) * 128],
                                xnw[:, c * 512:(c + 1) * 512],
                                start=(c == 0), stop=(c == NCH - 1)),
                                reads=[bws[par], bxnw[c]],
                                writes=[rb] if c == 0 else (), update=[rb] if c == NCH - 1 else (),
                                tag=f"bq{w}_{hl}_{c}")
                        P.op("dve", lambda en, hl=hl, w=w, reg=reg: en.tensor_copy(
                            q_fm[:, hl * T + w * 512:hl * T + (w + 1) * 512], reg),
                            reads=[rb],
                            writes=[bq] if (w == 0 and hl == 0) else (), update=[bq],
                            tag=f"bqev{w}_{hl}")
                    for gl in range(2):
                        par = gl % 2
                        P.op("sp" if par == 0 else "pool", lambda en, gl=gl, par=par: en.dma_start(
                            out=wslab[:, par * 2048:(par + 1) * 2048],
                            in_=dcache["wk_b"][:, gl * 2048:(gl + 1) * 2048]),
                            reads=[bcache["wk_b"]], writes=[bws[par]], dma=True, tag=f"bwk{w}_{gl}")
                        reg = ps2[:, gl * 512:(gl + 1) * 512]
                        rb = pb2[gl]
                        for c in range(NCH):
                            P.op("pe", lambda en, gl=gl, c=c, reg=reg, par=par: en.matmul(
                                reg, wslab[:, par * NCH * 128 + c * 128:par * NCH * 128 + (c + 1) * 128],
                                xnw[:, c * 512:(c + 1) * 512],
                                start=(c == 0), stop=(c == NCH - 1)),
                                reads=[bws[par], bxnw[c]],
                                writes=[rb] if c == 0 else (), update=[rb] if c == NCH - 1 else (),
                                tag=f"bk{w}_{gl}_{c}")
                        P.op("dve", lambda en, gl=gl, w=w, reg=reg: en.tensor_copy(
                            k_fm[:, gl * T + w * 512:gl * T + (w + 1) * 512], reg),
                            reads=[rb],
                            writes=[bk] if (w == 0 and gl == 0) else (), update=[bk],
                            tag=f"bkev{w}_{gl}")
                    if w == 0:
                        P.op("pool", lambda en: en.dma_start(
                            out=wvb[:], in_=din["wv_b"]),
                            writes=[bwvb], dma=True, tag=f"bwv{w}")
                    for tc4 in range(4):
                        tcg = w * 4 + tc4
                        reg = ps1[:, (tc4 % 4) * 512:(tc4 % 4) * 512 + 256]
                        rb = pb1[tc4 % 4]
                        for c in range(NCH):
                            P.op("pe", lambda en, c=c, tc4=tc4, reg=reg: en.matmul(
                                reg, xnw[:, c * 512 + tc4 * 128:c * 512 + tc4 * 128 + 128],
                                wvb[:, c * 256:(c + 1) * 256], start=(c == 0), stop=(c == NCH - 1)),
                                reads=[bwvb, bxnw[c]],
                                writes=[rb] if c == 0 else (), update=[rb] if c == NCH - 1 else (),
                                tag=f"bv{w}_{tc4}_{c}")
                        P.op("dve", lambda en, tcg=tcg, reg=reg: en.tensor_copy(
                            v_tm[:, tcg * 256:(tcg + 1) * 256], reg),
                            reads=[rb],
                            writes=[bv] if tcg == 0 else (), update=[bv],
                            tag=f"bvev{w}_{tc4}")
                for hl in range(10):
                    src = q_fm[:, hl * T:(hl + 1) * T] if hl < 8 else k_fm[:, (hl - 8) * T:(hl - 7) * T]
                    rdep = [bq] if hl < 8 else [bk]
                    P.op("act", lambda en, src=src: en.activation(out=sq[:], in_=src, func=AF.Square),
                         reads=rdep, writes=[bsq], tag=f"bsqr{hl}")
                    for qq in range(4):
                        P.op("pe", lambda en, qq=qq: en.matmul(
                            ps1[0:1, 1024:1536],
                            ones_bf[:], sq[:, qq * 512:(qq + 1) * 512],
                            start=True, stop=True),
                            reads=[bsq, bconsts["ones_bf"]],
                            writes=[pb1[2]], tag=f"bnm{hl}_{qq}")
                        P.op("dve", lambda en, hl=hl, qq=qq: en.reduce_max(
                            nrm[0:1, 32 + qq:33 + qq], ps1[0:1, 1024:1536], axis=mybir.AxisListType.X),
                            reads=[pb1[2]],
                            writes=[bnrm] if (hl == 0 and qq == 0) else (), update=[bnrm], tag=f"bnr{hl}_{qq}")
                    P.op("dve", lambda en, hl=hl: en.reduce_max(
                        nrm[0:1, hl:hl + 1], nrm[0:1, 32:36], axis=mybir.AxisListType.X),
                        reads=[bnrm], update=[bnrm], tag=f"bnrf{hl}")
                P.op("dve", lambda en: en.tensor_single_scalar(
                    nrm[0:1, 16:20], nrm[0:1, 0:4], nrm[0:1, 8:9], ALU.mult),
                    reads=[bnrm], update=[bnrm], tag="bmm0")
                P.op("dve", lambda en: en.tensor_single_scalar(
                    nrm[0:1, 20:24], nrm[0:1, 4:8], nrm[0:1, 9:10], ALU.mult),
                    reads=[bnrm], update=[bnrm], tag="bmm1")
                P.op("act", lambda en: en.activation(out=nrm[0:1, 24:32], in_=nrm[0:1, 16:24], func=AF.Ln),
                     reads=[bnrm], update=[bnrm], tag="bln")
                P.op("act", lambda en: en.activation(out=nrm[0:1, 16:24], in_=nrm[0:1, 24:32], func=AF.Exp, scale=0.5),
                     reads=[bnrm], update=[bnrm], tag="bexp")
                P.op("dve", lambda en: en.tensor_scalar_mul(nrm[0:1, 24:32], nrm[0:1, 16:24], -QSB),
                     reads=[bnrm], update=[bnrm], tag="bneg")
                P.op("pe", lambda en: en.matmul(
                    ps1[:, 1024:1032], ones_f[:], nrm[0:1, 24:32], start=True, stop=True),
                    reads=[bnrm, bconsts["ones_f"]], writes=[pb1[2]], tag="bmnbc")
                P.op("dve", lambda en: en.tensor_copy(mneg[:], ps1[:, 1024:1032]),
                     reads=[pb1[2]], writes=[bmneg], tag="bmnc")
                def emit_strip(qb, hl):
                    sp_ = hl % 2
                    P.op("sp", lambda en, hl=hl, sp_=sp_: en.dma_start(
                        out=strip[:, sp_ * STRIP_W:(sp_ + 1) * STRIP_W],
                        in_=din["strips"][hl]),
                        writes=[bstrip[sp_]], dma=True, tag=f"bstl{qb}_{hl}")

                SLOT = [0, 512, 1536]
                SBUF3 = [pb1[0], pb1[1], pb1[3]]

                for qb in range(4):
                    nkb = (qb + 1) * 4
                    NG = 8 * nkb

                    def S_of(g, qb=qb, nkb=nkb):
                        hl, kb = g // nkb, g % nkb
                        sl = SLOT[g % 3]
                        P.op("pe", lambda en, hl=hl, kb=kb, sl=sl, qb=qb: en.matmul(
                            ps1[:, sl:sl + 512],
                            k_fm[:, (hl // 4) * T + kb * 128:(hl // 4) * T + kb * 128 + 128],
                            q_fm[:, hl * T + qb * 512:hl * T + (qb + 1) * 512],
                            start=True, stop=True),
                            reads=[bk, bq], writes=[SBUF3[g % 3]], tag=f"bS{qb}_{hl}_{kb}")

                    def emit_tail(hl, qb=qb):
                        hp = hl % 2
                        psO = ps2[:, hp * 1024:hp * 1024 + 512]
                        pbO, pbS = pb2[hp * 2], pb2[hp * 2 + 1]
                        P.op("pe", lambda en: en.matmul(
                            ps1[:, 1024:1536], ones_f[:], rrow[0:1, 512:1024], start=True, stop=True),
                            reads=[brr, bconsts["ones_f"]], writes=[pb1[2]], tag=f"bbc{qb}_{hl}")
                        P.op("dve", lambda en: en.tensor_copy(bcst[:], ps1[:, 1024:1536]),
                             reads=[pb1[2]], writes=[bbc], tag=f"bbcc{qb}_{hl}")
                        P.op("dve", lambda en, hl=hl, psO=psO, pbO=pbO: en.tensor_mul(
                            o_fm[:, hl * 512:(hl + 1) * 512], psO, bcst[:]),
                            reads=[pbO, bbc],
                            writes=[bof] if hl == 0 else (), update=[bof] if hl == 7 else (),
                            tag=f"bon{qb}_{hl}")

                    emit_strip(qb, 0)
                    S_of(0)
                    S_of(1)
                    for g in range(NG):
                        hl, kb = g // nkb, g % nkb
                        hp = hl % 2
                        gl = hl // 4
                        sp_ = hl % 2
                        kp = g % 3
                        psO = ps2[:, hp * 1024:hp * 1024 + 512]
                        psS = ps2[0:1, hp * 1024 + 512:hp * 1024 + 1024]
                        pbO, pbS = pb2[hp * 2], pb2[hp * 2 + 1]
                        if kb == 0 and hl + 1 < 8:
                            emit_strip(qb, hl + 1)
                        if kb == 1 and hl > 0:
                            emit_tail(hl - 1)
                        rbC = SBUF3[kp]
                        regC = ps1[:, SLOT[kp]:SLOT[kp] + 512]
                        P.op("act", lambda en, hl=hl, kp=kp, regC=regC: en.activation(
                            out=Ptmp[:, kp * 512:(kp + 1) * 512], in_=regC, func=AF.Exp, scale=QSB, bias=mneg[:, hl:hl + 1]),
                            reads=[rbC, bmneg], writes=[bPt[kp]], tag=f"bE{qb}_{hl}_{kb}")
                        off = qb * 512 - kb * 128 + 384
                        P.op("dve", lambda en, kp=kp, sp_=sp_, off=off: en.tensor_mul(
                            Pbf[:, kp * 512:kp * 512 + 512], Ptmp[:, kp * 512:(kp + 1) * 512],
                            strip[:, sp_ * STRIP_W + off:sp_ * STRIP_W + off + 512]),
                            reads=[bPt[kp], bstrip[sp_]], writes=[bPb[kp]], tag=f"bW{qb}_{hl}_{kb}")
                        if g + 2 < NG:
                            S_of(g + 2)
                        P.op("pe", lambda en, kb=kb, kp=kp, gl=gl, psO=psO, first=(kb == 0), last=(kb == nkb - 1): en.matmul(
                            psO, v_tm[:, kb * 256 + gl * 128:kb * 256 + gl * 128 + 128],
                            Pbf[:, kp * 512:kp * 512 + 512],
                            start=first, stop=last),
                            reads=[bPb[kp], bv],
                            writes=[pbO] if kb == 0 else (), update=[pbO] if kb == nkb - 1 else (),
                            tag=f"bPV{qb}_{hl}_{kb}")
                        P.op("pe", lambda en, kb=kb, kp=kp, psS=psS, first=(kb == 0), last=(kb == nkb - 1): en.matmul(
                            psS, ones_bf[:], Pbf[:, kp * 512:kp * 512 + 512],
                            start=first, stop=last),
                            reads=[bPb[kp], bconsts["ones_bf"]],
                            writes=[pbS] if kb == 0 else (), update=[pbS] if kb == nkb - 1 else (),
                            tag=f"bSS{qb}_{hl}_{kb}")
                        if kb == nkb - 1:
                            P.op("act", lambda en, psS=psS, pbS=pbS: en.activation(out=rrow[0:1, 0:512], in_=psS, func=AF.Ln),
                                 reads=[pbS], writes=[brr], tag=f"bln{qb}_{hl}")
                            P.op("act", lambda en: en.activation(out=rrow[0:1, 512:1024], in_=rrow[0:1, 0:512], func=AF.Exp, scale=-1.0),
                                 reads=[brr], update=[brr], tag=f"bre{qb}_{hl}")
                    emit_tail(7)
                    for half in range(2):
                        for hh_ in range(4):
                            hglob = half * 4 + hh_
                            P.op("sp" if hh_ % 2 == 0 else "pool", lambda en, hglob=hglob, hh_=hh_: en.dma_start(
                                out=wob[:, hh_ * 2048:(hh_ + 1) * 2048],
                                in_=dcache["wo_b"][hglob * 128:(hglob + 1) * 128, :]),
                                reads=[bcache["wo_b"]], writes=[bwob[hh_]], dma=True, tag=f"bwo{qb}_{half}_{hh_}")
                        for dc in range(16):
                            reg = ps2[:, 1024 + (dc % 2) * 512:1024 + (dc % 2) * 512 + 512]
                            rb = pb2[2 + dc % 2]
                            for hh_ in range(4):
                                hglob = half * 4 + hh_
                                P.op("pe", lambda en, dc=dc, hh_=hh_, hglob=hglob, reg=reg: en.matmul(
                                    reg, wob[:, hh_ * 2048 + dc * 128:hh_ * 2048 + (dc + 1) * 128],
                                    o_fm[:, hglob * 512:(hglob + 1) * 512], start=(hh_ == 0), stop=(hh_ == 3)),
                                    reads=[bwob[hh_], bof],
                                    writes=[rb] if hh_ == 0 else (), update=[rb] if hh_ == 3 else (),
                                    tag=f"bwomm{qb}_{half}_{dc}_{hh_}")
                            P.op("act", lambda en, reg=reg: en.activation(out=stg[:], in_=reg, func=AF.Identity),
                                 reads=[rb], writes=[bstg2], tag=f"bwoev{qb}_{half}_{dc}")
                            if half == 0:
                                pt_toks[qb % 2].append(P.op("sp", lambda en, qb=qb, dc=dc: en.dma_start(
                                    out=ag_pt_in2[qb % 2][(qb // 2) * D + dc * 128:(qb // 2) * D + (dc + 1) * 128, :],
                                    in_=stg[:]),
                                    reads=[bstg2], writes=[bag_ptr[qb][dc]], dma=True, tag=f"bwod{qb}_{half}_{dc}"))
                            else:
                                pt_toks[qb % 2].append(P.op("pool", lambda en, qb=qb, dc=dc: en.dma_start(
                                    out=ag_pt_in2[qb % 2][(qb // 2) * D + dc * 128:(qb // 2) * D + (dc + 1) * 128, :],
                                    in_=stg[:], accum_op=ALU.add),
                                    reads=[bstg2], writes=[bag_ptr[qb][dc]], dma=True, tag=f"bwod{qb}_{half}_{dc}"))
                    if qb == 2:
                        emit_rs(0, "ag_pt_b0")
                P.fence([bq, bk, bv, bws[0], bws[1], bwvb, bstrip[0], bstrip[1],
                         bsq, bnrm, bmneg, brr, bbc, bof, bstg2] + bPt + bPb + bwob + bxnw, tag="bf")

            emit_rs(1, "ag_pt_b1")
            with ExitStack() as rctx:
                re_ = rctx.enter_context
                rb0 = re_(nc.sbuf_tensor("b_rb0", [128, 2 * TC], F32))
                brb0 = [[P.buf(f"brb0{i}_{ch}") for ch in range(2)] for i in range(2)]
                for c in range(NCH):
                    par = c % 2
                    for colh in range(2):
                        r0v = rb0[:, par * TC + colh * 512:par * TC + colh * 512 + 512]
                        P.op("sp" if colh == 0 else "pool", lambda en, c=c, colh=colh, r0v=r0v: en.dma_start(
                            out=r0v, in_=ag_pt_out2[colh][c * 128:(c + 1) * 128, :]),
                            reads=[bag_pt_out2[colh]], writes=[brb0[par][colh]], dma=True, tag=f"brb0_{c}_{colh}")
                        sl = slice(c * TC + colh * 512, c * TC + colh * 512 + 512)
                        P.op("dve", lambda en, sl=sl, r0v=r0v: en.tensor_add(x_res[:, sl], x_res[:, sl], r0v),
                             reads=[brb0[par][colh]], writes=[bxc[c]], tag=f"bres0_{c}_{colh}")
                P.fence([b for row in brb0 for b in row], tag="brbf")

        if nsub >= 1:
            attn_a()
        if nsub >= 2:
            ffn_sublayer("f1")
        if nsub >= 3:
            attn_b()
        if nsub >= 4:
            ffn_sublayer("f2")

        out_toks = []

        def ox(en):
            return en.dma_start(out=out_x.rearrange("(c p) t -> p c t", p=128),
                                in_=x_res.ap().rearrange("p (c t) -> p c t", c=NCH))
        out_toks.append(P.op("sp", ox, reads=list(bxc), dma=True, tag="out_x"))

        P.final_barrier("sp", out_toks)
        blk = e(nc.Block())
        P.finalize(blk)

    return nc, [k for k in dbg_out]


def kernel(**inputs):
    in_maps = make_in_maps(inputs)
    nc, dbg = build_program()
    res = run_bass_kernel_spmd(nc, in_maps, core_ids=list(range(NC_ACT)))
    out = np.empty((B, T, D), np.float32)
    for c in range(NC_ACT):
        b, hh = c // 2, c % 2
        out[b, hh * TC:(hh + 1) * TC, :] = res.results[c]["out_x"].T
    return out


# version 09
# speedup vs baseline: 1.0918x; 1.0840x over previous
import numpy as np
import ml_dtypes
from contextlib import ExitStack

import concourse.bass as bass
import concourse.mybir as mybir
from concourse.bass_utils import run_bass_kernel_spmd

F32 = mybir.dt.float32
BF16 = mybir.dt.bfloat16
I32 = mybir.dt.int32
AF = mybir.ActivationFunctionType
ALU = mybir.AluOpType
BF = ml_dtypes.bfloat16

NCORE = 8
NC_ACT = 8
D = 2048
TC = 1024
NCH = 16
B = 4
T = 2048

HA = 8
DK = 64
DV = 128
MSLOT = 64
CLAMP = 32.0
A_EPS = 1e-5
QSCALE = float(DK) ** -0.5
LN_EPS = 1e-5

CFG = {"n_sub": 4, "debug": [], "debug_shapes": {}}

ENGS = ("pe", "act", "dve", "pool", "sp")
BLKNAME = {"pe": "tensor", "act": "scalar", "dve": "vector", "pool": "gpsimd", "sp": "sync"}
QSEM = {"sp": "q_sp", "act": "q_act", "pool": "q_pool"}
CSEM = {e: "c_" + e for e in ENGS}
NDMASEM = 8


class Buf:
    __slots__ = ("name", "writer", "readers")

    def __init__(self, name):
        self.name = name
        self.writer = None
        self.readers = []


class Prog:
    def __init__(self, nc, semh):
        self.nc = nc
        self.semh = semh
        self.thunks = {e: [] for e in ENGS}
        self.cinc = {e: [] for e in ENGS}
        self.drain_cover = {e: 0 for e in ENGS}
        self.dma_idx = {q: 0 for q in QSEM}
        self.cc_idx = 0

    def buf(self, name):
        return Buf(name)

    def op(self, eng, fn, reads=(), writes=(), update=(), dma=False, cc=False, tag=None):
        if update and eng != "pe":
            writes = list(writes) + [b for b in update if b not in writes]
            update = ()
        seq = len(self.thunks[eng])
        kind = "d" if dma else ("x" if cc else "c")
        tok = (eng, seq, kind)
        th = {"fn": fn, "waits": [], "drain": False, "incs": [], "eng": eng,
              "seq": seq, "tag": tag, "xwaits": [], "dsem": None}
        if dma:
            assert eng in QSEM, f"dma on {eng}"
            i = self.dma_idx[eng]
            self.dma_idx[eng] = i + 1
            slot, gen = i % NDMASEM, i // NDMASEM + 1
            sem = f"{QSEM[eng]}_{slot}"
            th["dsem"] = (sem, 16 * gen)
            th["incs"].append((sem, 16))
            if gen > 1:
                th["xwaits"].append((sem, 16 * (gen - 1)))
        if cc:
            assert eng == "pool"
            self.cc_idx += 1
            th["dsem"] = ("cc", self.cc_idx)
            th["incs"].append(("cc", 1))
            if self.cc_idx > 1:
                th["xwaits"].append(("cc", self.cc_idx - 1))
        deps = []
        for b in reads:
            if b.writer is not None:
                deps.append(b.writer)
        for b in writes:
            if b.writer is not None:
                deps.append(b.writer)
            deps.extend(b.readers)
        seen = set()
        for d in deps:
            if d in seen:
                continue
            seen.add(d)
            self._need(th, d)
        self.thunks[eng].append(th)
        for b in reads:
            b.readers.append(tok)
        for b in writes:
            b.writer = tok
            b.readers = []
        for b in update:
            b.writer = tok
        return tok

    def _need(self, th, tok):
        eng, seq, kind = tok
        me = th["eng"]
        if eng == me and kind == "c":
            if me == "pe":
                return
            if seq >= self.drain_cover[me]:
                th["drain"] = True
                self.drain_cover[me] = th["seq"]
            return
        if kind == "c":
            self._force_inc(tok)
        th["waits"].append(tok)

    def _force_inc(self, tok):
        eng, seq, kind = tok
        import bisect
        lst = self.cinc[eng]
        i = bisect.bisect_left(lst, seq)
        if i < len(lst):
            return
        self.thunks[eng][seq]["incs"].append((CSEM[eng], 1))
        bisect.insort(lst, seq)

    def final_barrier(self, eng, toks):
        finals = {}
        for q, n in self.dma_idx.items():
            for i in range(n):
                sem = f"{QSEM[q]}_{i % NDMASEM}"
                finals[sem] = max(finals.get(sem, 0), 16 * (i // NDMASEM + 1))
        if self.cc_idx:
            finals["cc"] = self.cc_idx
        tok = self.barrier(eng, toks, tag="final")
        self.thunks[eng][-1]["xwaits"].extend(finals.items())
        return tok

    def fence(self, bufs, tag="fence"):
        toks = []
        for b in bufs:
            if b.writer is not None:
                toks.append(b.writer)
            toks.extend(b.readers)
        toks = list(dict.fromkeys(toks))
        for eng in ENGS:
            self.barrier(eng, toks, tag=f"{tag}_{eng}")

    def barrier(self, eng, toks, tag=None):
        def fn(e):
            return e.nop()
        th_tok = self.op(eng, fn, tag=tag or "barrier")
        th = self.thunks[eng][-1]
        for t in toks:
            self._need(th, t)
        return th_tok

    def finalize(self, blk):
        incpts = {}
        for eng in ENGS:
            cums = {}
            for th in self.thunks[eng]:
                for (sem, amt) in th["incs"]:
                    cums.setdefault(sem, [0, []])
                    cums[sem][0] += amt
                    cums[sem][1].append((th["seq"], cums[sem][0]))
            for sem, (_, lst) in cums.items():
                if sem.startswith("c_"):
                    assert sem not in incpts
                    incpts[sem] = lst
        import bisect

        def resolve(tok):
            eng, seq, kind = tok
            if kind in ("d", "x"):
                return self.thunks[eng][seq]["dsem"]
            sem = CSEM[eng]
            lst = incpts.get(sem)
            assert lst, f"no incs on {sem} for tok {tok}"
            seqs = [s for s, _ in lst]
            i = bisect.bisect_left(seqs, seq)
            assert i < len(lst), f"no inc covering {tok}"
            return sem, lst[i][1]

        for eng in ENGS:
            hw = {}
            for th in self.thunks[eng]:
                rw = {}
                for tok in th["waits"]:
                    sem, val = resolve(tok)
                    if val > rw.get(sem, -1):
                        rw[sem] = val
                for sem, val in th["xwaits"]:
                    if val > rw.get(sem, -1):
                        rw[sem] = val
                th["rwaits"] = []
                for sem, val in rw.items():
                    if val > hw.get(sem, -1):
                        th["rwaits"].append((sem, val))
                        hw[sem] = val

        semh = self.semh
        for eng in ENGS:
            ths = self.thunks[eng]
            if not ths:
                continue

            def runner(e, ths=ths):
                for th in ths:
                    if th["drain"]:
                        e.drain()
                    for (sem, val) in th["rwaits"]:
                        e.wait_ge(semh[sem], val)
                    inst = th["fn"](e)
                    for (sem, amt) in th["incs"]:
                        inst.then_inc(semh[sem], amt)
            getattr(blk, BLKNAME[eng])(runner)



def _alibi_slopes(n_heads):
    import math
    n = 2 ** int(math.floor(math.log2(n_heads)))
    m0 = 2.0 ** (-8.0 / n)
    slopes = m0 ** np.arange(1, n + 1)
    if n < n_heads:
        m1 = 2.0 ** (-4.0 / n)
        slopes = np.concatenate([slopes, m1 ** np.arange(1, 1 + 2 * (n_heads - n), 2)])
    return slopes.astype(np.float64)

STRIP_W = 2432


def _alibi_strips(hh):
    slopes = _alibi_slopes(16)
    heads = range(hh * 8, hh * 8 + 8)
    u = np.arange(STRIP_W)[None, :]
    p = np.arange(128)[:, None]
    d = (u - 384 - p).astype(np.float64)
    mask = d >= 0
    sq = np.sqrt(np.maximum(d, 0))
    out = np.empty((8, 128, STRIP_W), np.float32)
    for i, h in enumerate(heads):
        out[i] = np.where(mask, np.exp(-slopes[h] * sq), 0.0).astype(np.float32)
    return out.astype(BF)


def make_in_maps(inputs):
    x = np.asarray(inputs["L0_attn_a_hidden_states"])
    in_maps = []
    ffn_names = ["L0_ffn_Wg", "L0_ffn_bg", "L0_ffn_Wup", "L0_ffn_bup", "L0_ffn_Wsw",
                 "L0_ffn_bsw", "L0_ffn_Wdn", "L0_ffn_bdn", "L0_ffn_Wsu", "L0_ffn_bsu",
                 "L0_ffn_Wsa", "L0_ffn_bsa", "L0_ffn_Wsd", "L0_ffn_bsd"]
    ffn = {n.replace("L0_ffn_", "f_"): np.ascontiguousarray(np.asarray(inputs[n]), dtype=np.float32)
           for n in ffn_names}
    ffn["f_Wup_t"] = np.ascontiguousarray(
        ffn.pop("f_Wup").reshape(6, 16, 128, 8, 128).transpose(2, 0, 3, 1, 4).reshape(128, 6 * 8 * 2048))
    ffn["f_Wsw_t"] = np.ascontiguousarray(
        ffn.pop("f_Wsw").reshape(6, 8, 128, 8, 128).transpose(2, 0, 3, 1, 4).reshape(128, 6 * 8 * 1024))
    ffn["f_Wdn_t"] = np.ascontiguousarray(
        ffn.pop("f_Wdn").reshape(6, 2, 4, 128, 4, 512).transpose(3, 0, 1, 4, 2, 5).reshape(128, 6 * 2 * 4 * 2048))
    ffn["f_Wsu_t2"] = np.ascontiguousarray(
        ffn.pop("f_Wsu").reshape(16, 128, 16, 128).transpose(1, 2, 0, 3).reshape(128, 16 * 2048))
    ffn["f_Wsa_t2"] = np.ascontiguousarray(
        ffn.pop("f_Wsa").reshape(8, 128, 8, 128).transpose(1, 2, 0, 3).reshape(128, 8 * 1024))
    ffn["f_Wsd_t2"] = np.ascontiguousarray(
        ffn.pop("f_Wsd").reshape(8, 2, 128, 4, 512).transpose(2, 3, 0, 1, 4).reshape(128, 4 * 8 * 1024))
    wqbT = np.ascontiguousarray(np.asarray(inputs["L0_attn_b_Wq"]).T)
    wkbT = np.ascontiguousarray(np.asarray(inputs["L0_attn_b_Wk"]).T)
    wvbT = np.ascontiguousarray(np.asarray(inputs["L0_attn_b_Wv"]).T)
    wobT = np.ascontiguousarray(np.asarray(inputs["L0_attn_b_Wo"]).T)
    for c in range(NC_ACT):
        b, hh = c // 2, c % 2
        t0 = TC * hh
        im = {}
        im["x_core"] = np.ascontiguousarray(x[b, t0:t0 + TC, :].T)
        im["wq_a"] = np.ascontiguousarray(
            np.asarray(inputs["L0_attn_a_Wq"])[:, hh * 512:(hh + 1) * 512]
            .reshape(16, 128, 4, 128).transpose(1, 2, 0, 3).reshape(128, 4 * 2048))
        im["wk_a"] = np.ascontiguousarray(
            np.asarray(inputs["L0_attn_a_Wk"])[:, hh * 512:(hh + 1) * 512]
            .reshape(16, 128, 4, 128).transpose(1, 2, 0, 3).reshape(128, 4 * 2048))
        im["ws_a"] = np.ascontiguousarray(
            np.asarray(inputs["L0_attn_a_Ws"])[:, hh * 512:(hh + 1) * 512]
            .reshape(16, 128, 512).transpose(1, 0, 2).reshape(128, 16 * 512))
        im["wv_a"] = np.ascontiguousarray(np.asarray(inputs["L0_attn_a_Wv"])[:, hh * 1024:(hh + 1) * 1024])
        im["wg_a"] = np.ascontiguousarray(np.asarray(inputs["L0_attn_a_Wg"])[:, hh * 1024:(hh + 1) * 1024])
        im["wo_a"] = np.ascontiguousarray(np.asarray(inputs["L0_attn_a_Wo"])[hh * 1024:(hh + 1) * 1024, :])
        im["gnw_a"] = np.ascontiguousarray(np.asarray(inputs["L0_attn_a_g_norm_weight"]).reshape(128, 1))
        im["wq_b"] = np.ascontiguousarray(
            wqbT[:, hh * 1024:(hh + 1) * 1024]
            .reshape(16, 128, 8, 128).transpose(1, 2, 0, 3).reshape(128, 8 * 2048))
        im["wk_b"] = np.ascontiguousarray(
            wkbT[:, hh * 256:(hh + 1) * 256]
            .reshape(16, 128, 2, 128).transpose(1, 2, 0, 3).reshape(128, 2 * 2048))
        im["wv_b"] = np.ascontiguousarray(
            wvbT[:, hh * 256:(hh + 1) * 256]
            .reshape(16, 128, 256).transpose(1, 0, 2).reshape(128, 16 * 256))
        im["wo_b"] = np.ascontiguousarray(wobT[hh * 1024:(hh + 1) * 1024, :])
        im.update(ffn)
        im["f_bup_t"] = np.ascontiguousarray(ffn["f_bup"].reshape(6, 8, 128).transpose(2, 0, 1).reshape(128, 48))
        im["f_bsw_t"] = np.ascontiguousarray(ffn["f_bsw"].reshape(6, 8, 128).transpose(2, 0, 1).reshape(128, 48))
        im["f_bsa_t"] = np.ascontiguousarray(ffn["f_bsa"].reshape(8, 128).T)
        im["f_bsu_t"] = np.ascontiguousarray(ffn["f_bsu"].reshape(16, 128).T)
        im["f_bg_t"] = np.ascontiguousarray(ffn["f_bg"].T)
        im["f_wg_t"] = np.ascontiguousarray(
            ffn["f_Wg"].reshape(2, 16, 128, 8).transpose(2, 0, 1, 3).reshape(128, 2 * 16 * 8))
        im["strips"] = _alibi_strips(hh)
        lm = (np.arange(128)[:, None] <= np.arange(128)[None, :]).astype(np.float32)
        im["lmask"] = lm
        im["lmask8"] = np.ascontiguousarray(np.tile(lm, (1, 8)).astype(BF))
        im["ident_bf"] = np.eye(128, dtype=BF)
        im["ident_f"] = np.eye(128, dtype=np.float32)
        bc8 = np.zeros((8, 6 * 128), np.float32)
        for n_ in range(6):
            bc8[n_, n_ * 128:(n_ + 1) * 128] = 1.0
        im["bcast8"] = bc8.astype(BF)
        im["ones_bf"] = np.ones((128, 1), dtype=BF)
        im["ones_f"] = np.ones((1, 128), np.float32)
        in_maps.append(im)
    return in_maps



PAIRS = [[0, 1], [2, 3], [4, 5], [6, 7]]


def build_program():
    nsub = CFG["n_sub"]
    nc = bass.Bass("TRN2", target_bir_lowering=False)

    din = {}

    def dt_in(name, shape, dtype=F32):
        din[name] = nc.dram_tensor(name, list(shape), dtype, kind="ExternalInput").ap()
        return din[name]

    dt_in("x_core", [D, TC])
    for n, sh in [("wq_a", [128, 4 * 2048]), ("wk_a", [128, 4 * 2048]), ("ws_a", [128, 16 * 512]),
                  ("wv_a", [D, 1024]), ("wg_a", [D, 1024]), ("wo_a", [1024, D]),
                  ("gnw_a", [128, 1]),
                  ("wq_b", [128, 8 * 2048]), ("wk_b", [128, 2 * 2048]), ("wv_b", [128, 16 * 256]),
                  ("wo_b", [1024, D]),
                  ("f_Wg", [2, D, 8]), ("f_bg", [2, 8]),
                  ("f_Wup_t", [128, 6 * 8 * 2048]), ("f_bup", [6, 1024]),
                  ("f_Wsw_t", [128, 6 * 8 * 1024]), ("f_bsw", [6, 1024]),
                  ("f_Wdn_t", [128, 6 * 2 * 4 * 2048]), ("f_bdn", [6, D]),
                  ("f_Wsu_t2", [128, 16 * 2048]), ("f_bsu", [D]),
                  ("f_Wsa_t2", [128, 8 * 1024]), ("f_bsa", [1024]),
                  ("f_Wsd_t2", [128, 4 * 8 * 1024]), ("f_bsd", [D]),
                  ("lmask", [128, 128]),
                  ("ones_f", [1, 128]), ("ident_f", [128, 128]),
                  ("f_bup_t", [128, 48]), ("f_bsw_t", [128, 48]), ("f_bsa_t", [128, 8]),
                  ("f_bsu_t", [128, 16]), ("f_bg_t", [8, 2]), ("f_wg_t", [128, 256])]:
        dt_in(n, sh)
    dt_in("ident_bf", [128, 128], BF16)
    dt_in("lmask8", [128, 1024], BF16)
    dt_in("strips", [8, 128, STRIP_W], BF16)
    dt_in("bcast8", [8, 6 * 128], BF16)
    dt_in("ones_bf", [128, 1], BF16)

    out_x = nc.dram_tensor("out_x", [D, TC], F32, kind="ExternalOutput").ap()
    dbg_out = {}
    for name, shape in CFG.get("debug_shapes", {}).items():
        dbg_out[name] = nc.dram_tensor("dbg_" + name, list(shape), F32, kind="ExternalOutput").ap()

    cache_specs = [
        ("wq_b", [128, 16384]), ("wk_b", [128, 4096]), ("wo_b", [1024, D]),
    ]
    dcache = {n: nc.dram_tensor("c_" + n, list(sh), BF16, kind="Internal").ap()
              for n, sh in cache_specs}

    s_spill = nc.dram_tensor("s_spill", [64, HA * T], F32, kind="Internal").ap()
    ag_xn_in = nc.dram_tensor("ag_xn_in", [D, TC], BF16, kind="Internal").ap()
    ag_xn_out = nc.dram_tensor("ag_xn_out", [2 * D, TC], BF16, kind="Internal").ap()
    ag_pt_in2 = [nc.dram_tensor(f"ag_pt_in{i}", [2 * D, 512], F32, kind="Internal").ap() for i in range(2)]
    h_spill = nc.dram_tensor("h_spill", [128, 48 * TC], BF16, kind="Internal").ap()
    sh_spill = nc.dram_tensor("sh_spill", [128, 16 * TC], BF16, kind="Internal").ap()
    ag_pt_out2 = [nc.dram_tensor(f"ag_pt_out{i}", [D, 512], F32, kind="Internal").ap() for i in range(2)]

    with ExitStack() as ctx:
        e = ctx.enter_context
        semh = {}
        semnames = ["cc"] + [CSEM[x] for x in ENGS]
        for q in QSEM.values():
            semnames += [f"{q}_{i}" for i in range(NDMASEM)]
        for s in semnames:
            semh[s] = e(nc.semaphore(s))

        x_res = e(nc.sbuf_tensor("x_res", [128, NCH * TC], F32))
        lmask8 = e(nc.sbuf_tensor("sb_lmask8", [128, 1024], BF16))
        identf = e(nc.sbuf_tensor("sb_identf", [128, 128], F32))
        bc8t = e(nc.sbuf_tensor("sb_bc8", [8, 6 * 128], BF16))
        ones_bf = e(nc.sbuf_tensor("sb_ones_bf", [128, 1], BF16))
        ones_f = e(nc.sbuf_tensor("sb_ones_f", [1, 128], F32))
        gnw = e(nc.sbuf_tensor("sb_gnw", [128, 1], F32))
        epsc = e(nc.sbuf_tensor("sb_epsc", [128, 1], F32))

        P = Prog(nc, semh)

        bxc = [P.buf(f"x_res{c}") for c in range(NCH)]
        bconsts = {n: P.buf("const_" + n) for n in
                   ["lmask", "lmask8", "ident", "ones_bf", "ones_f", "gnw", "eps",
                    "identf", "bc8"]}

        bcstg = [P.buf("cstg0"), P.buf("cstg1")]
        bcache = {n: P.buf("cache_" + n) for n, _ in cache_specs}
        conv_jobs = []
        conv_last = {}
        for n_, sh_ in cache_specs:
            R_, CC_ = sh_
            for r_ in range(max(1, R_ // 128)):
                for i_ in range(max(1, CC_ // 1024)):
                    w_ = min(1024, CC_)
                    conv_jobs.append((n_, din[n_][r_ * 128:(r_ + 1) * 128, i_ * w_:(i_ + 1) * w_],
                                      dcache[n_][r_ * 128:(r_ + 1) * 128, i_ * w_:(i_ + 1) * w_], w_))
            conv_last[n_] = len(conv_jobs)
        conv_state = {"i": 0, "stg": None}

        def conv_done(name):
            return name in conv_last and conv_state["i"] >= conv_last[name]

        def feed_conv(k):
            stg = conv_state["stg"]
            while k > 0 and conv_state["i"] < len(conv_jobs):
                assert stg is not None
                idx = conv_state["i"]
                conv_state["i"] += 1
                nme, src, dst, w_ = conv_jobs[idx]
                par = idx % 2
                stg_v = stg[:, par * 1024:par * 1024 + w_]
                P.op("pool", lambda en, src=src, stg_v=stg_v: en.dma_start(out=stg_v, in_=src),
                     writes=[bcstg[par]], dma=True, tag=f"cvr{idx}")
                P.op("sp", lambda en, dst=dst, stg_v=stg_v: en.dma_start(out=dst, in_=stg_v),
                     reads=[bcstg[par]], update=[bcache[nme]], dma=True, tag=f"cvw{idx}")
                k -= 1

        for c_ in range(NCH):
            P.op("sp" if c_ % 2 == 0 else "pool", lambda en, c_=c_: en.dma_start(
                out=x_res[:, c_ * TC:(c_ + 1) * TC],
                in_=din["x_core"][c_ * 128:(c_ + 1) * 128, :]),
                writes=[bxc[c_]], dma=True, tag=f"ldx{c_}")
        for nm, dst in [("lmask8", lmask8), ("ones_f", ones_f),
                        ("gnw_a", gnw)]:
            key = "gnw" if nm == "gnw_a" else nm
            P.op("sp", lambda en, dst=dst, nm=nm: en.dma_start(out=dst[:], in_=din[nm]),
                 writes=[bconsts[key]], dma=True)
        P.op("sp", lambda en: en.dma_start(out=identf[:], in_=din["ident_f"]), writes=[bconsts["identf"]], dma=True)
        P.op("sp", lambda en: en.dma_start(out=bc8t[:], in_=din["bcast8"]), writes=[bconsts["bc8"]], dma=True)
        P.op("sp", lambda en: en.dma_start(out=ones_bf[:], in_=din["ones_bf"]), writes=[bconsts["ones_bf"]], dma=True)
        P.op("dve", lambda en: en.memset(epsc[:], LN_EPS), writes=[bconsts["eps"]], tag="eps")

        ps1 = e(nc.psum_tensor("ps1", [128, 2048], F32))
        ps2 = e(nc.psum_tensor("ps2", [128, 2048], F32))
        pb1 = [P.buf(f"ps1b{i}") for i in range(4)]
        pb2 = [P.buf(f"ps2b{i}") for i in range(4)]

        def layernorm(tag, xn_t, bxn_t):
            with ExitStack() as lctx:
                le = lctx.enter_context
                bca = le(nc.sbuf_tensor(tag + "_bca", [128, TC], F32))
                bcm = le(nc.sbuf_tensor(tag + "_bcm", [128, TC], F32))
                sc_a = le(nc.sbuf_tensor(tag + "_sca", [128, TC], F32))
                sc_b = le(nc.sbuf_tensor(tag + "_scb", [128, TC], F32))
                srow = le(nc.sbuf_tensor(tag + "_srow", [1, 3 * TC], F32))
                x2bf = le(nc.sbuf_tensor(tag + "_x2bf", [128, 2 * TC], BF16))
                xbf = le(nc.sbuf_tensor(tag + "_xbf", [128, 2 * TC], BF16))
                xnc = le(nc.sbuf_tensor(tag + "_xnc", [128, 2 * TC], BF16)) if xn_t is None else None
                bbc = P.buf(tag + "bcast")
                bsc = [P.buf(tag + "sca"), P.buf(tag + "scb")]
                bxbf = [P.buf(tag + "xbf0"), P.buf(tag + "xbf1")]
                bx2bf = [P.buf(tag + "x2bf0"), P.buf(tag + "x2bf1")]
                bsrow = P.buf(tag + "srow")
                bxnc = [P.buf(tag + "xnc0"), P.buf(tag + "xnc1")]
                stage_toks = []

                for c in range(NCH):
                    sl = slice(c * TC, (c + 1) * TC)
                    par = c % 2
                    xb = xbf[:, par * TC:(par + 1) * TC]
                    x2 = x2bf[:, par * TC:(par + 1) * TC]
                    P.op("dve", lambda en, sl=sl, xb=xb: en.tensor_copy(xb, x_res[:, sl]),
                         reads=[bxc[c]], writes=[bxbf[par]], tag=f"{tag}_xbf{c}")
                    P.op("act", lambda en, sl=sl, x2=x2: en.activation(out=x2, in_=x_res[:, sl], func=AF.Square),
                         reads=[bxc[c]], writes=[bx2bf[par]], tag=f"{tag}_x2{c}")
                    for h in range(2):
                        hs = slice(h * 512, (h + 1) * 512)
                        first, last = c == 0, c == NCH - 1
                        P.op("pe", lambda en, xb=xb, hs=hs, first=first, last=last, h=h:
                             en.matmul(ps1[0:1, h * 512:(h + 1) * 512], ones_bf[:], xb[:, hs], start=first, stop=last),
                             reads=[bxbf[par], bconsts["ones_bf"]],
                             writes=[pb1[h]] if first else (), update=[pb1[h]] if last else (),
                             tag=f"{tag}_s1_{c}_{h}")
                        P.op("pe", lambda en, x2=x2, hs=hs, first=first, last=last, h=h:
                             en.matmul(ps1[0:1, 1024 + h * 512:1024 + (h + 1) * 512], ones_bf[:], x2[:, hs], start=first, stop=last),
                             reads=[bx2bf[par], bconsts["ones_bf"]],
                             writes=[pb1[2 + h]] if first else (), update=[pb1[2 + h]] if last else (),
                             tag=f"{tag}_s2_{c}_{h}")
                m_row = srow[0:1, 0:TC]
                a_row = srow[0:1, TC:2 * TC]
                ss_row = srow[0:1, 2 * TC:3 * TC]
                v_row = a_row
                for h in range(2):
                    hs = slice(h * 512, (h + 1) * 512)
                    P.op("dve", lambda en, hs=hs, h=h:
                         en.tensor_scalar_mul(srow[0:1, hs], ps1[0:1, h * 512:(h + 1) * 512], 1.0 / D),
                         reads=[pb1[h]], writes=[bsrow] if h == 0 else (), update=[bsrow] if h else (),
                         tag=f"{tag}_m{h}")
                    P.op("dve", lambda en, h=h:
                         en.tensor_scalar_mul(srow[0:1, 2 * TC + h * 512:2 * TC + (h + 1) * 512],
                                              ps1[0:1, 1024 + h * 512:1024 + (h + 1) * 512], 1.0 / D),
                         reads=[pb1[2 + h]], update=[bsrow], tag=f"{tag}_ss{h}")
                P.op("dve", lambda en: en.scalar_tensor_tensor(
                    out=v_row, in0=m_row, scalar=-1.0, in1=m_row, op0=ALU.mult, op1=ALU.mult),
                    reads=[bsrow], update=[bsrow], tag=f"{tag}_mm")
                P.op("dve", lambda en: en.tensor_add(v_row, v_row, ss_row),
                     reads=[bsrow], update=[bsrow], tag=f"{tag}_var")
                P.op("act", lambda en: en.activation(out=ss_row, in_=v_row, func=AF.Ln, bias=epsc[0:1, :]),
                     reads=[bsrow, bconsts["eps"]], update=[bsrow], tag=f"{tag}_ln")
                P.op("act", lambda en: en.activation(out=a_row, in_=ss_row, func=AF.Exp, scale=-0.5),
                     reads=[bsrow], update=[bsrow], tag=f"{tag}_exp")
                bbcm = P.buf(tag + "bbcm")
                for h in range(2):
                    hs = slice(h * 512, (h + 1) * 512)
                    P.op("pe", lambda en, hs=hs:
                         en.matmul(ps2[:, 0:512], ones_f[:], m_row[0:1, hs], start=True, stop=True),
                         reads=[bsrow, bconsts["ones_f"]], writes=[pb2[0]], tag=f"{tag}_bcm{h}")
                    P.op("dve", lambda en, hs=hs: en.tensor_copy(bcm[:, hs], ps2[:, 0:512]),
                         reads=[pb2[0]], writes=[bbcm] if h == 0 else (), update=[bbcm] if h else (),
                         tag=f"{tag}_bcmc{h}")
                    P.op("pe", lambda en, hs=hs:
                         en.matmul(ps2[:, 512:1024], ones_f[:], a_row[0:1, hs], start=True, stop=True),
                         reads=[bsrow, bconsts["ones_f"]], writes=[pb2[1]], tag=f"{tag}_bca{h}")
                    P.op("dve", lambda en, hs=hs: en.tensor_copy(bca[:, hs], ps2[:, 512:1024]),
                         reads=[pb2[1]], writes=[bbc] if h == 0 else (), update=[bbc] if h else (),
                         tag=f"{tag}_bcac{h}")
                for c in range(NCH):
                    sl = slice(c * TC, (c + 1) * TC)
                    par = c % 2
                    sc = (sc_a, sc_b)[par]
                    P.op("dve", lambda en, sl=sl, sc=sc: en.tensor_sub(sc[:], x_res[:, sl], bcm[:]),
                         reads=[bxc[c], bbcm], writes=[bsc[par]], tag=f"{tag}_n1_{c}")
                    if xn_t is not None:
                        P.op("dve", lambda en, sl=sl, sc=sc: en.tensor_mul(xn_t[:, sl], sc[:], bca[:]),
                             reads=[bsc[par], bbc], writes=[bxn_t[c]], tag=f"{tag}_n2_{c}")
                    else:
                        dst = xnc[:, par * TC:(par + 1) * TC]
                        P.op("dve", lambda en, sc=sc, dst=dst: en.tensor_mul(dst, sc[:], bca[:]),
                             reads=[bsc[par], bbc], writes=[bxnc[par]], tag=f"{tag}_n2_{c}")
                        tkn = P.op("sp", lambda en, dst=dst, c=c: en.dma_start(
                            out=ag_xn_in.rearrange("(c p) t -> p c t", p=128)[:, c, :], in_=dst),
                            reads=[bxnc[par]], dma=True, tag=f"{tag}_st{c}")
                        stage_toks.append(tkn)
                P.fence([bbc, bbcm, bsc[0], bsc[1], bxbf[0], bxbf[1], bx2bf[0], bx2bf[1],
                         bsrow, bxnc[0], bxnc[1]], tag=tag + "f")
                return stage_toks

        bag_xn_in = P.buf("ag_xn_in")
        bag_xn_out = P.buf("ag_xn_out")
        bag_ptr = [[P.buf(f"agpt{w}_{dc}") for dc in range(16)] for w in range(4)]
        bag_pt_out2 = [P.buf("ag_pt_out0"), P.buf("ag_pt_out1")]
        bspill = [P.buf(f"s_spill{i}") for i in range(4)]

        def attn_a():
            stage_toks = layernorm("ln0", None, None)
            tcc = P.op("pool", lambda en: en.collective_compute(
                "AllGather", ALU.bypass, ins=[ag_xn_in.opt()], outs=[ag_xn_out.opt()],
                replica_groups=PAIRS),
                writes=[bag_xn_out], cc=True, tag="ag_xn")
            th = P.thunks["pool"][-1]
            for t in stage_toks:
                P._need(th, t)

            pt_toks = {0: [], 1: []}
            rs_emitted = [False, False]

            def emit_rs(colh, tg):
                rs_emitted[colh] = True
                P.op("pool", lambda en, colh=colh: en.collective_compute(
                    "ReduceScatter", ALU.add, ins=[ag_pt_in2[colh].opt()], outs=[ag_pt_out2[colh].opt()],
                    replica_groups=PAIRS),
                    writes=[bag_pt_out2[colh]], cc=True, tag=tg)
                thpt = P.thunks["pool"][-1]
                for t in pt_toks[colh]:
                    P._need(thpt, t)
            with ExitStack() as actx:
                ae = actx.enter_context
                xnw = ae(nc.sbuf_tensor("a_xnw", [128, NCH * 512], BF16))
                bxnw = [P.buf(f"xnw{c}") for c in range(NCH)]

                def gather_xnw(w, tag):
                    for c0 in range(0, NCH, 4):
                        def fn(en, c0=c0, w=w):
                            return en.dma_start(
                                out=xnw.ap().rearrange("p (c t) -> p c t", c=NCH)[:, c0:c0 + 4, :],
                                in_=ag_xn_out.rearrange("(c p) t -> p c t", p=128)[
                                    :, (w // 2) * NCH + c0:(w // 2) * NCH + c0 + 4,
                                    (w % 2) * 512:(w % 2) * 512 + 512])
                        P.op("pool", fn, reads=[bag_xn_out],
                             writes=[bxnw[c0 + i] for i in range(4)], dma=True, tag=f"{tag}_g{c0}")

                zst = ae(nc.sbuf_tensor("a_zst", [64, 8 * 8], F32))
                bzst = P.buf("zst")
                zneg8 = ae(nc.sbuf_tensor("a_zneg", [64, 8], F32))
                bzneg = P.buf("zneg8")
                with ExitStack() as p1ctx:
                    p1 = p1ctx.enter_context
                    wst = p1(nc.sbuf_tensor("a_wst", [128, NCH * 512], BF16))
                    s_win = p1(nc.sbuf_tensor("a_swin", [64, HA * 512], F32))
                    zdump = p1(nc.sbuf_tensor("a_zdump", [64, 512], F32))
                    bwst = [P.buf(f"wst{i}") for i in range(4)]
                    bswin = P.buf("swin")
                    bzdump = P.buf("zdump")
                    P.op("pool", lambda en: en.dma_start(out=wst[:], in_=din["ws_a"]),
                         writes=list(bwst), dma=True, tag="ws_all")
                    for w in range(4):
                        gather_xnw(w, f"p1x{w}")
                        for h in range(HA):
                            hl = h % 4
                            reg = ps2[0:64, hl * 512:(hl + 1) * 512]
                            rb = pb2[hl]
                            for c in range(NCH):
                                P.op("pe", lambda en, h=h, c=c, reg=reg:
                                     en.matmul(reg, wst[:, c * 512 + h * 64:c * 512 + h * 64 + 64],
                                               xnw[:, c * 512:(c + 1) * 512],
                                               start=(c == 0), stop=(c == NCH - 1)),
                                     reads=[bwst[h // 2], bxnw[c]],
                                     writes=[rb] if c == 0 else (), update=[rb] if c == NCH - 1 else (),
                                     tag=f"sproj{w}_{h}_{c}")
                            P.op("dve", lambda en, h=h, reg=reg, rb=rb: en.tensor_scalar(
                                out=s_win[:, h * 512:(h + 1) * 512], in0=reg,
                                scalar1=-CLAMP, scalar2=CLAMP, op0=ALU.max, op1=ALU.min),
                                reads=[rb],
                                writes=[bswin] if h == 0 else (), update=[bswin] if h == HA - 1 else (),
                                tag=f"sclip{w}_{h}")
                        P.op("sp", lambda en, w=w: en.dma_start(
                            out=s_spill.rearrange("m (h t) -> m h t", h=HA)[:, :, w * 512:(w + 1) * 512],
                            in_=s_win.ap().rearrange("m (h t) -> m h t", h=HA)),
                            reads=[bswin], writes=[bspill[w]], dma=True, tag=f"sspill{w}")
                        for h in range(HA):
                            P.op("dve", lambda en, h=h: en.reduce_max(
                                zst[:, h:h + 1], s_win[:, h * 512:(h + 1) * 512], axis=mybir.AxisListType.X),
                                reads=[bswin],
                                writes=[bzst] if (w == 0 and h == 0) else (), update=[bzst],
                                tag=f"zmax{w}_{h}")
                        P.op("dve", lambda en: en.tensor_scalar_mul(zst[:, 48:56], zst[:, 0:8], -1.0),
                             reads=[bzst], update=[bzst], tag=f"znegw{w}")
                        for h in range(HA):
                            P.op("act", lambda en, h=h: en.activation(
                                out=zdump[:], in_=s_win[:, h * 512:(h + 1) * 512], func=AF.Exp,
                                bias=zst[:, 48 + h:49 + h], accum_out=zst[:, 8 + h:9 + h]),
                                reads=[bswin, bzst], writes=[bzdump], update=[bzst], tag=f"zexp{w}_{h}")
                        if w == 0:
                            P.op("dve", lambda en: en.tensor_copy(zst[:, 16:24], zst[:, 0:8]),
                                 reads=[bzst], update=[bzst], tag="zm0")
                            P.op("dve", lambda en: en.tensor_copy(zst[:, 24:32], zst[:, 8:16]),
                                 reads=[bzst], update=[bzst], tag="zs0")
                        else:
                            P.op("dve", lambda en: en.tensor_max(zst[:, 32:40], zst[:, 16:24], zst[:, 0:8]),
                                 reads=[bzst], update=[bzst], tag=f"zmg{w}a")
                            P.op("dve", lambda en: en.tensor_sub(zst[:, 48:56], zst[:, 16:24], zst[:, 32:40]),
                                 reads=[bzst], update=[bzst], tag=f"zmg{w}b")
                            P.op("act", lambda en: en.activation(out=zst[:, 48:56], in_=zst[:, 48:56], func=AF.Exp),
                                 reads=[bzst], update=[bzst], tag=f"zmg{w}c")
                            P.op("dve", lambda en: en.tensor_mul(zst[:, 24:32], zst[:, 24:32], zst[:, 48:56]),
                                 reads=[bzst], update=[bzst], tag=f"zmg{w}d")
                            P.op("dve", lambda en: en.tensor_sub(zst[:, 56:64], zst[:, 0:8], zst[:, 32:40]),
                                 reads=[bzst], update=[bzst], tag=f"zmg{w}e")
                            P.op("act", lambda en: en.activation(out=zst[:, 56:64], in_=zst[:, 56:64], func=AF.Exp),
                                 reads=[bzst], update=[bzst], tag=f"zmg{w}f")
                            P.op("dve", lambda en: en.tensor_mul(zst[:, 48:56], zst[:, 8:16], zst[:, 56:64]),
                                 reads=[bzst], update=[bzst], tag=f"zmg{w}g")
                            P.op("dve", lambda en: en.tensor_add(zst[:, 24:32], zst[:, 24:32], zst[:, 48:56]),
                                 reads=[bzst], update=[bzst], tag=f"zmg{w}h")
                            P.op("dve", lambda en: en.tensor_copy(zst[:, 16:24], zst[:, 32:40]),
                                 reads=[bzst], update=[bzst], tag=f"zmg{w}i")
                    P.op("act", lambda en: en.activation(out=zst[:, 56:64], in_=zst[:, 24:32], func=AF.Ln),
                         reads=[bzst], update=[bzst], tag="zlog")
                    P.op("dve", lambda en: en.tensor_add(zst[:, 0:8], zst[:, 16:24], zst[:, 56:64]),
                         reads=[bzst], update=[bzst], tag="zfin")
                    P.op("dve", lambda en: en.tensor_scalar_mul(zneg8[:], zst[:, 0:8], -1.0),
                         reads=[bzst], writes=[bzneg], tag="znegf")
                    P.fence([bwst[0], bwst[1], bwst[2], bwst[3], bswin, bzdump] + bxnw, tag="p1f")

                wqk = ae(nc.sbuf_tensor("a_wqk", [128, 2 * 2048], BF16))
                wos = ae(nc.sbuf_tensor("a_wos", [128, 4 * 2048], BF16))
                wvg = ae(nc.sbuf_tensor("a_wvg", [128, 2 * 1024], BF16))
                q_win = ae(nc.sbuf_tensor("a_qwin", [64, HA * 512], BF16))
                k_win = ae(nc.sbuf_tensor("a_kwin", [64, HA * 512], BF16))
                v_win = ae(nc.sbuf_tensor("a_vwin", [128, 4 * 1024], BF16))
                g_win = ae(nc.sbuf_tensor("a_gwin", [128, 4 * 1024], BF16))
                s_ch = ae(nc.sbuf_tensor("a_sch", [64, HA * 128], F32))
                E_f = ae(nc.sbuf_tensor("a_Ef", [64, HA * 128], F32))
                E_bf = ae(nc.sbuf_tensor("a_Ebf", [64, HA * 128], BF16))
                k_f32 = ae(nc.sbuf_tensor("a_kf32", [64, HA * 128], F32))
                D_s = ae(nc.sbuf_tensor("a_Ds", [64, HA * 128], F32))
                R_t = ae(nc.sbuf_tensor("a_Rt", [64, HA * 128], F32))
                okh = ae(nc.sbuf_tensor("a_okh", [64, HA * 128], F32))
                pv_bf = ae(nc.sbuf_tensor("a_pv", [64, HA * 128], BF16))
                p_hat = ae(nc.sbuf_tensor("a_phat", [64, HA * 128], BF16))
                P_tm = ae(nc.sbuf_tensor("a_Ptm", [128, HA * 64], BF16))
                k_tm = ae(nc.sbuf_tensor("a_ktm", [128, HA * 64], BF16))
                ST_m = ae(nc.sbuf_tensor("a_STm", [128, HA * 128], BF16))
                Wt_m = ae(nc.sbuf_tensor("a_Wtm", [128, HA * 128], BF16))
                HK_f = ae(nc.sbuf_tensor("a_HKf", [64, HA * 64], F32))
                HK_bf = ae(nc.sbuf_tensor("a_HKbf", [64, HA * 64], BF16))
                HV_f = ae(nc.sbuf_tensor("a_HVf", [64, HA * 128], F32))
                HV_bf = ae(nc.sbuf_tensor("a_HVbf", [64, HA * 128], BF16))
                carry = ae(nc.sbuf_tensor("a_carry", [64, 8], F32))
                ones_scan = ae(nc.sbuf_tensor("a_ones", [64, 128], F32))
                sigf = ae(nc.sbuf_tensor("a_sigf", [128, 48], F32))
                o2d = ae(nc.sbuf_tensor("a_o2d", [128, 128], BF16))
                o_g = ae(nc.sbuf_tensor("a_og", [128, HA * 128], F32))
                og_fm = ae(nc.sbuf_tensor("a_ogfm", [128, HA * 512], BF16))
                stg = ae(nc.sbuf_tensor("a_stg", [128, 512], F32))

                bwqk = [P.buf("wqk0"), P.buf("wqk1")]
                bwos = [P.buf(f"wos{i}") for i in range(4)]
                bwvg = [P.buf("wvg0"), P.buf("wvg1")]
                bqw, bkw, bvw, bgw = P.buf("qwin"), P.buf("kwin"), P.buf("vwin"), P.buf("gwin")
                bsch = [P.buf("sch0")]
                bEf, bEbf, bkf32 = P.buf("Ef"), P.buf("Ebf"), P.buf("kf32")
                bDs, bRt = P.buf("Ds"), P.buf("Rt")
                bokh, bpv, bph = P.buf("okh"), P.buf("pv"), P.buf("phat")
                bPtm, bktm, bSTm, bWtm = P.buf("Ptm"), P.buf("ktm"), P.buf("STm"), P.buf("Wtm")
                bHKf, bHKbf, bHVf, bHVbf = P.buf("HKf"), P.buf("HKbf"), P.buf("HVf"), P.buf("HVbf")
                bcarry, bones_s, bsigf = P.buf("carry"), P.buf("ones_s"), P.buf("sigf")
                bo2, bog, bogfm = P.buf("o2"), P.buf("og"), P.buf("ogfm")
                bstg = [P.buf("stg0")]

                P.op("dve", lambda en: en.memset(ones_scan[:], 1.0), writes=[bones_s], tag="amemset_ones")
                P.op("dve", lambda en: en.memset(carry[:], 0.0), writes=[bcarry], tag="amemset_c")
                P.op("dve", lambda en: en.memset(HK_f[:], 0.0), writes=[bHKf], tag="amemset_hkf")
                P.op("dve", lambda en: en.memset(HK_bf[:], 0.0), writes=[bHKbf], tag="amemset_hkb")
                P.op("dve", lambda en: en.memset(HV_f[:], 0.0), writes=[bHVf], tag="amemset_hvf")
                P.op("dve", lambda en: en.memset(HV_bf[:], 0.0), writes=[bHVbf], tag="amemset_hvb")

                for w in range(4):
                    gather_xnw(w, f"p2x{w}")
                    for (wname, wslab, bslab, dwin, bwin, act_silu) in [
                            ("wv_a", wvg, bwvg, v_win, bvw, False),
                            ("wg_a", wvg, bwvg, g_win, bgw, True)]:
                        for c in range(NCH):
                            cp = c % 2
                            P.op("pool", lambda en, c=c, cp=cp, wname=wname, wslab=wslab: en.dma_start(
                                out=wslab[:, cp * 1024:(cp + 1) * 1024],
                                in_=din[wname][c * 128:(c + 1) * 128, :]),
                                writes=[bslab[cp]], dma=True, tag=f"{wname}s{w}_{c}")
                            for ridx in range(8):
                                tc4, hf = ridx // 2, ridx % 2
                                reg = (ps1 if ridx < 4 else ps2)[:, (ridx % 4) * 512:(ridx % 4 + 1) * 512]
                                rb = (pb1 if ridx < 4 else pb2)[ridx % 4]
                                P.op("pe", lambda en, c=c, cp=cp, tc4=tc4, hf=hf, reg=reg, wslab=wslab:
                                     en.matmul(reg, xnw[:, c * 512 + tc4 * 128:c * 512 + tc4 * 128 + 128],
                                               wslab[:, cp * 1024 + hf * 512:cp * 1024 + hf * 512 + 512],
                                               start=(c == 0), stop=(c == NCH - 1)),
                                     reads=[bxnw[c], bslab[cp]],
                                     writes=[rb] if c == 0 else (), update=[rb] if c == NCH - 1 else (),
                                     tag=f"{wname}mm{w}_{c}_{ridx}")
                        for ridx in range(8):
                            tc4, hf = ridx // 2, ridx % 2
                            reg = (ps1 if ridx < 4 else ps2)[:, (ridx % 4) * 512:(ridx % 4 + 1) * 512]
                            rb = (pb1 if ridx < 4 else pb2)[ridx % 4]
                            dst = dwin[:, tc4 * 1024 + hf * 512:tc4 * 1024 + hf * 512 + 512]
                            if act_silu:
                                P.op("act", lambda en, reg=reg, dst=dst: en.activation(out=dst, in_=reg, func=AF.Silu),
                                     reads=[rb], writes=[bwin] if ridx == 0 else (), update=[bwin] if ridx == 7 else (),
                                     tag=f"gsilu{w}_{ridx}")
                            else:
                                P.op("dve", lambda en, reg=reg, dst=dst: en.tensor_copy(dst, reg),
                                     reads=[rb], writes=[bwin] if ridx == 0 else (), update=[bwin] if ridx == 7 else (),
                                     tag=f"vevac{w}_{ridx}")
                    for (wname, wslab, bslab, dwin, bwin, scl) in [
                            ("wq_a", wqk, bwqk, q_win, bqw, QSCALE),
                            ("wk_a", wqk, bwqk, k_win, bkw, 1.0)]:
                        for pp in range(4):
                            par = pp % 2
                            P.op("pool", lambda en, pp=pp, par=par, wname=wname, wslab=wslab: en.dma_start(
                                out=wslab[:, par * 2048:(par + 1) * 2048],
                                in_=din[wname][:, pp * 2048:(pp + 1) * 2048]),
                                writes=[bslab[par]], dma=True, tag=f"{wname}l{w}_{pp}")
                            for hl in range(2):
                                h = pp * 2 + hl
                                reg = ps2[0:64, (h % 4) * 512:(h % 4 + 1) * 512]
                                rb = pb2[h % 4]
                                for c in range(NCH):
                                    P.op("pe", lambda en, h=h, c=c, par=par, reg=reg, wslab=wslab, hl=hl:
                                         en.matmul(reg, wslab[:, par * 2048 + c * 128 + hl * 64:par * 2048 + c * 128 + hl * 64 + 64],
                                                   xnw[:, c * 512:(c + 1) * 512],
                                                   start=(c == 0), stop=(c == NCH - 1)),
                                         reads=[bslab[par], bxnw[c]],
                                         writes=[rb] if c == 0 else (), update=[rb] if c == NCH - 1 else (),
                                         tag=f"{wname}mm{w}_{h}_{c}")
                                P.op("act", lambda en, h=h, reg=reg, dwin=dwin, scl=scl: en.activation(
                                    out=dwin[:, h * 512:(h + 1) * 512], in_=reg, func=AF.Identity, scale=scl),
                                    reads=[rb], writes=[bwin] if h == 0 else (), update=[bwin] if h == 7 else (),
                                    tag=f"{wname}ev{w}_{h}")

                    for tc4 in range(4):
                        cg = w * 4 + tc4
                        par = 0
                        schv = s_ch[:]
                        P.op("sp", lambda en, cg=cg, schv=schv: en.dma_start(
                            out=schv.rearrange("m (h t) -> m h t", h=HA),
                            in_=s_spill.rearrange("m (h t) -> m h t", h=HA)[:, :, cg * 128:(cg + 1) * 128]),
                            reads=[bspill[cg // 4]], writes=[bsch[par]], dma=True, tag=f"sld{cg}")
                        for h in range(HA):
                            P.op("act", lambda en, h=h, schv=schv: en.activation(
                                out=E_f[:, h * 128:(h + 1) * 128], in_=schv[:, h * 128:(h + 1) * 128],
                                func=AF.Exp, bias=zneg8[:, h:h + 1]),
                                reads=[bsch[par], bzneg],
                                writes=[bEf] if h == 0 else (), update=[bEf] if h == HA - 1 else (),
                                tag=f"E{cg}_{h}")
                        P.op("dve", lambda en: en.tensor_copy(E_bf[:], E_f[:]),
                             reads=[bEf], writes=[bEbf], tag=f"Ebf{cg}")
                        P.op("dve", lambda en, tc4=tc4: en.tensor_copy(
                            k_f32.ap().rearrange("m (h t) -> m h t", h=HA),
                            k_win.ap().rearrange("m (h t) -> m h t", h=HA)[:, :, tc4 * 128:(tc4 + 1) * 128]),
                            reads=[bkw], writes=[bkf32], tag=f"kf32{cg}")
                        for h in range(HA):
                            P.op("dve", lambda en, h=h: en.tensor_tensor_scan(
                                out=D_s[:, h * 128:(h + 1) * 128], data0=ones_scan[:],
                                data1=E_f[:, h * 128:(h + 1) * 128], initial=carry[:, h:h + 1],
                                op0=ALU.mult, op1=ALU.add),
                                reads=[bEf, bones_s, bcarry],
                                writes=[bDs] if h == 0 else (), update=[bDs] if h == HA - 1 else (),
                                tag=f"scan{cg}_{h}")
                        P.op("dve", lambda en: en.tensor_copy(
                            carry[:], D_s.ap().rearrange("m (h t) -> m h t", h=HA)[:, :, 127]),
                            reads=[bDs], writes=[bcarry], tag=f"carry{cg}")
                        P.op("act", lambda en: en.activation(out=okh[:], in_=D_s[:], func=AF.Ln),
                             reads=[bDs], writes=[bokh], tag=f"lnD{cg}")
                        P.op("act", lambda en: en.activation(out=R_t[:], in_=okh[:], func=AF.Exp, scale=-1.0),
                             reads=[bokh], writes=[bRt], tag=f"R{cg}")
                        for h in range(HA):
                            P.op("pe", lambda en, h=h: en.transpose(
                                ps1[:, h * 64:(h + 1) * 64], E_f[:, h * 128:(h + 1) * 128], identf[0:64, 0:64]),
                                reads=[bEf, bconsts["identf"]],
                                writes=[pb1[0]] if h == 0 else (),
                                update=[pb1[0]] if h == HA - 1 else (), tag=f"trP{cg}_{h}")
                        for h in range(HA):
                            P.op("pe", lambda en, h=h: en.transpose(
                                ps1[:, 512 + h * 64:512 + (h + 1) * 64], k_f32[:, h * 128:(h + 1) * 128], identf[0:64, 0:64]),
                                reads=[bkf32, bconsts["identf"]],
                                writes=[pb1[1]] if h == 0 else (),
                                update=[pb1[1]] if h == HA - 1 else (), tag=f"trK{cg}_{h}")
                        P.op("dve", lambda en: en.tensor_copy(P_tm[:], ps1[:, 0:512]),
                             reads=[pb1[0]], writes=[bPtm], tag=f"Ptm{cg}")
                        P.op("dve", lambda en: en.tensor_copy(k_tm[:], ps1[:, 512:1024]),
                             reads=[pb1[1]], writes=[bktm], tag=f"ktm{cg}")
                        for h in range(HA):
                            P.op("pe", lambda en, h=h, tc4=tc4: en.matmul(
                                ps1[:, h * 128:(h + 1) * 128],
                                k_win[:, h * 512 + tc4 * 128:h * 512 + tc4 * 128 + 128],
                                q_win[:, h * 512 + tc4 * 128:h * 512 + tc4 * 128 + 128],
                                start=True, stop=True),
                                reads=[bkw, bqw],
                                writes=[pb1[0], pb1[1]] if h == 0 else (),
                                update=[pb1[0], pb1[1]] if h == HA - 1 else (), tag=f"ST{cg}_{h}")
                        P.op("dve", lambda en: en.tensor_mul(ST_m[:], ps1[:, 0:1024], lmask8[:]),
                             reads=[pb1[0], pb1[1], bconsts["lmask8"]], writes=[bSTm], tag=f"STm{cg}")
                        for h in range(HA):
                            P.op("pe", lambda en, h=h: en.matmul(
                                ps2[0:64, h * 128:(h + 1) * 128], P_tm[:, h * 64:(h + 1) * 64],
                                ST_m[:, h * 128:(h + 1) * 128], start=True, stop=False),
                                reads=[bPtm, bSTm],
                                writes=[pb2[0], pb2[1]] if h == 0 else (), tag=f"oki{cg}_{h}")
                            P.op("pe", lambda en, h=h, tc4=tc4: en.matmul(
                                ps2[0:64, h * 128:(h + 1) * 128], HK_bf[:, h * 64:(h + 1) * 64],
                                q_win[:, h * 512 + tc4 * 128:h * 512 + tc4 * 128 + 128],
                                start=False, stop=True),
                                reads=[bHKbf, bqw],
                                update=[pb2[0], pb2[1]] if h == HA - 1 else (), tag=f"okc{cg}_{h}")
                        P.op("dve", lambda en: en.tensor_mul(okh[:], ps2[0:64, 0:1024], R_t[:]),
                             reads=[pb2[0], pb2[1], bRt], writes=[bokh], tag=f"okh{cg}")
                        P.op("act", lambda en: en.activation(out=pv_bf[:], in_=okh[:], func=AF.Exp),
                             reads=[bokh], writes=[bpv], tag=f"pv{cg}")
                        P.op("dve", lambda en: en.tensor_mul(p_hat[:], pv_bf[:], R_t[:]),
                             reads=[bpv, bRt], writes=[bph], tag=f"ph{cg}")
                        for h in range(HA):
                            P.op("pe", lambda en, h=h: en.matmul(
                                ps2[:, 2040 + h:2041 + h], pv_bf[:, h * 128:(h + 1) * 128],
                                ones_bf[0:64, :], start=True, stop=True),
                                reads=[bpv, bconsts["ones_bf"]],
                                writes=[pb2[3]] if h == 0 else (), update=[pb2[3]] if h == HA - 1 else (),
                                tag=f"sig{cg}_{h}")
                        for h in range(HA):
                            P.op("pe", lambda en, h=h: en.matmul(
                                ps1[:, h * 128:(h + 1) * 128], E_bf[:, h * 128:(h + 1) * 128],
                                p_hat[:, h * 128:(h + 1) * 128], start=True, stop=True),
                                reads=[bEbf, bph],
                                writes=[pb1[0], pb1[1]] if h == 0 else (),
                                update=[pb1[0], pb1[1]] if h == HA - 1 else (), tag=f"W{cg}_{h}")
                        P.op("dve", lambda en: en.tensor_mul(Wt_m[:], ps1[:, 0:1024], lmask8[:]),
                             reads=[pb1[0], pb1[1], bconsts["lmask8"]], writes=[bWtm], tag=f"Wtm{cg}")
                        for h in range(HA):
                            P.op("pe", lambda en, h=h, tc4=tc4: en.matmul(
                                ps2[:, h * 128:(h + 1) * 128], Wt_m[:, h * 128:(h + 1) * 128],
                                v_win[:, tc4 * 1024 + h * 128:tc4 * 1024 + h * 128 + 128],
                                start=True, stop=False),
                                reads=[bWtm, bvw],
                                writes=[pb2[0], pb2[1]] if h == 0 else (), tag=f"oi{cg}_{h}")
                            P.op("pe", lambda en, h=h: en.matmul(
                                ps2[:, h * 128:(h + 1) * 128], p_hat[:, h * 128:(h + 1) * 128],
                                HV_bf[:, h * 128:(h + 1) * 128], start=False, stop=True),
                                reads=[bph, bHVbf],
                                update=[pb2[0], pb2[1]] if h == HA - 1 else (), tag=f"oc{cg}_{h}")
                        P.op("act", lambda en: en.activation(out=sigf[:, 0:8], in_=ps2[:, 2040:2048], func=AF.Square),
                             reads=[pb2[3]], writes=[bsigf], tag=f"sg2{cg}")
                        for h in range(HA):
                            P.op("act", lambda en, h=h: en.activation(
                                out=o2d[:], in_=ps2[:, h * 128:(h + 1) * 128], func=AF.Square,
                                accum_out=sigf[:, 8 + h:9 + h]),
                                reads=[pb2[0], pb2[1]], writes=[bo2], update=[bsigf], tag=f"ss{cg}_{h}")
                        P.op("dve", lambda en: en.tensor_scalar_mul(sigf[:, 16:24], sigf[:, 8:16], 1.0 / DV),
                             reads=[bsigf], update=[bsigf], tag=f"f1{cg}")
                        P.op("dve", lambda en: en.scalar_tensor_tensor(
                            out=sigf[:, 24:32], in0=sigf[:, 0:8], scalar=A_EPS, in1=sigf[:, 16:24],
                            op0=ALU.mult, op1=ALU.add),
                            reads=[bsigf], update=[bsigf], tag=f"f2{cg}")
                        P.op("act", lambda en: en.activation(out=sigf[:, 32:40], in_=sigf[:, 24:32], func=AF.Ln),
                             reads=[bsigf], update=[bsigf], tag=f"f3{cg}")
                        P.op("act", lambda en: en.activation(out=sigf[:, 32:40], in_=sigf[:, 32:40], func=AF.Exp, scale=-0.5),
                             reads=[bsigf], update=[bsigf], tag=f"f4{cg}")
                        for h in range(HA):
                            P.op("dve", lambda en, h=h, tc4=tc4: en.scalar_tensor_tensor(
                                out=o_g[:, h * 128:(h + 1) * 128], in0=ps2[:, h * 128:(h + 1) * 128],
                                scalar=sigf[:, 32 + h:33 + h],
                                in1=g_win[:, tc4 * 1024 + h * 128:tc4 * 1024 + h * 128 + 128],
                                op0=ALU.mult, op1=ALU.mult),
                                reads=[pb2[0], pb2[1], bsigf, bgw],
                                writes=[bog] if h == 0 else (), update=[bog] if h == HA - 1 else (),
                                tag=f"og{cg}_{h}")
                        for h in range(HA):
                            P.op("pe", lambda en, h=h: en.matmul(
                                ps2[0:64, 1024 + h * 64:1024 + (h + 1) * 64], k_tm[:, h * 64:(h + 1) * 64],
                                P_tm[:, h * 64:(h + 1) * 64], start=True, stop=True),
                                reads=[bktm, bPtm],
                                writes=[pb2[2]] if h == 0 else (), update=[pb2[2]] if h == HA - 1 else (),
                                tag=f"dhk{cg}_{h}")
                            P.op("pe", lambda en, h=h, tc4=tc4: en.matmul(
                                ps1[0:64, 1024 + h * 128:1024 + (h + 1) * 128], P_tm[:, h * 64:(h + 1) * 64],
                                v_win[:, tc4 * 1024 + h * 128:tc4 * 1024 + h * 128 + 128],
                                start=True, stop=True),
                                reads=[bPtm, bvw],
                                writes=[pb1[2], pb1[3]] if h == 0 else (),
                                update=[pb1[2], pb1[3]] if h == HA - 1 else (), tag=f"dhv{cg}_{h}")
                        P.op("dve", lambda en: en.tensor_add(HK_f[:], HK_f[:], ps2[0:64, 1024:1536]),
                             reads=[pb2[2]], writes=[bHKf], tag=f"hkadd{cg}")
                        P.op("dve", lambda en: en.tensor_add(HV_f[:], HV_f[:], ps1[0:64, 1024:2048]),
                             reads=[pb1[2], pb1[3]], writes=[bHVf], tag=f"hvadd{cg}")
                        P.op("dve", lambda en: en.tensor_copy(HK_bf[:], HK_f[:]),
                             reads=[bHKf], writes=[bHKbf], tag=f"hkbf{cg}")
                        P.op("dve", lambda en: en.tensor_copy(HV_bf[:], HV_f[:]),
                             reads=[bHVf], writes=[bHVbf], tag=f"hvbf{cg}")
                        for h in range(HA):
                            P.op("pe", lambda en, h=h: en.transpose(
                                ps1[:, h * 128:(h + 1) * 128], o_g[:, h * 128:(h + 1) * 128], identf[:]),
                                reads=[bog, bconsts["identf"]],
                                writes=[pb1[0], pb1[1]] if h == 0 else (),
                                update=[pb1[0], pb1[1]] if h == HA - 1 else (),
                                tag=f"trO{cg}_{h}")
                        P.op("dve", lambda en, tc4=tc4: en.tensor_copy(
                            og_fm.ap().rearrange("p (h t) -> p h t", h=HA)[:, :, tc4 * 128:(tc4 + 1) * 128],
                            ps1.ap().rearrange("p (h t) -> p h t", h=16)[:, 0:8, :]),
                            reads=[pb1[0], pb1[1]],
                            writes=[bogfm] if tc4 == 0 else (), update=[bogfm] if tc4 == 3 else (),
                            tag=f"ogfm{cg}")
                    for half in range(2):
                        for hh_ in range(4):
                            hglob = half * 4 + hh_
                            P.op("pool", lambda en, hglob=hglob, hh_=hh_: en.dma_start(
                                out=wos[:, hh_ * 2048:(hh_ + 1) * 2048],
                                in_=din["wo_a"][hglob * 128:(hglob + 1) * 128, :]),
                                writes=[bwos[hh_]], dma=True, tag=f"ldwo{w}_{half}_{hh_}")
                        for hh_ in range(4):
                            P.op("dve", lambda en, hh_=hh_: en.tensor_scalar_mul(
                                wos[:, hh_ * 2048:(hh_ + 1) * 2048], wos[:, hh_ * 2048:(hh_ + 1) * 2048], gnw[:]),
                                reads=[bconsts["gnw"]], writes=[bwos[hh_]], tag=f"gnwf{w}_{half}_{hh_}")
                        for dc in range(16):
                            reg = ps2[:, (dc % 2) * 512:(dc % 2) * 512 + 512]
                            rb = pb2[dc % 2]
                            for hh_ in range(4):
                                hglob = half * 4 + hh_
                                P.op("pe", lambda en, dc=dc, hh_=hh_, hglob=hglob, reg=reg: en.matmul(
                                    reg, wos[:, hh_ * 2048 + dc * 128:hh_ * 2048 + (dc + 1) * 128],
                                    og_fm[:, hglob * 512:(hglob + 1) * 512], start=(hh_ == 0), stop=(hh_ == 3)),
                                    reads=[bwos[hh_], bogfm],
                                    writes=[rb] if hh_ == 0 else (), update=[rb] if hh_ == 3 else (),
                                    tag=f"wo{w}_{half}_{dc}_{hh_}")
                            sreg = stg[:]
                            if half == 0:
                                P.op("act", lambda en, reg=reg, sreg=sreg: en.activation(out=sreg, in_=reg, func=AF.Identity),
                                     reads=[rb], writes=[bstg[0]], tag=f"woev{w}_{half}_{dc}")
                                pt_toks[w % 2].append(P.op("sp", lambda en, w=w, dc=dc, sreg=sreg: en.dma_start(
                                    out=ag_pt_in2[w % 2][(w // 2) * D + dc * 128:(w // 2) * D + (dc + 1) * 128, :],
                                    in_=sreg),
                                    reads=[bstg[0]], writes=[bag_ptr[w][dc]], dma=True, tag=f"wodma{w}_{half}_{dc}"))
                            else:
                                P.op("act", lambda en, reg=reg, sreg=sreg: en.activation(out=sreg, in_=reg, func=AF.Identity),
                                     reads=[rb], writes=[bstg[0]], tag=f"woev{w}_{half}_{dc}")
                                pt_toks[w % 2].append(P.op("pool", lambda en, w=w, dc=dc, sreg=sreg: en.dma_start(
                                    out=ag_pt_in2[w % 2][(w // 2) * D + dc * 128:(w // 2) * D + (dc + 1) * 128, :],
                                    in_=sreg, accum_op=ALU.add),
                                    reads=[bstg[0]], writes=[bag_ptr[w][dc]], dma=True, tag=f"wodma{w}_{half}_{dc}"))
                    if w == 2:
                        emit_rs(0, "ag_pt0")

                P.fence([bqw, bkw, bvw, bgw, bEf, bEbf, bkf32, bDs, bRt, bokh, bpv, bph,
                         bPtm, bktm, bSTm, bWtm, bHKf, bHKbf, bHVf, bHVbf, bcarry, bones_s,
                         bsigf, bo2, bog, bogfm, bstg[0], bwos[0], bwos[1], bwos[2], bwos[3],
                         bsch[0], bzst, bzneg] + bwqk + bwvg + bxnw, tag="a2f")
            emit_rs(1, "ag_pt1")
            with ExitStack() as rctx:
                re_ = rctx.enter_context
                rb0 = re_(nc.sbuf_tensor("a_rb0", [128, 2 * 4 * 512], F32))
                brb0 = [[P.buf(f"rb0{i}_{ch}") for ch in range(2)] for i in range(2)]
                for colh in range(2):
                    for c0 in range(0, NCH, 4):
                        gp = (c0 // 4) % 2
                        r0g = rb0.ap().rearrange("p (ch g t) -> p ch g t", ch=2, g=4)[:, gp]
                        P.op("sp" if colh == 0 else "pool", lambda en, c0=c0, colh=colh, r0g=r0g: en.dma_start(
                            out=r0g,
                            in_=ag_pt_out2[colh].rearrange("(c p) t -> p c t", p=128)[:, c0:c0 + 4, :]),
                            reads=[bag_pt_out2[colh]], writes=[brb0[gp][colh]], dma=True, tag=f"rb0_{c0}_{colh}")
                        for ci in range(4):
                            c = c0 + ci
                            sl = slice(c * TC + colh * 512, c * TC + colh * 512 + 512)
                            r0v = rb0[:, gp * 4 * 512 + ci * 512:gp * 4 * 512 + (ci + 1) * 512]
                            P.op("dve", lambda en, sl=sl, r0v=r0v: en.tensor_add(x_res[:, sl], x_res[:, sl], r0v),
                                 reads=[brb0[gp][colh]], writes=[bxc[c]], tag=f"res0_{c}_{colh}")
                P.fence([b for row in brb0 for b in row], tag="rbf")

        def ffn_sublayer(tag):
            with ExitStack() as fctx:
                fe = fctx.enter_context
                if conv_state["i"] < len(conv_jobs) and conv_state["stg"] is None:
                    conv_state["stg"] = fe(nc.sbuf_tensor(tag + "_cstg", [128, 2 * 1024], BF16))

                def wsrc3(name, par):
                    if conv_done(name):
                        return dcache[name], ("sp" if par == 0 else "pool"), [bcache[name]]
                    return din[name], "pool", []
                xn_f = fe(nc.sbuf_tensor(tag + "_xn", [128, NCH * TC], BF16))
                xn1_f = fe(nc.sbuf_tensor(tag + "_xn1", [128, NCH * TC], BF16))
                bxnf = [P.buf(tag + f"xnf{c}") for c in range(NCH)]
                bxn1 = [P.buf(tag + f"xn1{c}") for c in range(NCH)]
                layernorm(tag + "ln", xn_f, bxnf)
                wg8 = fe(nc.sbuf_tensor(tag + "_wg8", [128, 256], BF16))
                bgc = fe(nc.sbuf_tensor(tag + "_bg", [8, 2], F32))
                bupc = fe(nc.sbuf_tensor(tag + "_bup", [128, 48], F32))
                bswc = fe(nc.sbuf_tensor(tag + "_bsw", [128, 48], F32))
                bsac = fe(nc.sbuf_tensor(tag + "_bsa", [128, 8], F32))
                bsuc = fe(nc.sbuf_tensor(tag + "_bsu", [128, 16], F32))
                bdn6 = fe(nc.sbuf_tensor(tag + "_bdn6", [6, 512], BF16))
                bsd1 = fe(nc.sbuf_tensor(tag + "_bsd1", [1, 512], BF16))
                ones512 = fe(nc.sbuf_tensor(tag + "_ones512", [1, 512], BF16))
                g8r = fe(nc.sbuf_tensor(tag + "_g8r", [8, 512], F32))
                gtm = fe(nc.sbuf_tensor(tag + "_gtm", [128, 8], F32))
                gsc = fe(nc.sbuf_tensor(tag + "_gsc", [128, 112], F32))
                w_fm = fe(nc.sbuf_tensor(tag + "_wfm", [8, TC], F32))
                w_bf = fe(nc.sbuf_tensor(tag + "_wbf", [8, TC], BF16))
                wb_n = fe(nc.sbuf_tensor(tag + "_wbn", [128, TC], BF16))
                u_n = fe(nc.sbuf_tensor(tag + "_un", [128, 8 * TC], BF16))
                silu_t = fe(nc.sbuf_tensor(tag + "_silu", [128, 512], BF16))
                t2b = fe(nc.sbuf_tensor(tag + "_t2", [128, 512], BF16))
                hst = fe(nc.sbuf_tensor(tag + "_hst", [128, 2 * 512], BF16))
                wups = fe(nc.sbuf_tensor(tag + "_wus", [128, 2 * 2048], BF16))
                wsws = wsus = wsas = wups
                wdns = fe(nc.sbuf_tensor(tag + "_wds", [128, 2 * 2048], BF16))
                wsds = wdns
                h_n = fe(nc.sbuf_tensor(tag + "_hn", [128, 8 * TC], BF16))
                shl = fe(nc.sbuf_tensor(tag + "_shl", [128, 2 * TC], BF16))
                xcs = fe(nc.sbuf_tensor(tag + "_xcs", [128, TC], F32)) if False else None

                bwg8, bbg = P.buf(tag + "wg8"), P.buf(tag + "bg")
                bbup, bbsw, bbsa, bbsu = P.buf(tag + "bup"), P.buf(tag + "bsw"), P.buf(tag + "bsa"), P.buf(tag + "bsu")
                bbdn6, bbsd1, bones512 = P.buf(tag + "bdn6"), P.buf(tag + "bsd1"), P.buf(tag + "o512")
                bg8r, bgtm, bgsc = P.buf(tag + "g8r"), P.buf(tag + "gtm"), P.buf(tag + "gsc")
                bwfm, bwbf, bwbn = P.buf(tag + "wfm"), P.buf(tag + "wbf"), P.buf(tag + "wbn")
                bun, bsilu, bt2, bhst = P.buf(tag + "un"), P.buf(tag + "silu"), P.buf(tag + "t2"), P.buf(tag + "hst")
                bwup = [P.buf(tag + "wus0"), P.buf(tag + "wus1")]
                bwsw = bwsu = bwsa = bwup
                bwdn = [P.buf(tag + "wds0"), P.buf(tag + "wds1")]
                bwsd = bwdn
                bhn = [P.buf(tag + "hn0"), P.buf(tag + "hn1")]
                bshl = [P.buf(tag + "shl0"), P.buf(tag + "shl1")]
                us_ctr = [0]
                ds_ctr = [0]
                bhsp = [[P.buf(tag + f"hsp{n}_{f}") for f in range(8)] for n in range(6)]
                bssp = [P.buf(tag + f"ssp{i}") for i in range(16)]

                P.op("pool", lambda en: en.dma_start(out=wg8[:], in_=din["f_wg_t"]), writes=[bwg8], dma=True)
                P.op("sp", lambda en: en.dma_start(out=bgc[:], in_=din["f_bg_t"]), writes=[bbg], dma=True)
                P.op("sp", lambda en: en.dma_start(out=bupc[:], in_=din["f_bup_t"]), writes=[bbup], dma=True)
                P.op("sp", lambda en: en.dma_start(out=bswc[:], in_=din["f_bsw_t"]), writes=[bbsw], dma=True)
                P.op("sp", lambda en: en.dma_start(out=bsac[:], in_=din["f_bsa_t"]), writes=[bbsa], dma=True)
                P.op("sp", lambda en: en.dma_start(out=bsuc[:], in_=din["f_bsu_t"]), writes=[bbsu], dma=True)

                P.op("dve", lambda en: en.memset(ones512[:], 1.0), writes=[bones512], tag=tag + "o512")

                for j in range(2):
                    xin = xn_f if j == 0 else xn1_f
                    bxin = bxnf if j == 0 else bxn1
                    for hf in range(2):
                        reg = ps1[0:8, hf * 512:(hf + 1) * 512]
                        for c in range(NCH):
                            P.op("pe", lambda en, j=j, hf=hf, c=c, reg=reg, xin=xin: en.matmul(
                                reg, wg8[:, j * 128 + c * 8:j * 128 + (c + 1) * 8],
                                xin[:, c * TC + hf * 512:c * TC + hf * 512 + 512],
                                start=(c == 0), stop=(c == NCH - 1)),
                                reads=[bwg8, bxin[c]],
                                writes=[pb1[hf]] if c == 0 else (), update=[pb1[hf]] if c == NCH - 1 else (),
                                tag=f"{tag}g{j}_{hf}_{c}")
                        P.op("act", lambda en, j=j, hf=hf, reg=reg: en.activation(
                            out=g8r[:], in_=reg, func=AF.Identity,
                            bias=bgc[:, j:j + 1]),
                            reads=[pb1[hf], bbg], writes=[bg8r],
                            tag=f"{tag}gev{j}_{hf}")
                        for sl4 in range(4):
                            sl8 = hf * 4 + sl4
                            P.op("pe", lambda en, sl4=sl4, sl8=sl8: en.transpose(
                                ps1[:, 1024 + sl8 * 8:1024 + (sl8 + 1) * 8],
                                g8r[0:8, sl4 * 128:(sl4 + 1) * 128], identf[0:8, 0:8]),
                                reads=[bg8r, bconsts["identf"]],
                                writes=[pb1[2]] if sl8 == 0 else (), update=[pb1[2]] if sl8 == 7 else (),
                                tag=f"{tag}gt{j}_{sl8}")
                    for sl8 in range(8):
                        P.op("dve", lambda en, sl8=sl8: en.tensor_copy(gtm[:], ps1[:, 1024 + sl8 * 8:1024 + (sl8 + 1) * 8]),
                             reads=[pb1[2]], writes=[bgtm], tag=f"{tag}gc{j}_{sl8}")
                        P.op("dve", lambda en: en.tensor_reduce(gsc[:, 0:1], gtm[:], axis=mybir.AxisListType.X, op=ALU.min),
                             reads=[bgtm], writes=[bgsc], tag=f"{tag}m1{j}_{sl8}")
                        P.op("dve", lambda en: en.tensor_single_scalar(gsc[:, 8:16], gtm[:], gsc[:, 0:1], ALU.is_equal),
                             reads=[bgtm, bgsc], update=[bgsc], tag=f"{tag}eq{j}_{sl8}")
                        P.op("dve", lambda en: en.scalar_tensor_tensor(
                            out=gsc[:, 16:24], in0=gsc[:, 8:16], scalar=1e30, in1=gtm[:], op0=ALU.mult, op1=ALU.add),
                            reads=[bgsc, bgtm], update=[bgsc], tag=f"{tag}gb{j}_{sl8}")
                        P.op("dve", lambda en: en.tensor_reduce(gsc[:, 1:2], gsc[:, 16:24], axis=mybir.AxisListType.X, op=ALU.min),
                             reads=[bgsc], update=[bgsc], tag=f"{tag}m2{j}_{sl8}")
                        P.op("dve", lambda en: en.tensor_single_scalar(gsc[:, 8:16], gtm[:], gsc[:, 1:2], ALU.is_gt),
                             reads=[bgtm, bgsc], update=[bgsc], tag=f"{tag}kp{j}_{sl8}")
                        P.op("dve", lambda en: en.tensor_scalar(
                            out=gsc[:, 16:24], in0=gsc[:, 8:16], scalar1=1.0, scalar2=60.0,
                            op0=ALU.subtract, op1=ALU.mult),
                            reads=[bgsc], update=[bgsc], tag=f"{tag}pn{j}_{sl8}")
                        P.op("dve", lambda en: en.tensor_add(gsc[:, 24:32], gtm[:], gsc[:, 16:24]),
                             reads=[bgtm, bgsc], update=[bgsc], tag=f"{tag}gl{j}_{sl8}")
                        P.op("dve", lambda en: en.reduce_max(gsc[:, 2:3], gsc[:, 24:32], axis=mybir.AxisListType.X),
                             reads=[bgsc], update=[bgsc], tag=f"{tag}rm{j}_{sl8}")
                        P.op("dve", lambda en: en.tensor_scalar_mul(gsc[:, 3:4], gsc[:, 2:3], -1.0),
                             reads=[bgsc], update=[bgsc], tag=f"{tag}rn{j}_{sl8}")
                        P.op("act", lambda en: en.activation(out=gsc[:, 32:40], in_=gsc[:, 24:32], func=AF.Exp,
                                                             bias=gsc[:, 3:4], accum_out=gsc[:, 4:5]),
                             reads=[bgsc], update=[bgsc], tag=f"{tag}ex{j}_{sl8}")
                        P.op("dve", lambda en: en.reciprocal(gsc[:, 5:6], gsc[:, 4:5]),
                             reads=[bgsc], update=[bgsc], tag=f"{tag}rc{j}_{sl8}")
                        P.op("dve", lambda en, sl8=sl8: en.tensor_single_scalar(
                            gsc[:, 48 + 8 * sl8:56 + 8 * sl8], gsc[:, 32:40], gsc[:, 5:6], ALU.mult),
                            reads=[bgsc], update=[bgsc], tag=f"{tag}wm{j}_{sl8}")

                    def gate_tail(sl8, j=j):
                        sl4 = sl8 % 4
                        P.op("pe", lambda en, sl4=sl4, sl8=sl8: en.transpose(
                            ps1[0:8, 1536 + sl4 * 128:1536 + (sl4 + 1) * 128], gsc[:, 48 + 8 * sl8:56 + 8 * sl8], identf[:]),
                            reads=[bgsc, bconsts["identf"]],
                            writes=[pb1[3]], tag=f"{tag}gbt{j}_{sl8}")
                        P.op("dve", lambda en, sl4=sl4, sl8=sl8: en.tensor_copy(
                            w_fm[:, sl8 * 128:(sl8 + 1) * 128], ps1[0:8, 1536 + sl4 * 128:1536 + (sl4 + 1) * 128]),
                            reads=[pb1[3]],
                            writes=[bwfm] if sl8 == 0 else (), update=[bwfm] if sl8 == 7 else (),
                            tag=f"{tag}wfm{j}_{sl8}")
                        if sl8 == 7:
                            P.op("dve", lambda en: en.tensor_copy(w_bf[:], w_fm[:]),
                                 reads=[bwfm], writes=[bwbf], tag=f"{tag}wbf{j}")
                    for s_ in range(2):
                        for sci in range(8):
                            sc = s_ * 8 + sci
                            reg2 = ps2[:, (sci % 2) * 1024:(sci % 2) * 1024 + 512]
                            reg2b = ps2[:, (sci % 2) * 1024 + 512:(sci % 2) * 1024 + 1024]
                            rbs = [pb2[(sci % 2) * 2], pb2[(sci % 2) * 2 + 1]]
                            sp_ = us_ctr[0] % 2
                            us_ctr[0] += 1
                            wsrc, wq, wrd = wsrc3("f_Wsu_t2", sp_)
                            P.op(wq, lambda en, sc=sc, sp_=sp_, wsrc=wsrc: en.dma_start(
                                out=wsus[:, sp_ * 2048:(sp_ + 1) * 2048],
                                in_=wsrc[:, sc * 2048:(sc + 1) * 2048]),
                                reads=wrd, writes=[bwsu[sp_]], dma=True, tag=f"{tag}wsu{j}_{sc}")
                            feed_conv(2)
                            for tf in range(2):
                                reg = reg2 if tf == 0 else reg2b
                                for c in range(NCH):
                                    P.op("pe", lambda en, c=c, tf=tf, sp_=sp_, reg=reg, xin=xin: en.matmul(
                                        reg, wsus[:, sp_ * 2048 + c * 128:sp_ * 2048 + (c + 1) * 128],
                                        xin[:, c * TC + tf * 512:c * TC + tf * 512 + 512],
                                        start=(c == 0), stop=(c == NCH - 1)),
                                        reads=[bwsu[sp_], bxin[c]],
                                        writes=[rbs[tf]] if c == 0 else (), update=[rbs[tf]] if c == NCH - 1 else (),
                                        tag=f"{tag}su{j}_{sc}_{tf}_{c}")
                                P.op("act", lambda en, sci=sci, tf=tf, sc=sc, reg=reg: en.activation(
                                    out=u_n[:, sci * TC + tf * 512:sci * TC + tf * 512 + 512], in_=reg,
                                    func=AF.Identity, bias=bsuc[:, sc:sc + 1]),
                                    reads=[rbs[tf], bbsu],
                                    writes=[bun] if (sci == 0 and tf == 0) else (), update=[bun],
                                    tag=f"{tag}suev{j}_{sc}_{tf}")
                            if sc < 8:
                                gate_tail(sc)
                        for fch in range(8):
                            reg2 = ps1[:, (fch % 2) * 1024:(fch % 2) * 1024 + 512]
                            reg2b = ps1[:, (fch % 2) * 1024 + 512:(fch % 2) * 1024 + 1024]
                            rbs = [pb1[(fch % 2) * 2], pb1[(fch % 2) * 2 + 1]]
                            fp_ = us_ctr[0] % 2
                            us_ctr[0] += 1
                            wsrc, wq, wrd = wsrc3("f_Wsa_t2", fp_)
                            P.op(wq, lambda en, fch=fch, fp_=fp_, wsrc=wsrc: en.dma_start(
                                out=wsas[:, fp_ * 2048:fp_ * 2048 + 1024],
                                in_=wsrc[:, fch * 1024:(fch + 1) * 1024]),
                                reads=wrd, writes=[bwsa[fp_]], dma=True, tag=f"{tag}wsa{j}_{s_}_{fch}")
                            feed_conv(2)
                            for tf in range(2):
                                reg = reg2 if tf == 0 else reg2b
                                for ec in range(8):
                                    P.op("pe", lambda en, ec=ec, tf=tf, fp_=fp_, reg=reg: en.matmul(
                                        reg, wsas[:, fp_ * 2048 + ec * 128:fp_ * 2048 + (ec + 1) * 128],
                                        u_n[:, ec * TC + tf * 512:ec * TC + tf * 512 + 512],
                                        start=(ec == 0), stop=(ec == 7)),
                                        reads=[bwsa[fp_], bun],
                                        writes=[rbs[tf]] if ec == 0 else (), update=[rbs[tf]] if ec == 7 else (),
                                        tag=f"{tag}sa{j}_{s_}_{fch}_{tf}_{ec}")
                                P.op("act", lambda en, fch=fch, tf=tf, reg=reg: en.activation(
                                    out=silu_t[:], in_=reg, func=AF.Silu, bias=bsac[:, fch:fch + 1]),
                                    reads=[rbs[tf], bbsa], writes=[bsilu], tag=f"{tag}sas{j}_{s_}_{fch}_{tf}")
                                P.op("dve", lambda en, fch=fch, tf=tf: en.tensor_mul(
                                    hst[:, tf * 512:(tf + 1) * 512], silu_t[:],
                                    u_n[:, fch * TC + tf * 512:fch * TC + tf * 512 + 512]),
                                    reads=[bsilu, bun], writes=[bhst], tag=f"{tag}sam{j}_{s_}_{fch}_{tf}")
                            sc = s_ * 8 + fch
                            P.op("sp", lambda en, sc=sc: en.dma_start(
                                out=sh_spill[:, sc * TC:(sc + 1) * TC], in_=hst[:]),
                                reads=[bhst], writes=[bssp[sc]], dma=True, tag=f"{tag}shsp{j}_{sc}")
                    for n in range(6):
                        for tf in range(2):
                            P.op("pe", lambda en, n=n, tf=tf: en.matmul(
                                ps1[:, 1024 + tf * 512:1024 + (tf + 1) * 512],
                                bc8t[:, n * 128:(n + 1) * 128],
                                w_bf[0:8, tf * 512:(tf + 1) * 512], start=True, stop=True),
                                reads=[bwbf, bconsts["bc8"]],
                                writes=[pb1[2 + tf]], tag=f"{tag}wb{j}_{n}_{tf}")
                            P.op("dve", lambda en, tf=tf: en.tensor_copy(
                                wb_n[:, tf * 512:(tf + 1) * 512], ps1[:, 1024 + tf * 512:1024 + (tf + 1) * 512]),
                                reads=[pb1[2 + tf]],
                                writes=[bwbn] if tf == 0 else (), update=[bwbn] if tf else (),
                                tag=f"{tag}wbc{j}_{n}_{tf}")
                        for ech in range(8):
                            reg2 = ps2[:, (ech % 2) * 1024:(ech % 2) * 1024 + 512]
                            reg2b = ps2[:, (ech % 2) * 1024 + 512:(ech % 2) * 1024 + 1024]
                            rbs = [pb2[(ech % 2) * 2], pb2[(ech % 2) * 2 + 1]]
                            up_ = us_ctr[0] % 2
                            us_ctr[0] += 1
                            wsrc, wq, wrd = wsrc3("f_Wup_t", up_)
                            P.op(wq, lambda en, n=n, ech=ech, up_=up_, wsrc=wsrc: en.dma_start(
                                out=wups[:, up_ * 2048:(up_ + 1) * 2048],
                                in_=wsrc[:, (n * 8 + ech) * 2048:(n * 8 + ech + 1) * 2048]),
                                reads=wrd, writes=[bwup[up_]], dma=True, tag=f"{tag}wup{j}_{n}_{ech}")
                            feed_conv(2)
                            for tf in range(2):
                                reg = reg2 if tf == 0 else reg2b
                                for c in range(NCH):
                                    P.op("pe", lambda en, c=c, tf=tf, up_=up_, reg=reg, xin=xin: en.matmul(
                                        reg, wups[:, up_ * 2048 + c * 128:up_ * 2048 + (c + 1) * 128],
                                        xin[:, c * TC + tf * 512:c * TC + tf * 512 + 512],
                                        start=(c == 0), stop=(c == NCH - 1)),
                                        reads=[bwup[up_], bxin[c]],
                                        writes=[rbs[tf]] if c == 0 else (), update=[rbs[tf]] if c == NCH - 1 else (),
                                        tag=f"{tag}up{j}_{n}_{ech}_{tf}_{c}")
                                P.op("act", lambda en, ech=ech, tf=tf, n=n, reg=reg: en.activation(
                                    out=u_n[:, ech * TC + tf * 512:ech * TC + tf * 512 + 512], in_=reg,
                                    func=AF.Identity, bias=bupc[:, n * 8 + ech:n * 8 + ech + 1]),
                                    reads=[rbs[tf], bbup],
                                    writes=[bun] if (ech == 0 and tf == 0) else (), update=[bun],
                                    tag=f"{tag}upev{j}_{n}_{ech}_{tf}")
                        for fch in range(8):
                            reg2 = ps1[:, (fch % 2) * 1024:(fch % 2) * 1024 + 512]
                            reg2b = ps1[:, (fch % 2) * 1024 + 512:(fch % 2) * 1024 + 1024]
                            rbs = [pb1[(fch % 2) * 2], pb1[(fch % 2) * 2 + 1]]
                            fp_ = us_ctr[0] % 2
                            us_ctr[0] += 1
                            wsrc, wq, wrd = wsrc3("f_Wsw_t", fp_)
                            P.op(wq, lambda en, n=n, fch=fch, fp_=fp_, wsrc=wsrc: en.dma_start(
                                out=wsws[:, fp_ * 2048:fp_ * 2048 + 1024],
                                in_=wsrc[:, (n * 8 + fch) * 1024:(n * 8 + fch + 1) * 1024]),
                                reads=wrd, writes=[bwsw[fp_]], dma=True, tag=f"{tag}wswl{j}_{n}_{fch}")
                            feed_conv(2)
                            for tf in range(2):
                                reg = reg2 if tf == 0 else reg2b
                                for ec in range(8):
                                    P.op("pe", lambda en, ec=ec, tf=tf, fp_=fp_, reg=reg: en.matmul(
                                        reg, wsws[:, fp_ * 2048 + ec * 128:fp_ * 2048 + (ec + 1) * 128],
                                        u_n[:, ec * TC + tf * 512:ec * TC + tf * 512 + 512],
                                        start=(ec == 0), stop=(ec == 7)),
                                        reads=[bwsw[fp_], bun],
                                        writes=[rbs[tf]] if ec == 0 else (), update=[rbs[tf]] if ec == 7 else (),
                                        tag=f"{tag}sw{j}_{n}_{fch}_{tf}_{ec}")
                                P.op("act", lambda en, n=n, fch=fch, tf=tf, reg=reg: en.activation(
                                    out=silu_t[:], in_=reg, func=AF.Silu, bias=bswc[:, n * 8 + fch:n * 8 + fch + 1]),
                                    reads=[rbs[tf], bbsw], writes=[bsilu], tag=f"{tag}sws{j}_{n}_{fch}_{tf}")
                                P.op("dve", lambda en, fch=fch, tf=tf: en.tensor_mul(
                                    t2b[:], silu_t[:],
                                    u_n[:, fch * TC + tf * 512:fch * TC + tf * 512 + 512]),
                                    reads=[bsilu, bun], writes=[bt2], tag=f"{tag}swm{j}_{n}_{fch}_{tf}")
                                P.op("dve", lambda en, tf=tf: en.tensor_mul(
                                    hst[:, tf * 512:(tf + 1) * 512], t2b[:], wb_n[:, tf * 512:(tf + 1) * 512]),
                                    reads=[bt2, bwbn], writes=[bhst], tag=f"{tag}swh{j}_{n}_{fch}_{tf}")
                            P.op("sp", lambda en, n=n, fch=fch: en.dma_start(
                                out=h_spill[:, (n * 8 + fch) * TC:(n * 8 + fch + 1) * TC], in_=hst[:]),
                                reads=[bhst], writes=[bhsp[n][fch]], dma=True, tag=f"{tag}hsp{j}_{n}_{fch}")
                    for dcg in range(4):
                        P.op("pool", lambda en, dcg=dcg: en.dma_start(
                            out=bdn6[:], in_=din["f_bdn"][:, dcg * 512:(dcg + 1) * 512]),
                            writes=[bbdn6], dma=True, tag=f"{tag}bdnl{j}_{dcg}")
                        P.op("pool", lambda en, dcg=dcg: en.dma_start(
                            out=bsd1[:], in_=din["f_bsd"].rearrange("d -> () d")[:, dcg * 512:(dcg + 1) * 512]),
                            writes=[bbsd1], dma=True, tag=f"{tag}bsdl{j}_{dcg}")
                        regs = {}
                        for dci in range(4):
                            for tf in range(2):
                                ridx = dci * 2 + tf
                                regs[(dci, tf)] = ((ps1 if ridx < 4 else ps2)[:, (ridx % 4) * 512:(ridx % 4 + 1) * 512],
                                                  (pb1 if ridx < 4 else pb2)[ridx % 4])
                        first_mm = {k: True for k in regs}
                        for dci in range(4):
                            dc = dcg * 4 + dci
                            for tf in range(2):
                                reg, rb = regs[(dci, tf)]
                                P.op("pe", lambda en, dci=dci, tf=tf, reg=reg: en.matmul(
                                    reg, bsd1[0:1, dci * 128:(dci + 1) * 128], ones512[0:1, :],
                                    start=True, stop=False),
                                    reads=[bbsd1, bones512],
                                    writes=[rb], tag=f"{tag}bsd{j}_{dc}_{tf}")
                                P.op("pe", lambda en, dci=dci, tf=tf, reg=reg: en.matmul(
                                    reg, bdn6[:, dci * 128:(dci + 1) * 128],
                                    w_bf[0:6, tf * 512:(tf + 1) * 512], start=False, stop=False),
                                    reads=[bbdn6, bwbf], tag=f"{tag}bdn{j}_{dc}_{tf}")
                        for n in range(6):
                            for ecg in range(2):
                                hp = (n * 2 + ecg) % 2
                                P.op("sp", lambda en, n=n, ecg=ecg, hp=hp: en.dma_start(
                                    out=h_n[:, hp * 4 * TC:(hp + 1) * 4 * TC],
                                    in_=h_spill[:, (n * 8 + ecg * 4) * TC:(n * 8 + ecg * 4 + 4) * TC]),
                                    reads=[bhsp[n][f] for f in range(ecg * 4, ecg * 4 + 4)],
                                    writes=[bhn[hp]], dma=True, tag=f"{tag}hld{j}_{dcg}_{n}_{ecg}")
                                wp_ = ds_ctr[0] % 2
                                ds_ctr[0] += 1
                                wsrc, wq, wrd = wsrc3("f_Wdn_t", wp_)
                                P.op(wq, lambda en, n=n, dcg=dcg, ecg=ecg, wp_=wp_, wsrc=wsrc: en.dma_start(
                                    out=wdns[:, wp_ * 2048:(wp_ + 1) * 2048],
                                    in_=wsrc[:, ((n * 2 + ecg) * 4 + dcg) * 2048:((n * 2 + ecg) * 4 + dcg + 1) * 2048]),
                                    reads=wrd, writes=[bwdn[wp_]], dma=True, tag=f"{tag}wdnl{j}_{dcg}_{n}_{ecg}")
                                feed_conv(2)
                                for dci in range(4):
                                    for tf in range(2):
                                        reg, rb = regs[(dci, tf)]
                                        for eci in range(4):
                                            P.op("pe", lambda en, dci=dci, tf=tf, eci=eci, hp=hp, wp_=wp_, reg=reg: en.matmul(
                                                reg, wdns[:, wp_ * 2048 + eci * 512 + dci * 128:wp_ * 2048 + eci * 512 + dci * 128 + 128],
                                                h_n[:, hp * 4 * TC + eci * TC + tf * 512:hp * 4 * TC + eci * TC + tf * 512 + 512],
                                                start=False, stop=False),
                                                reads=[bwdn[wp_], bhn[hp]], tag=f"{tag}dn{j}_{dcg}_{n}_{dci}_{tf}_{eci}")

                        for scp in range(8):
                            wp = ds_ctr[0] % 2
                            ds_ctr[0] += 1
                            wsrc, wq, wrd = wsrc3("f_Wsd_t2", wp)
                            P.op(wq, lambda en, dcg=dcg, scp=scp, wp=wp, wsrc=wsrc: en.dma_start(
                                out=wsds[:, wp * 2048:wp * 2048 + 1024],
                                in_=wsrc[:, (dcg * 8 + scp) * 1024:(dcg * 8 + scp + 1) * 1024]),
                                reads=wrd, writes=[bwsd[wp]], dma=True, tag=f"{tag}wsdl{j}_{dcg}_{scp}")
                            feed_conv(2)
                            for sci in range(2):
                                sc = scp * 2 + sci
                                sp_ = sc % 2
                                P.op("sp", lambda en, sc=sc, sp_=sp_: en.dma_start(
                                    out=shl[:, sp_ * TC:(sp_ + 1) * TC],
                                    in_=sh_spill[:, sc * TC:(sc + 1) * TC]),
                                    reads=[bssp[sc]], writes=[bshl[sp_]], dma=True,
                                    tag=f"{tag}shl{j}_{dcg}_{sc}")
                                for dci in range(4):
                                    for tf in range(2):
                                        reg, rb = regs[(dci, tf)]
                                        P.op("pe", lambda en, sci=sci, sp_=sp_, wp=wp, dci=dci, tf=tf, sc=sc, reg=reg: en.matmul(
                                            reg, wsds[:, wp * 2048 + sci * 512 + dci * 128:wp * 2048 + sci * 512 + dci * 128 + 128],
                                            shl[:, sp_ * TC + tf * 512:sp_ * TC + tf * 512 + 512],
                                            start=False, stop=(sc == 15)),
                                            reads=[bwsd[wp], bshl[sp_]],
                                            update=[regs[(dci, tf)][1]] if sc == 15 else (),
                                            tag=f"{tag}sd{j}_{dcg}_{sc}_{dci}_{tf}")
                        for dci in range(4):
                            dc = dcg * 4 + dci
                            for tf in range(2):
                                reg, rb = regs[(dci, tf)]
                                xs = slice(dc * TC + tf * 512, dc * TC + tf * 512 + 512)
                                if j == 0:
                                    P.op("dve", lambda en, reg=reg, xs=xs: en.tensor_add(
                                        xn1_f[:, xs], reg, xn_f[:, xs]),
                                        reads=[rb, bxnf[dc]],
                                        writes=[bxn1[dc]] if tf == 0 else (), update=[bxn1[dc]] if tf else (),
                                        tag=f"{tag}ev0_{dc}_{tf}")
                                else:
                                    P.op("dve", lambda en, reg=reg, xs=xs: en.tensor_add(
                                        x_res[:, xs], x_res[:, xs], reg),
                                        reads=[rb], writes=[bxc[dc]], tag=f"{tag}ev1a_{dc}_{tf}")
                                    P.op("dve", lambda en, xs=xs: en.tensor_add(
                                        x_res[:, xs], x_res[:, xs], xn_f[:, xs]),
                                        reads=[bxnf[dc]], writes=[bxc[dc]], tag=f"{tag}ev1b_{dc}_{tf}")
                    if j == 0 and "xt0" in dbg_out:
                        P.op("pool", lambda en: en.dma_start(
                            out=dbg_out["xt0"].rearrange("(c p) t -> p c t", p=128),
                            in_=xn1_f.ap().rearrange("p (c t) -> p c t", c=NCH)),
                            reads=bxn1, dma=True, tag="dbg_xt0")
                    if "w" + str(j) in dbg_out:
                        P.op("pool", lambda en, j=j: en.dma_start(
                            out=dbg_out["w" + str(j)], in_=w_fm[:]),
                            reads=[bwfm], dma=True, tag=f"dbg_w{j}")
                    if j == 0 and "sh0" in dbg_out:
                        P.op("pool", lambda en: en.dma_start(out=dbg_out["sh0"], in_=sh_spill),
                             reads=[bssp[i] for i in range(16)], dma=True, tag="dbg_sh0")
                    if j == 0 and "h0" in dbg_out:
                        P.op("pool", lambda en: en.dma_start(out=dbg_out["h0"], in_=h_spill),
                             reads=[b for row in bhsp for b in row], dma=True, tag="dbg_h0")
                feed_conv(len(conv_jobs))
                P.fence([bwg8, bbg, bbup, bbsw, bbsa, bbsu, bbdn6, bbsd1, bones512,
                         bg8r, bgtm, bgsc, bwfm, bwbf, bwbn, bun, bsilu, bt2, bhst]
                        + bwup + bwsw + bwsu + bwsa + bwdn + bwsd + bhn + bshl + bcstg
                        + bxnf + bxn1 + [b for row in bhsp for b in row] + bssp, tag=tag + "f")
                if conv_state["stg"] is not None and conv_state["i"] >= len(conv_jobs):
                    conv_state["stg"] = None

        QSB = 1.0 / (128.0 ** 0.5)

        def attn_b():
            stage_toks = layernorm("ln2", None, None)
            tcc = P.op("pool", lambda en: en.collective_compute(
                "AllGather", ALU.bypass, ins=[ag_xn_in.opt()], outs=[ag_xn_out.opt()],
                replica_groups=PAIRS),
                writes=[bag_xn_out], cc=True, tag="ag_xn_b")
            th = P.thunks["pool"][-1]
            for t in stage_toks:
                P._need(th, t)

            pt_toks = {0: [], 1: []}
            rs_emitted = [False, False]

            def emit_rs(colh, tg):
                rs_emitted[colh] = True
                P.op("pool", lambda en, colh=colh: en.collective_compute(
                    "ReduceScatter", ALU.add, ins=[ag_pt_in2[colh].opt()], outs=[ag_pt_out2[colh].opt()],
                    replica_groups=PAIRS),
                    writes=[bag_pt_out2[colh]], cc=True, tag=tg)
                thpt = P.thunks["pool"][-1]
                for t in pt_toks[colh]:
                    P._need(thpt, t)
            with ExitStack() as bctx:
                be = bctx.enter_context
                xnw = be(nc.sbuf_tensor("b_xnw", [128, NCH * 512], BF16))
                bxnw = [P.buf(f"bxnw{c}") for c in range(NCH)]
                q_fm = be(nc.sbuf_tensor("b_qfm", [128, 8 * T], BF16))
                k_fm = be(nc.sbuf_tensor("b_kfm", [128, 2 * T], BF16))
                v_tm = be(nc.sbuf_tensor("b_vtm", [128, NCH * 256], BF16))
                wslab = be(nc.sbuf_tensor("b_wslab", [128, 2 * NCH * 128], BF16))
                wvb = be(nc.sbuf_tensor("b_wvb", [128, NCH * 256], BF16))
                strip = be(nc.sbuf_tensor("b_strip", [128, 2 * STRIP_W], BF16))
                Ptmp = be(nc.sbuf_tensor("b_Ptmp", [128, 3 * 512], F32))
                Pbf = be(nc.sbuf_tensor("b_Pbf", [128, 3 * 512], BF16))
                sq = be(nc.sbuf_tensor("b_sq", [128, T], BF16))
                nrm = be(nc.sbuf_tensor("b_nrm", [1, 64], F32))
                mneg = be(nc.sbuf_tensor("b_mneg", [128, 8], F32))
                rrow = be(nc.sbuf_tensor("b_rrow", [1, 1024], F32))
                bcst = be(nc.sbuf_tensor("b_bcst", [128, 512], F32))
                o_fm = be(nc.sbuf_tensor("b_ofm", [128, 8 * 512], BF16))
                wob = be(nc.sbuf_tensor("b_wob", [128, 4 * 2048], BF16))
                stg = be(nc.sbuf_tensor("b_stg", [128, 512], F32))

                bq, bk, bv = P.buf("bq"), P.buf("bk"), P.buf("bv")
                bws = [P.buf("bws0"), P.buf("bws1")]
                bwvb = P.buf("bwvb")
                bstrip = [P.buf("bstrip0"), P.buf("bstrip1")]
                bPt = [P.buf("bPt0"), P.buf("bPt1"), P.buf("bPt2")]
                bPb = [P.buf("bPb0"), P.buf("bPb1"), P.buf("bPb2")]
                bsq, bnrm, bmneg = P.buf("bsq"), P.buf("bnrm"), P.buf("bmneg")
                brr, bbc = P.buf("brr"), P.buf("bbc2")
                bof = P.buf("bof")
                bwob = [P.buf(f"bwob{i}") for i in range(4)]
                bstg2 = P.buf("bstg2")

                for w in range(4):
                    for c0 in range(0, NCH, 4):
                        def fn(en, c0=c0, w=w):
                            return en.dma_start(
                                out=xnw.ap().rearrange("p (c t) -> p c t", c=NCH)[:, c0:c0 + 4, :],
                                in_=ag_xn_out.rearrange("(c p) t -> p c t", p=128)[
                                    :, (w // 2) * NCH + c0:(w // 2) * NCH + c0 + 4,
                                    (w % 2) * 512:(w % 2) * 512 + 512])
                        P.op("pool", fn, reads=[bag_xn_out],
                             writes=[bxnw[c0 + i] for i in range(4)], dma=True, tag=f"bx{w}_{c0}")
                    for hl in range(8):
                        par = hl % 2
                        P.op("sp" if par == 0 else "pool", lambda en, hl=hl, par=par: en.dma_start(
                            out=wslab[:, par * 2048:(par + 1) * 2048],
                            in_=dcache["wq_b"][:, hl * 2048:(hl + 1) * 2048]),
                            reads=[bcache["wq_b"]], writes=[bws[par]], dma=True, tag=f"bwq{w}_{hl}")
                        reg = ps2[:, (hl % 4) * 512:(hl % 4 + 1) * 512]
                        rb = pb2[hl % 4]
                        for c in range(NCH):
                            P.op("pe", lambda en, hl=hl, c=c, reg=reg, par=par: en.matmul(
                                reg, wslab[:, par * NCH * 128 + c * 128:par * NCH * 128 + (c + 1) * 128],
                                xnw[:, c * 512:(c + 1) * 512],
                                start=(c == 0), stop=(c == NCH - 1)),
                                reads=[bws[par], bxnw[c]],
                                writes=[rb] if c == 0 else (), update=[rb] if c == NCH - 1 else (),
                                tag=f"bq{w}_{hl}_{c}")
                        P.op("dve", lambda en, hl=hl, w=w, reg=reg: en.tensor_copy(
                            q_fm[:, hl * T + w * 512:hl * T + (w + 1) * 512], reg),
                            reads=[rb],
                            writes=[bq] if (w == 0 and hl == 0) else (), update=[bq],
                            tag=f"bqev{w}_{hl}")
                    for gl in range(2):
                        par = gl % 2
                        P.op("sp" if par == 0 else "pool", lambda en, gl=gl, par=par: en.dma_start(
                            out=wslab[:, par * 2048:(par + 1) * 2048],
                            in_=dcache["wk_b"][:, gl * 2048:(gl + 1) * 2048]),
                            reads=[bcache["wk_b"]], writes=[bws[par]], dma=True, tag=f"bwk{w}_{gl}")
                        reg = ps2[:, gl * 512:(gl + 1) * 512]
                        rb = pb2[gl]
                        for c in range(NCH):
                            P.op("pe", lambda en, gl=gl, c=c, reg=reg, par=par: en.matmul(
                                reg, wslab[:, par * NCH * 128 + c * 128:par * NCH * 128 + (c + 1) * 128],
                                xnw[:, c * 512:(c + 1) * 512],
                                start=(c == 0), stop=(c == NCH - 1)),
                                reads=[bws[par], bxnw[c]],
                                writes=[rb] if c == 0 else (), update=[rb] if c == NCH - 1 else (),
                                tag=f"bk{w}_{gl}_{c}")
                        P.op("dve", lambda en, gl=gl, w=w, reg=reg: en.tensor_copy(
                            k_fm[:, gl * T + w * 512:gl * T + (w + 1) * 512], reg),
                            reads=[rb],
                            writes=[bk] if (w == 0 and gl == 0) else (), update=[bk],
                            tag=f"bkev{w}_{gl}")
                    if w == 0:
                        P.op("pool", lambda en: en.dma_start(
                            out=wvb[:], in_=din["wv_b"]),
                            writes=[bwvb], dma=True, tag=f"bwv{w}")
                    for tc4 in range(4):
                        tcg = w * 4 + tc4
                        reg = ps1[:, (tc4 % 4) * 512:(tc4 % 4) * 512 + 256]
                        rb = pb1[tc4 % 4]
                        for c in range(NCH):
                            P.op("pe", lambda en, c=c, tc4=tc4, reg=reg: en.matmul(
                                reg, xnw[:, c * 512 + tc4 * 128:c * 512 + tc4 * 128 + 128],
                                wvb[:, c * 256:(c + 1) * 256], start=(c == 0), stop=(c == NCH - 1)),
                                reads=[bwvb, bxnw[c]],
                                writes=[rb] if c == 0 else (), update=[rb] if c == NCH - 1 else (),
                                tag=f"bv{w}_{tc4}_{c}")
                        P.op("dve", lambda en, tcg=tcg, reg=reg: en.tensor_copy(
                            v_tm[:, tcg * 256:(tcg + 1) * 256], reg),
                            reads=[rb],
                            writes=[bv] if tcg == 0 else (), update=[bv],
                            tag=f"bvev{w}_{tc4}")
                for hl in range(10):
                    src = q_fm[:, hl * T:(hl + 1) * T] if hl < 8 else k_fm[:, (hl - 8) * T:(hl - 7) * T]
                    rdep = [bq] if hl < 8 else [bk]
                    P.op("act", lambda en, src=src: en.activation(out=sq[:], in_=src, func=AF.Square),
                         reads=rdep, writes=[bsq], tag=f"bsqr{hl}")
                    for qq in range(4):
                        P.op("pe", lambda en, qq=qq: en.matmul(
                            ps1[0:1, 1024:1536],
                            ones_bf[:], sq[:, qq * 512:(qq + 1) * 512],
                            start=True, stop=True),
                            reads=[bsq, bconsts["ones_bf"]],
                            writes=[pb1[2]], tag=f"bnm{hl}_{qq}")
                        P.op("dve", lambda en, hl=hl, qq=qq: en.reduce_max(
                            nrm[0:1, 32 + qq:33 + qq], ps1[0:1, 1024:1536], axis=mybir.AxisListType.X),
                            reads=[pb1[2]],
                            writes=[bnrm] if (hl == 0 and qq == 0) else (), update=[bnrm], tag=f"bnr{hl}_{qq}")
                    P.op("dve", lambda en, hl=hl: en.reduce_max(
                        nrm[0:1, hl:hl + 1], nrm[0:1, 32:36], axis=mybir.AxisListType.X),
                        reads=[bnrm], update=[bnrm], tag=f"bnrf{hl}")
                P.op("dve", lambda en: en.tensor_single_scalar(
                    nrm[0:1, 16:20], nrm[0:1, 0:4], nrm[0:1, 8:9], ALU.mult),
                    reads=[bnrm], update=[bnrm], tag="bmm0")
                P.op("dve", lambda en: en.tensor_single_scalar(
                    nrm[0:1, 20:24], nrm[0:1, 4:8], nrm[0:1, 9:10], ALU.mult),
                    reads=[bnrm], update=[bnrm], tag="bmm1")
                P.op("act", lambda en: en.activation(out=nrm[0:1, 24:32], in_=nrm[0:1, 16:24], func=AF.Ln),
                     reads=[bnrm], update=[bnrm], tag="bln")
                P.op("act", lambda en: en.activation(out=nrm[0:1, 16:24], in_=nrm[0:1, 24:32], func=AF.Exp, scale=0.5),
                     reads=[bnrm], update=[bnrm], tag="bexp")
                P.op("dve", lambda en: en.tensor_scalar_mul(nrm[0:1, 24:32], nrm[0:1, 16:24], -QSB),
                     reads=[bnrm], update=[bnrm], tag="bneg")
                P.op("pe", lambda en: en.matmul(
                    ps1[:, 1024:1032], ones_f[:], nrm[0:1, 24:32], start=True, stop=True),
                    reads=[bnrm, bconsts["ones_f"]], writes=[pb1[2]], tag="bmnbc")
                P.op("dve", lambda en: en.tensor_copy(mneg[:], ps1[:, 1024:1032]),
                     reads=[pb1[2]], writes=[bmneg], tag="bmnc")
                def emit_strip(qb, hl):
                    sp_ = hl % 2
                    P.op("sp", lambda en, hl=hl, sp_=sp_: en.dma_start(
                        out=strip[:, sp_ * STRIP_W:(sp_ + 1) * STRIP_W],
                        in_=din["strips"][hl]),
                        writes=[bstrip[sp_]], dma=True, tag=f"bstl{qb}_{hl}")

                SLOT = [0, 512, 1536]
                SBUF3 = [pb1[0], pb1[1], pb1[3]]

                for qb in range(4):
                    nkb = (qb + 1) * 4
                    NG = 8 * nkb

                    def S_of(g, qb=qb, nkb=nkb):
                        hl, kb = g // nkb, g % nkb
                        sl = SLOT[g % 3]
                        P.op("pe", lambda en, hl=hl, kb=kb, sl=sl, qb=qb: en.matmul(
                            ps1[:, sl:sl + 512],
                            k_fm[:, (hl // 4) * T + kb * 128:(hl // 4) * T + kb * 128 + 128],
                            q_fm[:, hl * T + qb * 512:hl * T + (qb + 1) * 512],
                            start=True, stop=True),
                            reads=[bk, bq], writes=[SBUF3[g % 3]], tag=f"bS{qb}_{hl}_{kb}")

                    def emit_tail(hl, qb=qb):
                        hp = hl % 2
                        psO = ps2[:, hp * 1024:hp * 1024 + 512]
                        pbO, pbS = pb2[hp * 2], pb2[hp * 2 + 1]
                        P.op("pe", lambda en: en.matmul(
                            ps1[:, 1024:1536], ones_f[:], rrow[0:1, 512:1024], start=True, stop=True),
                            reads=[brr, bconsts["ones_f"]], writes=[pb1[2]], tag=f"bbc{qb}_{hl}")
                        P.op("dve", lambda en: en.tensor_copy(bcst[:], ps1[:, 1024:1536]),
                             reads=[pb1[2]], writes=[bbc], tag=f"bbcc{qb}_{hl}")
                        P.op("dve", lambda en, hl=hl, psO=psO, pbO=pbO: en.tensor_mul(
                            o_fm[:, hl * 512:(hl + 1) * 512], psO, bcst[:]),
                            reads=[pbO, bbc],
                            writes=[bof] if hl == 0 else (), update=[bof] if hl == 7 else (),
                            tag=f"bon{qb}_{hl}")

                    emit_strip(qb, 0)
                    S_of(0)
                    S_of(1)
                    for g in range(NG):
                        hl, kb = g // nkb, g % nkb
                        hp = hl % 2
                        gl = hl // 4
                        sp_ = hl % 2
                        kp = g % 3
                        psO = ps2[:, hp * 1024:hp * 1024 + 512]
                        psS = ps2[0:1, hp * 1024 + 512:hp * 1024 + 1024]
                        pbO, pbS = pb2[hp * 2], pb2[hp * 2 + 1]
                        if kb == 0 and hl + 1 < 8:
                            emit_strip(qb, hl + 1)
                        if kb == 1 and hl > 0:
                            emit_tail(hl - 1)
                        rbC = SBUF3[kp]
                        regC = ps1[:, SLOT[kp]:SLOT[kp] + 512]
                        P.op("act", lambda en, hl=hl, kp=kp, regC=regC: en.activation(
                            out=Ptmp[:, kp * 512:(kp + 1) * 512], in_=regC, func=AF.Exp, scale=QSB, bias=mneg[:, hl:hl + 1]),
                            reads=[rbC, bmneg], writes=[bPt[kp]], tag=f"bE{qb}_{hl}_{kb}")
                        off = qb * 512 - kb * 128 + 384
                        P.op("dve", lambda en, kp=kp, sp_=sp_, off=off: en.tensor_mul(
                            Pbf[:, kp * 512:kp * 512 + 512], Ptmp[:, kp * 512:(kp + 1) * 512],
                            strip[:, sp_ * STRIP_W + off:sp_ * STRIP_W + off + 512]),
                            reads=[bPt[kp], bstrip[sp_]], writes=[bPb[kp]], tag=f"bW{qb}_{hl}_{kb}")
                        if g + 2 < NG:
                            S_of(g + 2)
                        P.op("pe", lambda en, kb=kb, kp=kp, gl=gl, psO=psO, first=(kb == 0), last=(kb == nkb - 1): en.matmul(
                            psO, v_tm[:, kb * 256 + gl * 128:kb * 256 + gl * 128 + 128],
                            Pbf[:, kp * 512:kp * 512 + 512],
                            start=first, stop=last),
                            reads=[bPb[kp], bv],
                            writes=[pbO] if kb == 0 else (), update=[pbO] if kb == nkb - 1 else (),
                            tag=f"bPV{qb}_{hl}_{kb}")
                        P.op("pe", lambda en, kb=kb, kp=kp, psS=psS, first=(kb == 0), last=(kb == nkb - 1): en.matmul(
                            psS, ones_bf[:], Pbf[:, kp * 512:kp * 512 + 512],
                            start=first, stop=last),
                            reads=[bPb[kp], bconsts["ones_bf"]],
                            writes=[pbS] if kb == 0 else (), update=[pbS] if kb == nkb - 1 else (),
                            tag=f"bSS{qb}_{hl}_{kb}")
                        if kb == nkb - 1:
                            P.op("act", lambda en, psS=psS, pbS=pbS: en.activation(out=rrow[0:1, 0:512], in_=psS, func=AF.Ln),
                                 reads=[pbS], writes=[brr], tag=f"bln{qb}_{hl}")
                            P.op("act", lambda en: en.activation(out=rrow[0:1, 512:1024], in_=rrow[0:1, 0:512], func=AF.Exp, scale=-1.0),
                                 reads=[brr], update=[brr], tag=f"bre{qb}_{hl}")
                    emit_tail(7)
                    for half in range(2):
                        for hh_ in range(4):
                            hglob = half * 4 + hh_
                            P.op("sp" if hh_ % 2 == 0 else "pool", lambda en, hglob=hglob, hh_=hh_: en.dma_start(
                                out=wob[:, hh_ * 2048:(hh_ + 1) * 2048],
                                in_=dcache["wo_b"][hglob * 128:(hglob + 1) * 128, :]),
                                reads=[bcache["wo_b"]], writes=[bwob[hh_]], dma=True, tag=f"bwo{qb}_{half}_{hh_}")
                        for dc in range(16):
                            reg = ps2[:, 1024 + (dc % 2) * 512:1024 + (dc % 2) * 512 + 512]
                            rb = pb2[2 + dc % 2]
                            for hh_ in range(4):
                                hglob = half * 4 + hh_
                                P.op("pe", lambda en, dc=dc, hh_=hh_, hglob=hglob, reg=reg: en.matmul(
                                    reg, wob[:, hh_ * 2048 + dc * 128:hh_ * 2048 + (dc + 1) * 128],
                                    o_fm[:, hglob * 512:(hglob + 1) * 512], start=(hh_ == 0), stop=(hh_ == 3)),
                                    reads=[bwob[hh_], bof],
                                    writes=[rb] if hh_ == 0 else (), update=[rb] if hh_ == 3 else (),
                                    tag=f"bwomm{qb}_{half}_{dc}_{hh_}")
                            P.op("act", lambda en, reg=reg: en.activation(out=stg[:], in_=reg, func=AF.Identity),
                                 reads=[rb], writes=[bstg2], tag=f"bwoev{qb}_{half}_{dc}")
                            if half == 0:
                                pt_toks[qb % 2].append(P.op("sp", lambda en, qb=qb, dc=dc: en.dma_start(
                                    out=ag_pt_in2[qb % 2][(qb // 2) * D + dc * 128:(qb // 2) * D + (dc + 1) * 128, :],
                                    in_=stg[:]),
                                    reads=[bstg2], writes=[bag_ptr[qb][dc]], dma=True, tag=f"bwod{qb}_{half}_{dc}"))
                            else:
                                pt_toks[qb % 2].append(P.op("pool", lambda en, qb=qb, dc=dc: en.dma_start(
                                    out=ag_pt_in2[qb % 2][(qb // 2) * D + dc * 128:(qb // 2) * D + (dc + 1) * 128, :],
                                    in_=stg[:], accum_op=ALU.add),
                                    reads=[bstg2], writes=[bag_ptr[qb][dc]], dma=True, tag=f"bwod{qb}_{half}_{dc}"))
                    if qb == 2:
                        emit_rs(0, "ag_pt_b0")
                P.fence([bq, bk, bv, bws[0], bws[1], bwvb, bstrip[0], bstrip[1],
                         bsq, bnrm, bmneg, brr, bbc, bof, bstg2] + bPt + bPb + bwob + bxnw, tag="bf")

            emit_rs(1, "ag_pt_b1")
            with ExitStack() as rctx:
                re_ = rctx.enter_context
                rb0 = re_(nc.sbuf_tensor("b_rb0", [128, 2 * 4 * 512], F32))
                brb0 = [[P.buf(f"brb0{i}_{ch}") for ch in range(2)] for i in range(2)]
                for colh in range(2):
                    for c0 in range(0, NCH, 4):
                        gp = (c0 // 4) % 2
                        r0g = rb0.ap().rearrange("p (ch g t) -> p ch g t", ch=2, g=4)[:, gp]
                        P.op("sp" if colh == 0 else "pool", lambda en, c0=c0, colh=colh, r0g=r0g: en.dma_start(
                            out=r0g,
                            in_=ag_pt_out2[colh].rearrange("(c p) t -> p c t", p=128)[:, c0:c0 + 4, :]),
                            reads=[bag_pt_out2[colh]], writes=[brb0[gp][colh]], dma=True, tag=f"brb0_{c0}_{colh}")
                        for ci in range(4):
                            c = c0 + ci
                            sl = slice(c * TC + colh * 512, c * TC + colh * 512 + 512)
                            r0v = rb0[:, gp * 4 * 512 + ci * 512:gp * 4 * 512 + (ci + 1) * 512]
                            P.op("dve", lambda en, sl=sl, r0v=r0v: en.tensor_add(x_res[:, sl], x_res[:, sl], r0v),
                                 reads=[brb0[gp][colh]], writes=[bxc[c]], tag=f"bres0_{c}_{colh}")
                P.fence([b for row in brb0 for b in row], tag="brbf")

        if nsub >= 1:
            attn_a()
        if nsub >= 2:
            ffn_sublayer("f1")
        if nsub >= 3:
            attn_b()
        if nsub >= 4:
            ffn_sublayer("f2")

        out_toks = []

        def ox(en):
            return en.dma_start(out=out_x.rearrange("(c p) t -> p c t", p=128),
                                in_=x_res.ap().rearrange("p (c t) -> p c t", c=NCH))
        out_toks.append(P.op("sp", ox, reads=list(bxc), dma=True, tag="out_x"))

        P.final_barrier("sp", out_toks)
        blk = e(nc.Block())
        P.finalize(blk)

    return nc, [k for k in dbg_out]


def kernel(**inputs):
    in_maps = make_in_maps(inputs)
    nc, dbg = build_program()
    res = run_bass_kernel_spmd(nc, in_maps, core_ids=list(range(NC_ACT)))
    out = np.empty((B, T, D), np.float32)
    for c in range(NC_ACT):
        b, hh = c // 2, c % 2
        out[b, hh * TC:(hh + 1) * TC, :] = res.results[c]["out_x"].T
    return out
